# version 03
# baseline (speedup 1.0000x reference)
import numpy as np
import ml_dtypes
import concourse.bass as bass
import concourse.mybir as mybir
from concourse.bass_utils import run_bass_kernel_spmd
from contextlib import ExitStack

F32 = mybir.dt.float32
BF16 = mybir.dt.bfloat16
I32 = mybir.dt.int32
AF = mybir.ActivationFunctionType
ALU = mybir.AluOpType
BF16NP = ml_dtypes.bfloat16

NCORES = 8
S = 8192
D = 4096
HD = 128
NQH = 4
TOK = 1024
QC = 512
EPS = 1e-5
SCALE = 1.0 / float(np.sqrt(128.0))
OROWS = 512
RG = [list(range(NCORES))]


def _host_consts():
    inv_freq = 1.0 / (10000.0 ** (np.arange(0, HD, 2, dtype=np.float32) / HD))
    t = np.arange(S, dtype=np.float32)
    freqs = np.outer(t, inv_freq)
    emb = np.concatenate([freqs, freqs], axis=-1)
    cosT = np.ascontiguousarray(np.cos(emb).astype(np.float32).T)
    sinT = np.ascontiguousarray(np.sin(emb).astype(np.float32).T)
    sinTs = sinT.copy()
    sinTs[:64] = -sinTs[:64]
    m01 = np.zeros((4, 128, QC), dtype=np.float32)
    kk = np.arange(128)[:, None]
    qq = np.arange(QC)[None, :]
    for i in range(4):
        m01[i] = np.where(128 * i + kk <= qq, 0.0, -2000.0)
    masks = m01.astype(BF16NP)
    ident = np.eye(128, dtype=BF16NP)
    return cosT, sinTs, masks, ident


class Ctr:
    def __init__(self):
        self.v = 0

    def bump(self, n=1):
        self.v += n
        return self.v


def build(dbg=False):
    nc = bass.Bass("TRN2", target_bir_lowering=False, debug=False)
    cosT_np, sinTs_np, masks_np, ident_np = _host_consts()

    d_x = nc.dram_tensor("x_mine", [TOK, D], F32, kind="ExternalInput")
    d_wq = nc.dram_tensor("wq_m", [D, NQH * HD], F32, kind="ExternalInput")
    d_wk = nc.dram_tensor("wk_m", [D, HD], F32, kind="ExternalInput")
    d_wv = nc.dram_tensor("wv_m", [D, HD], F32, kind="ExternalInput")
    d_wo = nc.dram_tensor("wo", [D, D], F32, kind="ExternalInput")
    d_qw = nc.dram_tensor("qw_m", [8, OROWS, 128], I32, kind="ExternalInput")
    d_lut4 = nc.dram_tensor("lut4_m", [OROWS, 16], F32, kind="ExternalInput")
    d_lut8 = nc.dram_tensor("lut8_m", [OROWS, 256], F32, kind="ExternalInput")
    d_bias = nc.dram_tensor("bias", [1, D], F32, kind="ExternalInput")
    d_cos = nc.dram_tensor("cosT", [HD, S], F32, kind="ExternalInput")
    d_sin = nc.dram_tensor("sinTs", [HD, S], F32, kind="ExternalInput")
    d_out = nc.dram_tensor("out_m", [TOK, D], F32, kind="ExternalOutput")

    d_masks = nc.inline_tensor(masks_np.reshape(4 * 128, QC), name="masks")
    d_ident = nc.inline_tensor(ident_np, name="ident")

    agx_in = nc.dram_tensor("agx_in", [D, TOK], BF16)
    agx = nc.dram_tensor("agx", [NCORES * D, TOK], BF16)
    qspill = nc.dram_tensor("qspill", [NQH * HD, S], BF16)
    ago_in = nc.dram_tensor("ago_in", [NQH * HD, S], BF16)
    ago = nc.dram_tensor("ago", [NCORES * NQH * HD, S], BF16)
    agom = nc.dram_tensor("agom", [NCORES * NQH * HD, TOK], BF16)
    agw_in = nc.dram_tensor("agw_in", [D, 2 * OROWS], BF16)
    agw = nc.dram_tensor("agw", [NCORES * D, 2 * OROWS], BF16)
    x2sp = nc.dram_tensor("x2sp", [TOK, D], F32)
    wsp = nc.dram_tensor("wsp", [2 * OROWS, D], BF16)

    dbg_t = {}
    if dbg:
        for nm, shp in [("z1", [128, D]), ("kT", [128, 512]), ("expS", [128, 1024]),
                        ("x2", [128, D]), ("wl", [128, 512]), ("idx8", [128, 512]),
                        ("mcol", [128, 8]), ("oT", [128, 512]), ("den", [1, 512]),
                        ("scr", [128, 512]), ("rot", [128, 512]), ("cs", [128, 512]),
                        ("sn", [128, 512]), ("kTlate", [128, 512]),
                        ("ag6", [128, 1024]), ("o2", [128, 512]),
                        ("axin", [128, 512]), ("agx0", [128, 512]),
                        ("x2sp", [128, D]), ("x2nT", [128, 1024]),
                        ("x0", [128, D]), ("st0", [128, 20]),
                        ("mfin", [128, 8]), ("m1fin", [128, 8]),
                        ("xnt7", [128, 1024]), ("x2sp7a", [128, D]),
                        ("x2sp7b", [128, D]), ("xntall", [128, 32 * TOK]),
                        ("agw0", [128, 512]), ("x2all", [1024, D]),
                        ("stB", [128, 20]), ("ago9", [4096, 512]),
                        ("oT9", [128, 1024]), ("den9", [2, 512]),
                        ("es9", [128, 1024])]:
            dbg_t[nm] = nc.dram_tensor("dbg_" + nm, shp, F32, kind="ExternalOutput")

    ops = {k: [] for k in "gtvsy"}

    def p(eng, fn):
        ops[eng].append(fn)

    sems = {}
    ctrs = {}
    for nm in ["ld", "x1", "v1", "sc", "tp", "ag", "xn", "xnc", "ev", "rot",
               "g1", "q4", "exp", "msk", "av", "nrm", "rec", "bc", "ao", "wo",
               "mm6", "ep6", "x2", "u", "sw", "wst", "ff", "fe", "st", "cs",
               "vt", "qs", "x2s", "x7",
               "x1b", "sx", "xn1", "xn2", "xn3", "csb", "rotb", "qp",
               "q4b", "x7b", "wt6", "wo2"]:
        sems[nm] = nc.alloc_semaphore(nm)
        ctrs[nm] = Ctr()

    SB = ExitStack()

    def sb(name, shape, dt, stack):
        return stack.enter_context(nc.sbuf_tensor(name, shape, dt))

    def ps(name, shape, dt, stack):
        return stack.enter_context(nc.psum_tensor(name, shape, dt))

    with SB:
        s_ident = sb("s_ident", [128, 128], BF16, SB)
        s_ones_col = sb("s_ones_col", [128, 1], BF16, SB)
        s_ones_row = sb("s_ones_row", [1, 128], F32, SB)
        s_bias = sb("s_bias", [1, D], F32, SB)
        s_m = sb("s_m", [128, 8], F32, SB)
        s_m1 = sb("s_m1", [128, 8], F32, SB)
        stat2b = [sb("statA", [128, 20], F32, SB),
                  sb("statB", [128, 20], F32, SB)]
        s_recip = [sb("s_recip0", [1, 512], F32, SB),
                   sb("s_recip1", [1, 512], F32, SB)]

        def setup_y(y):
            y.dma_start(out=s_ident[:], in_=d_ident.ap()).then_inc(sems["ld"], 16)
            y.dma_start(out=s_bias[:], in_=d_bias.ap()).then_inc(sems["ld"], 16)

        p("y", setup_y)
        ld_setup = ctrs["ld"].bump(32)

        def setup_g(g):
            g.memset(s_ones_col[:], 1.0)
            g.memset(s_ones_row[:], 1.0).then_inc(sems["g1"], 1)

        p("g", setup_g)
        g1_setup = ctrs["g1"].bump(1)

        def emit_lnstats(stat, buf, ldwait, wdbg, sqscr, p7):
            wsem, wval = ldwait

            def v0(v, wval=wval):
                v.wait_ge(sems[wsem], wval)
                v.reduce_sum(stat[:, 0:1], buf[:],
                             axis=mybir.AxisListType.X).then_inc(sems["v1"], 1)

            p("v", v0)
            tv0 = ctrs["v1"].bump(1)

            def s0(s, wval=wval, wdbg=wdbg, tv0=tv0):
                s.wait_ge(sems[wsem], wval)
                if wdbg:
                    s.wait_ge(sems["st"], wdbg)
                s.activation(sqscr[:], buf[:], AF.Square, accum_out=stat[:, 2:3])
                s.wait_ge(sems["v1"], tv0)
                s.activation(stat[:, 1:2], stat[:, 0:1], AF.Copy,
                             scale=1.0 / D).then_inc(sems["sc"], 1)

            p("s", s0)
            ts0 = ctrs["sc"].bump(1)

            def v1f(v, ts0=ts0):
                v.wait_ge(sems["sc"], ts0)
                v.tensor_scalar(stat[:, 3:4], stat[:, 2:3], 1.0 / D, None, ALU.mult)
                v.scalar_tensor_tensor(stat[:, 4:5], stat[:, 1:2], -1.0,
                                       stat[:, 1:2], ALU.mult,
                                       ALU.mult).then_inc(sems["v1"], 1)

            p("v", v1f)
            tv1 = ctrs["v1"].bump(1)

            def s1(s, tv1=tv1):
                s.wait_ge(sems["v1"], tv1)
                s.add(stat[:, 6:7], stat[:, 3:4], stat[:, 4:5]).then_inc(sems["sc"], 1)

            p("s", s1)
            ts1 = ctrs["sc"].bump(1)

            def v2(v, ts1=ts1):
                v.wait_ge(sems["sc"], ts1)
                v.tensor_scalar(stat[:, 7:8], stat[:, 6:7], EPS, None,
                                ALU.add).then_inc(sems["v1"], 1)

            p("v", v2)
            tv2 = ctrs["v1"].bump(1)

            def s2(s, tv2=tv2):
                s.wait_ge(sems["v1"], tv2)
                s.activation(stat[:, 8:9], stat[:, 7:8], AF.Ln).then_inc(sems["sc"], 1)

            p("s", s2)
            ts2 = ctrs["sc"].bump(1)

            def v3(v, ts2=ts2):
                v.wait_ge(sems["sc"], ts2)
                v.tensor_copy(stat[:, 9:10], stat[:, 8:9]).then_inc(sems["v1"], 1)

            p("v", v3)
            tv3 = ctrs["v1"].bump(1)

            def s3(s, tv3=tv3):
                s.wait_ge(sems["v1"], tv3)
                s.activation(stat[:, 10:11], stat[:, 9:10], AF.Exp,
                             scale=-0.5).then_inc(sems["sc"], 1)

            p("s", s3)
            ts3 = ctrs["sc"].bump(1)

            def v4(v, ts3=ts3):
                v.wait_ge(sems["sc"], ts3)
                v.tensor_mul(stat[:, 11:12], stat[:, 10:11], stat[:, 10:11])
                v.tensor_copy(stat[:, 5:6], stat[:, 10:11]).then_inc(sems["v1"], 1)

            p("v", v4)
            tv4 = ctrs["v1"].bump(1)

            def s4(s, tv4=tv4):
                s.wait_ge(sems["v1"], tv4)
                s.mul(stat[:, 12:13], stat[:, 11:12], stat[:, 7:8]).then_inc(sems["sc"], 1)

            p("s", s4)
            ts4 = ctrs["sc"].bump(1)

            def v5(v, ts4=ts4):
                v.wait_ge(sems["sc"], ts4)
                v.tensor_scalar(stat[:, 14:15], stat[:, 12:13], -0.5, 1.5,
                                ALU.mult, ALU.add).then_inc(sems["v1"], 1)

            p("v", v5)
            tv5 = ctrs["v1"].bump(1)

            def s5(s, tv5=tv5):
                s.wait_ge(sems["v1"], tv5)
                s.mul(stat[:, 13:14], stat[:, 14:15],
                      stat[:, 5:6]).then_inc(sems["sc"], 1)

            p("s", s5)
            ts5 = ctrs["sc"].bump(1)
            if not p7:
                return ts5

            def v6(v, ts5=ts5):
                v.wait_ge(sems["sc"], ts5)
                v.tensor_mul(stat[:, 15:16], stat[:, 13:14], stat[:, 13:14])
                v.tensor_scalar(stat[:, 17:18], stat[:, 6:7], float(D), None,
                                ALU.mult).then_inc(sems["v1"], 1)

            p("v", v6)
            tv6 = ctrs["v1"].bump(1)

            def s6(s, tv6=tv6):
                s.wait_ge(sems["v1"], tv6)
                s.mul(stat[:, 16:17], stat[:, 15:16],
                      stat[:, 17:18]).then_inc(sems["sc"], 1)

            p("s", s6)
            return ctrs["sc"].bump(1)

        with ExitStack() as DQ:
            HW = 2048
            s_lut4 = sb("s_lut4", [128, 4 * 16], F32, DQ)
            s_lut8 = sb("s_lut8", [128, 4 * 256], F32, DQ)
            s_qw = sb("s_qw", [128, 8 * 64], I32, DQ)
            s_ctrI = [sb("s_ctrI0", [128, HW], I32, DQ),
                      sb("s_ctrI1", [128, HW], I32, DQ)]
            s_idx8I = sb("s_idx8I", [128, HW], I32, DQ)
            s_ctr = sb("s_ctr", [128, HW], BF16, DQ)
            s_idx8 = sb("s_idx8", [128, HW], BF16, DQ)
            s_idx4 = sb("s_idx4", [128, HW], BF16, DQ)
            s_wl = sb("s_wl", [128, HW], BF16, DQ)
            s_wh = sb("s_wh", [128, HW], BF16, DQ)

            def lut_loads(y):
                for j in range(4):
                    y.dma_start(out=s_lut4[:, 16 * j:16 * (j + 1)],
                                in_=d_lut4[128 * j:128 * (j + 1), :]).then_inc(sems["ld"], 16)
                    y.dma_start(out=s_lut8[:, 256 * j:256 * (j + 1)],
                                in_=d_lut8[128 * j:128 * (j + 1), :]).then_inc(sems["ld"], 16)

            p("y", lut_loads)
            ld_luts = ctrs["ld"].bump(8 * 16)

            def emit_sweep():
                last_store = [0]
                for j in range(4):
                    for hf in range(2):
                        uprev = ctrs["u"].v

                        def ldqw(y, j=j, hf=hf, uprev=uprev):
                            y.wait_ge(sems["u"], uprev)
                            src = bass.AP(d_qw, 128 * j * 128 + 64 * hf,
                                          [[128, 128], [OROWS * 128, 8], [1, 64]])
                            dst = bass.AP(s_qw, 0, [[8 * 64, 128], [64, 8], [1, 64]])
                            y.dma_start(out=dst, in_=src).then_inc(sems["qs"], 16)

                        yield ("y", ldqw)
                        qs_t = ctrs["qs"].bump(16)

                        gor = {}
                        for pl in range(8):
                            def vext(v, pl=pl, qs_t=qs_t, ld_luts=ld_luts,
                                     wprev=last_store[0], wg=gor.get(pl - 2, 0)):
                                if pl == 0:
                                    v.wait_ge(sems["qs"], qs_t)
                                    v.wait_ge(sems["ld"], ld_luts)
                                    v.wait_ge(sems["wst"], wprev)
                                v.wait_ge(sems["g1"], wg)
                                tgt = s_idx8I if pl == 0 else s_ctrI[pl % 2]
                                ins = None
                                for b in range(32):
                                    sh = b - pl
                                    op0 = (ALU.logical_shift_right if sh >= 0
                                           else ALU.logical_shift_left)
                                    dst = bass.AP(tgt, b, [[HW, 128], [32, 64]])
                                    ins = v.tensor_scalar(dst,
                                                          s_qw[:, 64 * pl:64 * (pl + 1)],
                                                          abs(sh), 1 << pl, op0,
                                                          ALU.bitwise_and)
                                ins.then_inc(sems["msk"], 1)

                            yield ("v", vext)
                            vet = ctrs["msk"].bump(1)

                            if pl >= 1:
                                def gor_fn(g, pl=pl, vet=vet):
                                    g.wait_ge(sems["msk"], vet)
                                    g.tensor_tensor(s_idx8I[:], s_idx8I[:],
                                                    s_ctrI[pl % 2][:],
                                                    ALU.add).then_inc(sems["g1"], 1)

                                yield ("g", gor_fn)
                                gor[pl] = ctrs["g1"].bump(1)

                        def vun(v, wg=gor[7]):
                            v.wait_ge(sems["g1"], wg)
                            v.tensor_copy(s_idx8[:], s_idx8I[:])
                            v.tensor_scalar(s_ctrI[0][:], s_idx8I[:], 15, None,
                                            ALU.bitwise_and)
                            v.tensor_copy(s_idx4[:], s_ctrI[0][:])
                            v.memset(s_wl[:], 0.0)
                            v.memset(s_wh[:], 0.0).then_inc(sems["u"], 1)

                        yield ("v", vun)
                        ut = ctrs["u"].bump(1)
                        if dbg and j == 0 and hf == 0:
                            def di(g, ut=ut):
                                g.wait_ge(sems["u"], ut)
                                g.dma_start(out=dbg_t["idx8"].ap(),
                                            in_=s_idx8[:, 0:512]).then_inc(sems["st"], 16)

                            yield ("g", di)
                            ctrs["st"].bump(16)

                        for c in range(16):
                            def vc4(v, c=c, j=j):
                                v.tensor_scalar(s_ctr[:], s_idx4[:], float(c),
                                                s_lut4[:, 16 * j + c:16 * j + c + 1],
                                                ALU.is_equal, ALU.mult)
                                v.tensor_add(s_wl[:], s_wl[:], s_ctr[:])

                            yield ("v", vc4)
                        for c in range(256):
                            def vc8(v, c=c, j=j):
                                v.tensor_scalar(s_ctr[:], s_idx8[:], float(c),
                                                s_lut8[:, 256 * j + c:256 * j + c + 1],
                                                ALU.is_equal, ALU.mult)
                                v.tensor_add(s_wh[:], s_wh[:], s_ctr[:])

                            yield ("v", vc8)

                        def vdone(v):
                            v.tensor_scalar(s_wh[:], s_wh[:], 0.0, None,
                                            ALU.add).then_inc(sems["sw"], 1)

                        yield ("v", vdone)
                        swt = ctrs["sw"].bump(1)
                        if dbg and j == 0 and hf == 0:
                            def dw(g, swt=swt):
                                g.wait_ge(sems["sw"], swt)
                                g.dma_start(out=dbg_t["wl"].ap(),
                                            in_=s_wl[:, 0:512]).then_inc(sems["st"], 16)

                            yield ("g", dw)
                            ctrs["st"].bump(16)

                        def stw(y, j=j, hf=hf, swt=swt):
                            y.wait_ge(sems["sw"], swt)
                            y.dma_start(out=wsp[128 * j:128 * (j + 1),
                                                HW * hf:HW * (hf + 1)],
                                        in_=s_wl[:]).then_inc(sems["wst"], 16)
                            y.dma_start(out=wsp[OROWS + 128 * j:OROWS + 128 * (j + 1),
                                                HW * hf:HW * (hf + 1)],
                                        in_=s_wh[:]).then_inc(sems["wst"], 16)

                        yield ("y", stw)
                        last_store[0] = ctrs["wst"].bump(32)

            sweep_gen = emit_sweep()
            sweep_live = [True]

            def drain_sweep(n):
                while sweep_live[0] and n > 0:
                    try:
                        eng, fn = next(sweep_gen)
                        p(eng, fn)
                        n -= 1
                    except StopIteration:
                        sweep_live[0] = False

            with ExitStack() as KV:
                s_kT = sb("s_kT", [128, S], BF16, KV)
                s_V = sb("s_V", [128, S], BF16, KV)

                with ExitStack() as P1S:
                    s_x = [sb(f"s_x{i}", [128, D], F32, P1S) for i in range(2)]
                    s_z = sb("s_z", [128, D], BF16, P1S)
                    s_sqscr = sb("s_sqscr", [128, D], F32, P1S)
                    s_xnstage = sb("s_xnstage", [128, D], BF16, P1S)
                    p_tp = [ps(f"p_tp{i}", [128, 1024], BF16, P1S) for i in range(2)]

                    apply_t = {}
                    dbg_gate = [0]
                    buf0_ref = s_x[0]
                    sq_t = {}
                    ev_t = {}
                    ttr_t = {}
                    stx_t = {}

                    for i in range(8):
                        buf = s_x[i % 2]
                        xsem = "x1" if i % 2 == 0 else "x1b"

                        def ldx(y, i=i, buf=buf, xsem=xsem,
                                wv=apply_t.get(i - 2, 0), ws=sq_t.get(i - 2, 0)):
                            y.wait_ge(sems["v1"], wv)
                            y.wait_ge(sems["sc"], ws)
                            y.dma_start(out=buf[:],
                                        in_=d_x[128 * i:128 * (i + 1), :]).then_inc(sems[xsem], 16)

                        p("y", ldx)
                        x1t = ctrs[xsem].bump(16)

                        stat = stat2b[i % 2]
                        sc_b = emit_lnstats(stat, buf, (xsem, x1t),
                                            (dbg_gate[0] if (dbg and i == 1) else 0),
                                            s_sqscr, p7=False)
                        sq_t[i] = sc_b

                        def vapply(v, buf=buf, stat=stat, sc_b=sc_b,
                                   wtp=ttr_t.get(i - 1, 0),
                                   wdbg=(dbg_gate[0] if (dbg and i == 1) else 0)):
                            v.wait_ge(sems["sc"], sc_b)
                            v.wait_ge(sems["tp"], wtp)
                            if wdbg:
                                v.wait_ge(sems["st"], wdbg)
                            v.tensor_scalar(s_z[:], buf[:], stat[:, 1:2], stat[:, 13:14],
                                            ALU.subtract, ALU.mult).then_inc(sems["v1"], 1)

                        p("v", vapply)
                        apply_t[i] = ctrs["v1"].bump(1)

                        if dbg and i == 0:
                            def dz(g, w=apply_t[i], stat=stat):
                                g.wait_ge(sems["v1"], w)
                                g.dma_start(out=dbg_t["z1"].ap(),
                                            in_=s_z[:]).then_inc(sems["st"], 16)
                                g.dma_start(out=dbg_t["x0"].ap(),
                                            in_=buf0_ref[:]).then_inc(sems["st"], 16)
                                g.dma_start(out=dbg_t["st0"].ap(),
                                            in_=stat[:]).then_inc(sems["st"], 16)

                            p("g", dz)
                            dz_t = ctrs["st"].bump(48)
                            dbg_gate[0] = dz_t

                        for gidx in range(4):
                            pb = p_tp[gidx % 2]
                            gi = 4 * i + gidx

                            def ttr(t, gidx=gidx, pb=pb, wv=apply_t[i], i=i,
                                    wev=ev_t.get((i, gidx - 2) if gidx >= 2 else (i - 1, gidx + 2), 0)):
                                if i == 0 and gidx == 0:
                                    t.wait_ge(sems["ld"], 160)
                                t.wait_ge(sems["v1"], wv)
                                t.wait_ge(sems["ev"], wev)
                                ins = None
                                for f in range(8):
                                    ft = 8 * gidx + f
                                    ins = t.transpose(pb[:, 128 * f:128 * (f + 1)],
                                                      s_z[:, 128 * ft:128 * (ft + 1)], s_ident[:])
                                ins.then_inc(sems["tp"], 1)

                            p("t", ttr)
                            tpt = ctrs["tp"].bump(1)
                            ttr_t[i] = tpt

                            def sev(s, gidx=gidx, pb=pb, tpt=tpt, wst=stx_t.get(i - 1, 0)):
                                s.wait_ge(sems["tp"], tpt)
                                if gidx == 0:
                                    s.wait_ge(sems["sx"], wst)
                                s.copy(s_xnstage[:, 1024 * gidx:1024 * (gidx + 1)],
                                       pb[:]).then_inc(sems["ev"], 1)

                            p("s", sev)
                            ev_t[(i, gidx)] = ctrs["ev"].bump(1)

                        def stx(y, i=i, w=ev_t[(i, 3)]):
                            y.wait_ge(sems["ev"], w)
                            src = bass.AP(s_xnstage, 0, [[D, 128], [128, 32], [1, 128]])
                            dst = bass.AP(agx_in, 128 * i,
                                          [[TOK, 128], [128 * TOK, 32], [1, 128]])
                            y.dma_start(out=dst, in_=src).then_inc(sems["sx"], 16)

                        p("y", stx)
                        stx_t[i] = ctrs["sx"].bump(16)

                    emit_lnstats(stat2b[0], s_x[1], ("x1b", ctrs["x1b"].v), 0,
                                 s_sqscr, p7=False)

                    ag1_after = (ctrs["x1"].v, ctrs["x1b"].v, ctrs["sx"].v)

                    def ag1(g, w=ag1_after):
                        g.wait_ge(sems["x1"], w[0])
                        g.wait_ge(sems["x1b"], w[1])
                        g.wait_ge(sems["sx"], w[2])
                        g.collective_compute("AllGather", ALU.bypass, replica_groups=RG,
                                             ins=[agx_in.ap().opt()],
                                             outs=[agx.ap().opt()]).then_inc(sems["ag"], 1)

                    p("g", ag1)
                    ag1_t = ctrs["ag"].bump(1)

                    if dbg:
                        def daxin(g, w=ag1_t):
                            g.wait_ge(sems["ag"], w)
                            g.dma_start(out=dbg_t["axin"].ap(),
                                        in_=agx_in[0:128, 0:512]).then_inc(sems["st"], 16)
                            g.dma_start(out=dbg_t["agx0"].ap(),
                                        in_=agx[0:128, 0:512]).then_inc(sems["st"], 16)

                        p("g", daxin)
                        ctrs["st"].bump(32)

                with ExitStack() as P3S:
                    s_wqkv = sb("s_wqkv", [128, 32 * 6 * 128], BF16, P3S)
                    s_xnr = [sb(f"s_xnr{i}", [128, 512], BF16, P3S) for i in range(4)]
                    s_cs = [sb(f"s_cs{i}", [128, 512], F32, P3S) for i in range(2)]
                    s_sn = [sb(f"s_sn{i}", [128, 512], F32, P3S) for i in range(2)]
                    s_scr = [sb(f"s_scr{i}", [128, 512], F32, P3S) for i in range(2)]
                    s_rot = [sb(f"s_rot{i}", [128, 512], F32, P3S) for i in range(2)]
                    s_qstage = sb("s_qstage", [128, 512], BF16, P3S)
                    s_vtb = sb("s_vtb", [128, 512], BF16, P3S)
                    p_qkv = [ps(f"p_qkv{i}", [128, 512], F32, P3S) for i in range(6)]
                    p_vt = ps("p_vt", [128, 512], BF16, P3S)

                    def wload(g, w=g1_setup, wp1=ag1_after):
                        g.wait_ge(sems["g1"], w)
                        g.wait_ge(sems["x1"], wp1[0])
                        g.wait_ge(sems["x1b"], wp1[1])
                        g.wait_ge(sems["sx"], wp1[2])
                        for ft in range(32):
                            base = ft * 6 * 128
                            g.dma_start(out=s_wqkv[:, base:base + 512],
                                        in_=d_wq[128 * ft:128 * (ft + 1), :]).then_inc(sems["ld"], 16)
                            g.dma_start(out=s_wqkv[:, base + 512:base + 640],
                                        in_=d_wk[128 * ft:128 * (ft + 1), :]).then_inc(sems["ld"], 16)
                            g.dma_start(out=s_wqkv[:, base + 640:base + 768],
                                        in_=d_wv[128 * ft:128 * (ft + 1), :]).then_inc(sems["ld"], 16)

                    p("g", wload)
                    wld_t = ctrs["ld"].bump(96 * 16)

                    scr_war = {}
                    qst_war = [0]
                    cs_war = {}
                    vt_war = [0]
                    evac3_t = {}

                    for tc in range(16):
                        rnk, ch = tc // 2, tc % 2
                        for ft in range(32):
                            ring = s_xnr[ft % 4]
                            gidx_xn = 32 * tc + ft
                            xnsem = ["xn", "xn1", "xn2", "xn3"][ft % 4]

                            def ldxn(y, tc=tc, ft=ft, ring=ring, rnk=rnk, ch=ch,
                                     xnsem=xnsem,
                                     agw_=ag1_t, xw=max(0, 32 * tc + ft - 3)):
                                y.wait_ge(sems["ag"], agw_)
                                y.wait_ge(sems["xnc"], xw)
                                y.dma_start(out=ring[:],
                                            in_=agx[D * rnk + 128 * ft:D * rnk + 128 * (ft + 1),
                                                    512 * ch:512 * (ch + 1)]).then_inc(sems[xnsem], 16)

                            p("y", ldxn)
                            xnt = ctrs[xnsem].bump(16)

                            def mm6(t, tc=tc, ft=ft, ring=ring, xnt=xnt, wld_t=wld_t,
                                    xnsem=xnsem,
                                    wev=evac3_t.get(tc - 1, 0)):
                                t.wait_ge(sems[xnsem], xnt)
                                if ft == 0:
                                    t.wait_ge(sems["ev"], wev)
                                if tc == 0 and ft == 0:
                                    t.wait_ge(sems["ld"], wld_t)
                                base = ft * 6 * 128
                                for ti in range(6):
                                    ins = t.matmul(p_qkv[ti][:],
                                                   s_wqkv[:, base + 128 * ti:base + 128 * (ti + 1)],
                                                   ring[:], start=(ft == 0), stop=(ft == 31))
                                ins.then_inc(sems["xnc"], 1)

                            p("t", mm6)
                            ctrs["xnc"].bump(1)
                        vtt = ctrs["xnc"].v

                        cssem = "cs" if tc % 2 == 0 else "csb"

                        def ldcs(y, tc=tc, w=cs_war.get(tc % 2, 0), wag=ag1_t,
                                 cssem=cssem):
                            y.wait_ge(sems["g1"], w)
                            y.wait_ge(sems["ag"], wag)
                            y.dma_start(out=s_cs[tc % 2][:],
                                        in_=d_cos[:, 512 * tc:512 * (tc + 1)]).then_inc(sems[cssem], 16)
                            y.dma_start(out=s_sn[tc % 2][:],
                                        in_=d_sin[:, 512 * tc:512 * (tc + 1)]).then_inc(sems[cssem], 16)

                        p("y", ldcs)
                        cst = ctrs[cssem].bump(32)

                        for ti in range(5):
                            scr = s_scr[ti % 2]
                            rot = s_rot[ti % 2]

                            def sev2(s, ti=ti, scr=scr, vtt=vtt, w=scr_war.get(ti % 2, 0)):
                                s.wait_ge(sems["xnc"], vtt)
                                s.wait_ge(sems["g1"], w)
                                s.copy(scr[:], p_qkv[ti][:]).then_inc(sems["ev"], 1)

                            p("s", sev2)
                            evt2 = ctrs["ev"].bump(1)

                            rotsem = "rot" if ti % 2 == 0 else "rotb"

                            def rotd(y, scr=scr, rot=rot, evt2=evt2, rotsem=rotsem):
                                y.wait_ge(sems["ev"], evt2)
                                y.dma_start(out=rot[0:64, :], in_=scr[64:128, :]).then_inc(sems[rotsem], 16)
                                y.dma_start(out=rot[64:128, :], in_=scr[0:64, :]).then_inc(sems[rotsem], 16)

                            p("y", rotd)
                            rott = ctrs[rotsem].bump(32)

                            def grope(g, ti=ti, tc=tc, scr=scr, rot=rot, rott=rott,
                                      cst=cst, evt2=evt2, qw=qst_war[0],
                                      rotsem=rotsem, cssem=cssem):
                                g.wait_ge(sems[rotsem], rott)
                                g.wait_ge(sems[cssem], cst)
                                g.wait_ge(sems["ev"], evt2)
                                if ti < 4:
                                    g.wait_ge(sems["qp"], qw)
                                cs_ = s_cs[tc % 2]
                                sn_ = s_sn[tc % 2]
                                g.tensor_mul(scr[:], scr[:], cs_[:])
                                g.tensor_mul(rot[:], rot[:], sn_[:])
                                if ti < 4:
                                    g.tensor_add(s_qstage[:], scr[:], rot[:]).then_inc(sems["g1"], 1)
                                else:
                                    g.tensor_add(s_kT[:, 512 * tc:512 * (tc + 1)],
                                                 scr[:], rot[:]).then_inc(sems["g1"], 1)

                            p("g", grope)
                            g1t = ctrs["g1"].bump(1)
                            scr_war[ti % 2] = g1t
                            cs_war[tc % 2] = g1t

                            if ti < 4:
                                def stq(y, ti=ti, tc=tc, g1t=g1t):
                                    y.wait_ge(sems["g1"], g1t)
                                    y.dma_start(out=qspill[128 * ti:128 * (ti + 1),
                                                           512 * tc:512 * (tc + 1)],
                                                in_=s_qstage[:]).then_inc(sems["qp"], 16)

                                p("y", stq)
                                qst_war[0] = ctrs["qp"].bump(16)
                            if dbg and tc == 0 and ti == 4:
                                def dk(g, g1t=g1t, scr=scr, rot=rot, tc=tc):
                                    g.wait_ge(sems["g1"], g1t)
                                    g.dma_start(out=dbg_t["kT"].ap(),
                                                in_=s_kT[:, 0:512]).then_inc(sems["st"], 16)
                                    g.dma_start(out=dbg_t["scr"].ap(),
                                                in_=scr[:]).then_inc(sems["st"], 16)
                                    g.dma_start(out=dbg_t["rot"].ap(),
                                                in_=rot[:]).then_inc(sems["st"], 16)
                                    g.dma_start(out=dbg_t["cs"].ap(),
                                                in_=s_cs[tc % 2][:]).then_inc(sems["st"], 16)
                                    g.dma_start(out=dbg_t["sn"].ap(),
                                                in_=s_sn[tc % 2][:]).then_inc(sems["st"], 16)

                                p("g", dk)
                                ctrs["st"].bump(16 * 5)

                        def sev3(s, vtt=vtt, w=vt_war[0]):
                            s.wait_ge(sems["xnc"], vtt)
                            s.wait_ge(sems["ev"], w)
                            s.copy(s_vtb[:], p_qkv[5][:]).then_inc(sems["ev"], 1)

                        p("s", sev3)
                        evt3 = ctrs["ev"].bump(1)
                        evac3_t[tc] = evt3

                        def tvt(t, evt3=evt3, wvt=vt_war[0]):
                            t.wait_ge(sems["ev"], evt3)
                            t.wait_ge(sems["ev"], wvt)
                            ins = None
                            for f in range(4):
                                ins = t.transpose(p_vt[:, 128 * f:128 * (f + 1)],
                                                  s_vtb[:, 128 * f:128 * (f + 1)], s_ident[:])
                            ins.then_inc(sems["tp"], 1)

                        p("t", tvt)
                        tpt2 = ctrs["tp"].bump(1)

                        def sev4(s, tpt2=tpt2, tc=tc):
                            s.wait_ge(sems["tp"], tpt2)
                            s.copy(s_V[:, 512 * tc:512 * (tc + 1)],
                                   p_vt[:]).then_inc(sems["ev"], 1)

                        p("s", sev4)
                        vt_war[0] = ctrs["ev"].bump(1)

                        drain_sweep(20)

                    p3_ev = ctrs["ev"].v
                    p3_g1 = ctrs["g1"].v
                    p3_qs = ctrs["qs"].v
                    p3_qp = ctrs["qp"].v

                with ExitStack() as P4S:
                    s_masks = sb("s_masks", [128, 4 * 512], BF16, P4S)
                    s_qt = [sb(f"s_qt{i}", [128, 1024], BF16, P4S) for i in range(2)]
                    s_es = [sb(f"s_es{i}", [128, 1024], BF16, P4S) for i in range(2)]
                    s_ot = sb("s_ot", [128, 1024], BF16, P4S)
                    s_rb = sb("s_rb", [128, 512], F32, P4S)
                    s_dbgk = sb("s_dbgk", [128, 512], BF16, P4S) if dbg else None
                    p_S = ps("p_S", [128, 1024], F32, P4S)
                    p_oT = [ps(f"p_oT{i}", [128, 1024], F32, P4S) for i in range(2)]
                    p_den = ps("p_den", [128, 512], F32, P4S)
                    p_bc = ps("p_bc", [128, 512], F32, P4S)

                    def ldmask(y, wev=p3_ev, wg=p3_g1, wq=p3_qs, wqp=p3_qp):
                        y.wait_ge(sems["ev"], wev)
                        y.wait_ge(sems["g1"], wg)
                        y.wait_ge(sems["qs"], wq)
                        y.wait_ge(sems["qp"], wqp)
                        src = bass.AP(d_masks, 0, [[512, 128], [128 * 512, 4], [1, 512]])
                        dst = bass.AP(s_masks, 0, [[4 * 512, 128], [512, 4], [1, 512]])
                        y.dma_start(out=dst, in_=src).then_inc(sems["ld"], 16)

                    p("y", ldmask)
                    mask_t = ctrs["ld"].bump(16)

                    if dbg:
                        def dkcap(s, w=p3_g1):
                            s.wait_ge(sems["g1"], w)
                            s.copy(s_dbgk[:], s_kT[:, 0:512]).then_inc(sems["sc"], 1)

                        p("s", dkcap)
                        dk_t = ctrs["sc"].bump(1)

                        def dkd(g, w=dk_t):
                            g.wait_ge(sems["sc"], w)
                            g.dma_start(out=dbg_t["kTlate"].ap(),
                                        in_=s_dbgk[:]).then_inc(sems["st"], 16)

                        p("g", dkd)
                        ctrs["st"].bump(16)

                    q_war = {0: 0, 1: 0}
                    es_war = {0: 0, 1: 0}
                    nrm_war = {0: 0, 1: 0}
                    rec_war = [0]
                    bc_war = [0]
                    ot_war = [0]
                    rb_war = [0]

                    for qc in range(16):
                        ntile = 4 * qc + 4
                        for pr in range(2):
                            qb = s_qt[pr]

                            q4sem = "q4" if pr == 0 else "q4b"

                            def ldq(y, qc=qc, pr=pr, qb=qb, p3_qp=p3_qp, w=q_war[pr],
                                    wev=p3_ev, q4sem=q4sem):
                                y.wait_ge(sems["qp"], p3_qp)
                                y.wait_ge(sems["ev"], wev)
                                y.wait_ge(sems["av"], w)
                                for hl in range(2):
                                    h = 2 * pr + hl
                                    y.dma_start(out=qb[:, 512 * hl:512 * (hl + 1)],
                                                in_=qspill[128 * h:128 * (h + 1),
                                                           512 * qc:512 * (qc + 1)]).then_inc(sems[q4sem], 16)

                            p("y", ldq)
                            q4t = ctrs[q4sem].bump(32)

                            for t_i in range(ntile):
                                es = s_es[t_i % 2]
                                diag = t_i >= 4 * qc
                                di = t_i - 4 * qc

                                def smm(t, t_i=t_i, qb=qb, q4t=q4t, qc=qc, pr=pr,
                                        p3_ev=p3_ev, p3_g1=p3_g1, expw=ctrs["exp"].v,
                                        q4sem=q4sem, diag=diag, di=di, mask_t=mask_t):
                                    if t_i == 0:
                                        t.wait_ge(sems[q4sem], q4t)
                                        if qc == 0 and pr == 0:
                                            t.wait_ge(sems["g1"], p3_g1)
                                            t.wait_ge(sems["ev"], p3_ev)
                                    if diag:
                                        t.wait_ge(sems["ld"], mask_t)
                                    t.wait_ge(sems["exp"], expw)
                                    ins = None
                                    for hl in range(2):
                                        ins = t.matmul(p_S[:, 512 * hl:512 * (hl + 1)],
                                                       s_kT[:, 128 * t_i:128 * (t_i + 1)],
                                                       qb[:, 512 * hl:512 * (hl + 1)],
                                                       start=True, stop=not diag)
                                        if diag:
                                            ins = t.matmul(p_S[:, 512 * hl:512 * (hl + 1)],
                                                           s_ident[:],
                                                           s_masks[:, 512 * di:512 * (di + 1)],
                                                           start=False, stop=True)
                                    ins.then_inc(sems["av"], 1)

                                p("t", smm)
                                a1 = ctrs["av"].bump(1)
                                q_war[pr] = a1

                                def sexp(s, a1=a1, es=es, esw=es_war[t_i % 2]):
                                    s.wait_ge(sems["av"], a1)
                                    s.wait_ge(sems["av"], esw)
                                    s.activation(es[:], p_S[:], AF.Exp,
                                                 scale=SCALE).then_inc(sems["exp"], 1)

                                p("s", sexp)
                                expt = ctrs["exp"].bump(1)

                                def avmm(t, t_i=t_i, es=es, expt=expt, pr=pr, qc=qc,
                                         wn=nrm_war[pr], wr=rec_war[0]):
                                    t.wait_ge(sems["exp"], expt)
                                    if t_i == 0:
                                        t.wait_ge(sems["nrm"], wn)
                                        t.wait_ge(sems["rec"], wr)
                                    first = t_i == 0
                                    last = t_i == 4 * qc + 3
                                    ins = None
                                    for hl in range(2):
                                        base = 32 * hl
                                        t.matmul(p_oT[pr][:, 512 * hl:512 * (hl + 1)],
                                                 s_V[:, 128 * t_i:128 * (t_i + 1)],
                                                 es[:, 512 * hl:512 * (hl + 1)],
                                                 start=first, stop=last)
                                        ins = t.matmul(p_den[base:base + 1, :],
                                                       s_ones_col[:],
                                                       es[:, 512 * hl:512 * (hl + 1)],
                                                       start=first, stop=last,
                                                       tile_position=(0, base))
                                    ins.then_inc(sems["av"], 1)

                                p("t", avmm)
                                es_war[t_i % 2] = ctrs["av"].bump(1)

                            av_end = ctrs["av"].v
                            if dbg and qc == 9 and pr == 1:
                                def dexp9(g, av_end=av_end, par=(4 * 9 + 4 - 1) % 2):
                                    g.wait_ge(sems["av"], av_end)
                                    g.dma_start(out=dbg_t["es9"].ap(),
                                                in_=s_es[par][:]).then_inc(sems["st"], 16)

                                p("g", dexp9)
                                ctrs["st"].bump(16)
                            if dbg and qc == 1 and pr == 0:
                                def dexp(g, av_end=av_end, par=(4 * 1 + 4 - 1) % 2):
                                    g.wait_ge(sems["av"], av_end)
                                    g.dma_start(out=dbg_t["expS"].ap(),
                                                in_=s_es[par][:]).then_inc(sems["st"], 16)

                                p("g", dexp)
                                ctrs["st"].bump(16)

                            nrm0 = ctrs["nrm"].v
                            for hl in range(2):
                                def srec(s, hl=hl, av_end=av_end):
                                    s.wait_ge(sems["av"], av_end)
                                    base = 32 * hl
                                    s.activation(s_recip[hl][:],
                                                 p_den[base:base + 1, :], AF.Ln)
                                    s.activation(s_recip[hl][:],
                                                 s_recip[hl][:], AF.Exp,
                                                 scale=-1.0).then_inc(sems["rec"], 1)

                                p("s", srec)
                                rect = ctrs["rec"].bump(1)
                                rec_war[0] = rect
                                if dbg and qc == 1 and pr == 0 and hl == 0:
                                    def dden(g, rect=rect):
                                        g.wait_ge(sems["rec"], rect)
                                        g.dma_start(out=dbg_t["den"].ap(),
                                                    in_=s_recip[0][:]).then_inc(sems["st"], 16)

                                    p("g", dden)
                                    ctrs["st"].bump(16)

                                def tbc(t, hl=hl, rect=rect, w=bc_war[0]):
                                    t.wait_ge(sems["rec"], rect)
                                    t.wait_ge(sems["rec"], w)
                                    t.matmul(p_bc[:], s_ones_row[:],
                                             s_recip[hl][:], start=True,
                                             stop=True).then_inc(sems["bc"], 1)

                                p("t", tbc)
                                bct = ctrs["bc"].bump(1)

                                def sbcc(s, bct=bct, wrb=rb_war[0]):
                                    s.wait_ge(sems["bc"], bct)
                                    s.wait_ge(sems["nrm"], wrb)
                                    s.copy(s_rb[:], p_bc[:]).then_inc(sems["rec"], 1)

                                p("s", sbcc)
                                rbt = ctrs["rec"].bump(1)
                                bc_war[0] = rbt

                                def vnrm(v, hl=hl, pr=pr, rbt=rbt, w=ot_war[0]):
                                    v.wait_ge(sems["rec"], rbt)
                                    if hl == 0:
                                        v.wait_ge(sems["ao"], w)
                                    v.tensor_mul(s_ot[:, 512 * hl:512 * (hl + 1)],
                                                 p_oT[pr][:, 512 * hl:512 * (hl + 1)],
                                                 s_rb[:]).then_inc(sems["nrm"], 1)

                                p("v", vnrm)
                                rb_war[0] = ctrs["nrm"].bump(1)
                            nrm_war[pr] = ctrs["nrm"].v

                            nrmt = ctrs["nrm"].v

                            def sto(y, qc=qc, pr=pr, nrmt=nrmt):
                                y.wait_ge(sems["nrm"], nrmt)
                                for hl in range(2):
                                    h = 2 * pr + hl
                                    y.dma_start(out=ago_in[128 * h:128 * (h + 1),
                                                           512 * qc:512 * (qc + 1)],
                                                in_=s_ot[:, 512 * hl:512 * (hl + 1)]).then_inc(sems["ao"], 16)

                            p("y", sto)
                            ot_war[0] = ctrs["ao"].bump(32)
                            if dbg and qc == 9 and pr == 1:
                                def dot9(g, nrmt=nrmt):
                                    g.wait_ge(sems["nrm"], nrmt)
                                    g.dma_start(out=dbg_t["oT9"].ap(),
                                                in_=s_ot[:]).then_inc(sems["st"], 16)
                                    g.dma_start(out=dbg_t["den9"][0:1, :],
                                                in_=s_recip[0][:]).then_inc(sems["st"], 16)
                                    g.dma_start(out=dbg_t["den9"][1:2, :],
                                                in_=s_recip[1][:]).then_inc(sems["st"], 16)

                                p("g", dot9)
                                ctrs["st"].bump(48)
                            if dbg and qc == 0 and pr == 0:
                                def dot(g, nrmt=nrmt):
                                    g.wait_ge(sems["nrm"], nrmt)
                                    g.dma_start(out=dbg_t["oT"].ap(),
                                                in_=s_ot[:, 0:512]).then_inc(sems["st"], 16)

                                p("g", dot)
                                ctrs["st"].bump(16)

                            drain_sweep(45)

                    ao_all = ctrs["ao"].v

                    def ag2(g, w=ao_all):
                        g.wait_ge(sems["ao"], w)
                        g.collective_compute("AllGather", ALU.bypass, replica_groups=RG,
                                             ins=[ago_in.ap().opt()],
                                             outs=[ago.ap().opt()]).then_inc(sems["ag"], 1)

                    p("g", ag2)
                    ag2_t = ctrs["ag"].bump(1)

            with ExitStack() as P6S:
                s_woH = [sb(f"s_wo{i}", [128, 16 * 512], BF16, P6S) for i in range(2)]
                s_xc = sb("s_xc", [128, 512], F32, P6S)
                s_x2st = sb("s_x2st", [128, 512], F32, P6S)
                s_do2 = sb("s_do2", [128, 512], F32, P6S) if dbg else None
                AG6S = ExitStack()
                s_ag6 = sb("s_ag6", [128, 32 * 1024], BF16, AG6S)
                p_o2 = [ps(f"p_o2{i}", [128, 512], F32, P6S) for i in range(2)]
                p_wt = [ps(f"p_wt{i}", [128, 1024], BF16, P6S) for i in range(2)]

                def ldago(g, w=ag2_t):
                    g.wait_ge(sems["ag"], w)
                    pid = nc.partition_id([mybir.EngineType.Pool])
                    with g.register("agoff0") as r0:
                        g.reg_alu(r0, pid, TOK, ALU.mult)
                        with g.register("agoff") as r:
                            for ch in range(8):
                                g.reg_alu(r, r0, 512 * ch * S, ALU.add)
                                g.dma_start(
                                    out=bass.AP(agom, 512 * ch * TOK, [[TOK, 512], [1, TOK]]),
                                    in_=bass.AP(ago, r, [[S, 512], [1, TOK]]),
                                ).then_inc(sems["ao"], 16)
                    g.wait_ge(sems["ao"], stage_done)
                    for ct in range(32):
                        row0 = 512 * (ct // 4) + 128 * (ct % 4)
                        g.dma_start(out=s_ag6[:, 1024 * ct:1024 * (ct + 1)],
                                    in_=agom[row0:row0 + 128, :]).then_inc(sems["ao"], 16)

                stage_done = ctrs["ao"].v + 8 * 16
                p("g", ldago)
                ago_ld = ctrs["ao"].bump(40 * 16)

                if dbg:
                    def dag6(g, w=ago_ld):
                        g.wait_ge(sems["ao"], w)
                        g.dma_start(out=dbg_t["ag6"].ap(),
                                    in_=s_ag6[:, 0:1024]).then_inc(sems["st"], 16)

                    p("g", dag6)
                    ctrs["st"].bump(16)

                woh_war = {}
                ep6_war = {}
                xc_war = [0]
                x2st_war = [0]

                for F in range(8):
                    woth = {}
                    for h in range(2):
                        wsem = "wo" if h == 0 else "wo2"

                        def ldwoH(g, F=F, h=h, wsem=wsem,
                                  w=woh_war.get((F - 1, h), 0), wag=ag2_t):
                            g.wait_ge(sems["ag"], wag)
                            g.wait_ge(sems["mm6"], w)
                            src = bass.AP(d_wo, 512 * F + 16 * h * 128 * D,
                                          [[D, 128], [128 * D, 16], [1, 512]])
                            dst = bass.AP(s_woH[h], 0,
                                          [[16 * 512, 128], [512, 16], [1, 512]])
                            g.dma_start(out=dst, in_=src).then_inc(sems[wsem], 16)

                        p("g", ldwoH)
                        woth[h] = ctrs[wsem].bump(16)

                    for TC in range(8):
                        pb = p_o2[TC % 2]
                        g6 = 8 * F + TC

                        def mmwoA(t, F=F, TC=TC, pb=pb, wot=woth[0], ago_ld=ago_ld,
                                  wep=max(0, 8 * F + TC - 1)):
                            t.wait_ge(sems["wo"], wot)
                            if F == 0 and TC == 0:
                                t.wait_ge(sems["ao"], ago_ld)
                            t.wait_ge(sems["ep6"], wep)
                            for ct in range(16):
                                ins = t.matmul(pb[:],
                                               s_ag6[:, 1024 * ct + 128 * TC:1024 * ct + 128 * (TC + 1)],
                                               s_woH[0][:, 512 * ct:512 * (ct + 1)],
                                               start=(ct == 0), stop=False)
                            ins.then_inc(sems["mm6"], 1)

                        p("t", mmwoA)
                        mm6a = ctrs["mm6"].bump(1)

                        def mmwoB(t, F=F, TC=TC, pb=pb, wot=woth[1]):
                            t.wait_ge(sems["wo2"], wot)
                            for ct in range(16):
                                ins = t.matmul(pb[:],
                                               s_ag6[:, 1024 * (16 + ct) + 128 * TC:1024 * (16 + ct) + 128 * (TC + 1)],
                                               s_woH[1][:, 512 * ct:512 * (ct + 1)],
                                               start=False, stop=(ct == 15))
                            ins.then_inc(sems["mm6"], 1)

                        p("t", mmwoB)
                        mm6t = ctrs["mm6"].bump(1)
                        if TC == 7:
                            woh_war[(F, 0)] = mm6a
                            woh_war[(F, 1)] = mm6t

                        if False and dbg and F == 0 and TC == 0:
                            def do2(s, pb=pb, w=mm6t):
                                s.wait_ge(sems["mm6"], w)
                                s.copy(s_do2[:], pb[:]).then_inc(sems["sc"], 1)

                            p("s", do2)
                            do2t = ctrs["sc"].bump(1)

                            def do2d(g, w=do2t):
                                g.wait_ge(sems["sc"], w)
                                g.dma_start(out=dbg_t["o2"].ap(),
                                            in_=s_do2[:]).then_inc(sems["st"], 16)

                            p("g", do2d)
                            ctrs["st"].bump(16)

                        def ldxc(y, F=F, TC=TC, w=g6):
                            y.wait_ge(sems["ep6"], w)
                            y.dma_start(out=s_xc[:],
                                        in_=bass.AP(d_x, 128 * TC * D + 512 * F,
                                                    [[D, 128], [1, 512]])).then_inc(sems["x2"], 16)

                        p("y", ldxc)
                        xct = ctrs["x2"].bump(16)

                        def vep6(v, pb=pb, mm6t=mm6t, xct=xct, w=x2st_war[0]):
                            v.wait_ge(sems["mm6"], mm6t)
                            v.wait_ge(sems["x2"], xct)
                            v.wait_ge(sems["x2s"], w)
                            v.tensor_add(s_x2st[:], pb[:], s_xc[:]).then_inc(sems["ep6"], 1)

                        p("v", vep6)
                        ep6t = ctrs["ep6"].bump(1)

                        def stx2(y, F=F, TC=TC, ep6t=ep6t):
                            y.wait_ge(sems["ep6"], ep6t)
                            y.dma_start(out=bass.AP(x2sp, 128 * TC * D + 512 * F,
                                                    [[D, 128], [1, 512]]),
                                        in_=s_x2st[:]).then_inc(sems["x2s"], 16)

                        p("y", stx2)
                        x2st_war[0] = ctrs["x2s"].bump(16)

                        drain_sweep(14)

                drain_sweep(10 ** 9)
                sw_all = ctrs["sw"].v
                wst_all = ctrs["wst"].v
                mm6_all = ctrs["mm6"].v
                AG6S.close()
                s_wts = sb("s_wts", [128, D], BF16, P6S)
                s_wtst = sb("s_wtst", [128, D], BF16, P6S)

                wts_war = [0]
                tp6_war = [0]
                ev6_war = {}
                for wi in range(2):
                    for j in range(4):
                        def ldwt(y, wi=wi, j=j, w=wst_all, wtp=tp6_war[0], wag=ag2_t,
                                 wmm=mm6_all):
                            y.wait_ge(sems["ag"], wag)
                            y.wait_ge(sems["wst"], w)
                            y.wait_ge(sems["tp"], wtp)
                            y.wait_ge(sems["mm6"], wmm)
                            y.dma_start(out=s_wts[:],
                                        in_=wsp[OROWS * wi + 128 * j:OROWS * wi + 128 * (j + 1),
                                                :]).then_inc(sems["wt6"], 16)

                        p("y", ldwt)
                        wtt = ctrs["wt6"].bump(16)

                        for gidx in range(4):
                            pbw = p_wt[gidx % 2]
                            key = (wi, j, gidx)
                            prev_key = (wi, j, gidx - 2) if gidx >= 2 else \
                                ((wi, j - 1, gidx + 2) if j >= 1 else
                                 ((wi - 1, 3, gidx + 2) if wi >= 1 else None))

                            def twt(t, gidx=gidx, pbw=pbw, wtt=wtt,
                                    wev=ev6_war.get(prev_key, 0)):
                                t.wait_ge(sems["wt6"], wtt)
                                t.wait_ge(sems["ev"], wev)
                                ins = None
                                for f in range(8):
                                    ft = 8 * gidx + f
                                    ins = t.transpose(pbw[:, 128 * f:128 * (f + 1)],
                                                      s_wts[:, 128 * ft:128 * (ft + 1)], s_ident[:])
                                ins.then_inc(sems["tp"], 1)

                            p("t", twt)
                            tptw = ctrs["tp"].bump(1)
                            tp6_war[0] = tptw

                            def swev(s, gidx=gidx, pbw=pbw, tptw=tptw, w=wts_war[0]):
                                s.wait_ge(sems["tp"], tptw)
                                if gidx == 0:
                                    s.wait_ge(sems["wst"], w)
                                s.copy(s_wtst[:, 1024 * gidx:1024 * (gidx + 1)],
                                       pbw[:]).then_inc(sems["ev"], 1)

                            p("s", swev)
                            ev6_war[key] = ctrs["ev"].bump(1)

                        def stwT(y, wi=wi, j=j, w=ev6_war[(wi, j, 3)]):
                            y.wait_ge(sems["ev"], w)
                            src = bass.AP(s_wtst, 0, [[D, 128], [128, 32], [1, 128]])
                            dst = bass.AP(agw_in, OROWS * wi + 128 * j,
                                          [[2 * OROWS, 128], [128 * 2 * OROWS, 32], [1, 128]])
                            y.dma_start(out=dst, in_=src).then_inc(sems["wst"], 16)

                        p("y", stwT)
                        wts_war[0] = ctrs["wst"].bump(16)

                agw_done = ctrs["wst"].v

                def ag3(g, w=agw_done):
                    g.wait_ge(sems["wst"], w)
                    g.collective_compute("AllGather", ALU.bypass, replica_groups=RG,
                                         ins=[agw_in.ap().opt()],
                                         outs=[agw.ap().opt()]).then_inc(sems["ag"], 1)

                p("g", ag3)
                ag3_t = ctrs["ag"].bump(1)
                p6_x2s = ctrs["x2s"].v

                if dbg:
                    def dx2sp(g, w=p6_x2s):
                        g.wait_ge(sems["x2s"], w)
                        g.dma_start(out=dbg_t["x2sp"].ap(),
                                    in_=x2sp[0:128, :]).then_inc(sems["st"], 16)
                        g.dma_start(out=dbg_t["x2sp7a"].ap(),
                                    in_=x2sp[896:1024, :]).then_inc(sems["st"], 16)

                    p("g", dx2sp)
                    ctrs["st"].bump(32)

        with ExitStack() as X2N:
            s_x2nT = sb("s_x2nT", [128, 32 * TOK], BF16, X2N)
            x2nT_ref = s_x2nT
            with ExitStack() as P7S:
                s_x2r = [sb(f"s_x2r{i}", [128, D], F32, P7S) for i in range(2)]
                s_z2 = sb("s_z2", [128, D], BF16, P7S)
                s_sq2 = sb("s_sq2", [128, D], F32, P7S)
                p_t7 = [ps(f"p_t7{i}", [128, 1024], BF16, P7S) for i in range(2)]

                v7_done = {}
                s7_done = {}
                ev7 = {}
                t7_t = {}
                for TC in range(8):
                    buf = s_x2r[TC % 2]
                    x7sem = "x7" if TC % 2 == 0 else "x7b"

                    def ldx2(y, TC=TC, buf=buf, w=p6_x2s, wagw=agw_done, x7sem=x7sem,
                             wv=v7_done.get(TC - 2, 0), ws=s7_done.get(TC - 2, 0)):
                        y.wait_ge(sems["x2s"], w)
                        y.wait_ge(sems["wst"], wagw)
                        y.wait_ge(sems["v1"], wv)
                        y.wait_ge(sems["sc"], ws)
                        y.dma_start(out=buf[:],
                                    in_=x2sp[128 * TC:128 * (TC + 1), :]).then_inc(sems[x7sem], 16)

                    p("y", ldx2)
                    x2t = ctrs[x7sem].bump(16)

                    stat = stat2b[TC % 2]
                    s7rt = emit_lnstats(stat, buf, (x7sem, x2t), 0, s_sq2, p7=True)
                    s7_done[TC] = s7rt

                    def v7c(v, buf=buf, stat=stat, s7rt=s7rt, TC=TC,
                            wtp=t7_t.get(TC - 1, 0)):
                        v.wait_ge(sems["sc"], s7rt)
                        v.wait_ge(sems["tp"], wtp)
                        v.tensor_scalar(s_m[:, TC:TC + 1], stat[:, 16:17], 4096.0, None, ALU.is_gt)
                        v.tensor_scalar(s_m1[:, TC:TC + 1], stat[:, 16:17], 4096.0, None, ALU.is_le)
                        v.tensor_scalar(s_z2[:], buf[:], stat[:, 1:2], stat[:, 13:14],
                                        ALU.subtract, ALU.mult).then_inc(sems["v1"], 1)

                    p("v", v7c)
                    v7_done[TC] = ctrs["v1"].bump(1)
                    if dbg and TC == 0:
                        def dx2(g, w=v7_done[TC], buf=buf):
                            g.wait_ge(sems["v1"], w)
                            g.dma_start(out=dbg_t["x2"].ap(), in_=buf[:]).then_inc(sems["st"], 16)
                            g.dma_start(out=dbg_t["mcol"].ap(), in_=s_m[:]).then_inc(sems["st"], 16)

                        p("g", dx2)
                        ctrs["st"].bump(32)

                    for gidx in range(4):
                        pb7 = p_t7[gidx % 2]
                        key7 = (TC, gidx)
                        prev7 = (TC, gidx - 2) if gidx >= 2 else (TC - 1, gidx + 2)

                        def t7(t, gidx=gidx, pb7=pb7, w=v7_done[TC],
                               wev=ev7.get(prev7, 0)):
                            t.wait_ge(sems["v1"], w)
                            t.wait_ge(sems["ev"], wev)
                            ins = None
                            for f in range(8):
                                ft = 8 * gidx + f
                                ins = t.transpose(pb7[:, 128 * f:128 * (f + 1)],
                                                  s_z2[:, 128 * ft:128 * (ft + 1)], s_ident[:])
                            ins.then_inc(sems["tp"], 1)

                        p("t", t7)
                        t7t = ctrs["tp"].bump(1)
                        t7_t[TC] = t7t

                        def s7e(s, gidx=gidx, pb7=pb7, t7t=t7t, TC=TC):
                            s.wait_ge(sems["tp"], t7t)
                            ins = None
                            for f in range(8):
                                ft = 8 * gidx + f
                                ins = s.copy(s_x2nT[:, TOK * ft + 128 * TC:TOK * ft + 128 * (TC + 1)],
                                             pb7[:, 128 * f:128 * (f + 1)])
                            ins.then_inc(sems["ev"], 1)

                        p("s", s7e)
                        ev7[key7] = ctrs["ev"].bump(1)

                emit_lnstats(stat2b[0], s_x2r[1], ("x7b", ctrs["x7b"].v), 0,
                             s_sq2, p7=True)

            p7_ev = ctrs["ev"].v
            p7_v1 = ctrs["v1"].v

            if dbg:
                def dx2nT(g, w=p7_ev):
                    g.wait_ge(sems["ev"], w)
                    g.dma_start(out=dbg_t["x2nT"].ap(),
                                in_=s_x2nT[:, 0:1024]).then_inc(sems["st"], 16)

                p("g", dx2nT)
                ctrs["st"].bump(16)

            with ExitStack() as P9S:
                s_wbl = sb("s_wbl", [128, 32 * 512], BF16, P9S)
                s_wbh = sb("s_wbh", [128, 32 * 512], BF16, P9S)
                s_ylev = sb("s_ylev", [128, 512], F32, P9S)
                s_t1 = sb("s_t1", [128, 512], F32, P9S)
                s_t2 = sb("s_t2", [128, 512], F32, P9S)
                s_t3 = sb("s_t3", [128, 512], F32, P9S)
                s_t4 = sb("s_t4", [128, 512], F32, P9S)
                s_x2c = sb("s_x2c", [128, 512], F32, P9S)
                s_bb = sb("s_bb", [128, 512], F32, P9S)
                s_ost = sb("s_ost", [128, 512], F32, P9S)
                p_y = [ps(f"p_y{i}", [128, 1024], F32, P9S) for i in range(2)]
                p_bb = ps("p_bb", [128, 512], F32, P9S)

                ff_rank_done = {}
                fe_war = {}
                ye_war = [0]
                ye_war2 = [0]
                ge_war = [0]
                bb_war = [0]
                x2c_war = [0]
                ost_war = [0]

                for r in range(NCORES):
                    def ldw(g, r=r, w=ag3_t, wff=ff_rank_done.get(r - 1, 0),
                            wev=p7_ev):
                        g.wait_ge(sems["ag"], w)
                        g.wait_ge(sems["ev"], wev)
                        g.wait_ge(sems["ff"], wff)
                        srcl = bass.AP(agw, (D * r) * (2 * OROWS),
                                       [[2 * OROWS, 128], [128 * 2 * OROWS, 32], [1, 512]])
                        srch = bass.AP(agw, (D * r) * (2 * OROWS) + OROWS,
                                       [[2 * OROWS, 128], [128 * 2 * OROWS, 32], [1, 512]])
                        dstl = bass.AP(s_wbl, 0, [[32 * 512, 128], [512, 32], [1, 512]])
                        dsth = bass.AP(s_wbh, 0, [[32 * 512, 128], [512, 32], [1, 512]])
                        g.dma_start(out=dstl, in_=srcl).then_inc(sems["wo"], 16)
                        g.dma_start(out=dsth, in_=srch).then_inc(sems["wo"], 16)

                    p("g", ldw)
                    wt9 = ctrs["wo"].bump(32)

                    def tb9(t, r=r, wld=ld_setup, w=bb_war[0], wev=p7_ev):
                        t.wait_ge(sems["ld"], wld)
                        t.wait_ge(sems["ev"], wev)
                        t.wait_ge(sems["fe"], w)
                        t.matmul(p_bb[:], s_ones_row[:],
                                 s_bias[0:1, 512 * r:512 * (r + 1)],
                                 start=True, stop=True).then_inc(sems["bc"], 1)

                    p("t", tb9)
                    bct9 = ctrs["bc"].bump(1)

                    def sb9(s, bct9=bct9):
                        s.wait_ge(sems["bc"], bct9)
                        s.copy(s_bb[:], p_bb[:]).then_inc(sems["rec"], 1)

                    p("s", sb9)
                    sb9t = ctrs["rec"].bump(1)

                    for TCi, TC in enumerate(range(8)):
                        py = p_y[TC % 2]
                        g9 = 8 * r + TCi

                        def mf(t, r=r, TC=TC, py=py, wt9=wt9, p7_ev=p7_ev,
                               wfe=fe_war.get(g9 - 2, 0)):
                            t.wait_ge(sems["wo"], wt9)
                            if r == 0 and TC == 0:
                                t.wait_ge(sems["ev"], p7_ev)
                            t.wait_ge(sems["fe"], wfe)
                            for it in range(32):
                                xt = s_x2nT[:, TOK * it + 128 * TC:TOK * it + 128 * (TC + 1)]
                                t.matmul(py[:, 0:512], xt,
                                         s_wbl[:, 512 * it:512 * (it + 1)],
                                         start=(it == 0), stop=(it == 31))
                                ins = t.matmul(py[:, 512:1024], xt,
                                               s_wbh[:, 512 * it:512 * (it + 1)],
                                               start=(it == 0), stop=(it == 31))
                            ins.then_inc(sems["ff"], 1)

                        p("t", mf)
                        fft = ctrs["ff"].bump(1)
                        if TC == 7:
                            ff_rank_done[r] = fft

                        def ldx2c(y, r=r, TC=TC, w=p6_x2s, wfe=x2c_war[0]):
                            y.wait_ge(sems["x2s"], w)
                            y.wait_ge(sems["fe"], wfe)
                            y.dma_start(out=s_x2c[:],
                                        in_=bass.AP(x2sp, 128 * TC * D + 512 * r,
                                                    [[D, 128], [1, 512]])).then_inc(sems["x2"], 16)

                        p("y", ldx2c)
                        x2ct = ctrs["x2"].bump(16)

                        def sy9b(s, fft=fft, py=py, w=ye_war2[0], TC=TC, wv=p7_v1):
                            s.wait_ge(sems["ff"], fft)
                            s.wait_ge(sems["v1"], w)
                            s.wait_ge(sems["v1"], wv)
                            s.copy(s_ylev[:], py[:, 0:512])
                            s.mul(s_t1[:], s_ylev[:], s_m1[:, TC:TC + 1])
                            s.mul(s_t2[:], py[:, 512:1024],
                                  s_m[:, TC:TC + 1]).then_inc(sems["sc"], 1)

                        p("s", sy9b)
                        syt = ctrs["sc"].bump(1)

                        def ve9(v, syt=syt, x2ct=x2ct, sb9t=sb9t, wge=ge_war[0]):
                            v.wait_ge(sems["sc"], syt)
                            v.wait_ge(sems["x2"], x2ct)
                            v.wait_ge(sems["rec"], sb9t)
                            v.wait_ge(sems["fe"], wge)
                            v.tensor_add(s_t3[:], s_t1[:], s_t2[:])
                            v.tensor_add(s_t4[:], s_bb[:], s_x2c[:]).then_inc(sems["v1"], 1)

                        p("v", ve9)
                        vet9 = ctrs["v1"].bump(1)
                        ye_war2[0] = vet9

                        def ge9(g, vet9=vet9, wost=ost_war[0]):
                            g.wait_ge(sems["v1"], vet9)
                            g.wait_ge(sems["st"], wost)
                            g.tensor_add(s_ost[:], s_t3[:], s_t4[:]).then_inc(sems["fe"], 1)

                        p("g", ge9)
                        fet = ctrs["fe"].bump(1)
                        ge_war[0] = fet
                        fe_war[g9] = fet
                        ye_war[0] = ctrs["sc"].v
                        x2c_war[0] = fet
                        bb_war[0] = fet if TC == 7 else bb_war[0]

                        def sto9(y, r=r, TC=TC, fet=fet):
                            y.wait_ge(sems["fe"], fet)
                            y.dma_start(out=bass.AP(d_out, 128 * TC * D + 512 * r,
                                                    [[D, 128], [1, 512]]),
                                        in_=s_ost[:]).then_inc(sems["st"], 16)

                        p("y", sto9)
                        ost_war[0] = ctrs["st"].bump(16)

        if dbg:
            def dfin(g, w=ctrs["fe"].v):
                g.wait_ge(sems["fe"], w)
                g.dma_start(out=dbg_t["mfin"].ap(), in_=s_m[:]).then_inc(sems["st"], 16)
                g.dma_start(out=dbg_t["m1fin"].ap(), in_=s_m1[:]).then_inc(sems["st"], 16)
                g.dma_start(out=dbg_t["xnt7"].ap(),
                            in_=x2nT_ref[:, 0:1024]).then_inc(sems["st"], 16)
                g.dma_start(out=dbg_t["x2sp7b"].ap(),
                            in_=x2sp[896:1024, :]).then_inc(sems["st"], 16)
                g.dma_start(out=dbg_t["xntall"].ap(),
                            in_=x2nT_ref[:]).then_inc(sems["st"], 16)
                g.dma_start(out=dbg_t["agw0"].ap(),
                            in_=agw[0:128, 0:512]).then_inc(sems["st"], 16)
                g.dma_start(out=dbg_t["x2all"].ap(),
                            in_=x2sp.ap()).then_inc(sems["st"], 16)
                g.dma_start(out=dbg_t["stB"].ap(),
                            in_=stat2b[1][:]).then_inc(sems["st"], 16)
                g.dma_start(out=dbg_t["ago9"].ap(),
                            in_=ago[0:4096, 4608:5120]).then_inc(sems["st"], 16)

            p("g", dfin)
            ctrs["st"].bump(144)

        st_all = ctrs["st"].v

        def fin(y, w=st_all):
            y.wait_ge(sems["st"], w)

        p("y", fin)

        with nc.Block() as block:
            @block.gpsimd
            def _(g):
                for fn in ops["g"]:
                    fn(g)

            @block.tensor
            def _(t):
                for fn in ops["t"]:
                    fn(t)

            @block.vector
            def _(v):
                for fn in ops["v"]:
                    fn(v)

            @block.scalar
            def _(s):
                for fn in ops["s"]:
                    fn(s)

            @block.sync
            def _(y):
                for fn in ops["y"]:
                    fn(y)

    return nc


def kernel(L0_attn_hidden_states, L0_attn_Wq, L0_attn_Wk, L0_attn_Wv, L0_attn_Wo,
           L0_ffn_qweight, L0_ffn_lut4, L0_ffn_lut8, L0_ffn_bias, _dbg=False):
    x = np.ascontiguousarray(np.asarray(L0_attn_hidden_states)[0], dtype=np.float32)
    Wq = np.asarray(L0_attn_Wq, dtype=np.float32)
    Wk = np.asarray(L0_attn_Wk, dtype=np.float32)
    Wv = np.asarray(L0_attn_Wv, dtype=np.float32)
    Wo = np.ascontiguousarray(np.asarray(L0_attn_Wo, dtype=np.float32))
    qw = np.ascontiguousarray(np.asarray(L0_ffn_qweight, dtype=np.int32))
    lut4 = np.asarray(L0_ffn_lut4, dtype=np.float32)
    lut8 = np.asarray(L0_ffn_lut8, dtype=np.float32)
    bias = np.ascontiguousarray(np.asarray(L0_ffn_bias, dtype=np.float32).reshape(1, D))
    cosT, sinTs, _, _ = _host_consts()

    nc = build(dbg=_dbg)
    in_maps = []
    for c in range(NCORES):
        in_maps.append({
            "x_mine": np.ascontiguousarray(x[TOK * c:TOK * (c + 1)]),
            "wq_m": np.ascontiguousarray(Wq[:, 512 * c:512 * (c + 1)]),
            "wk_m": np.ascontiguousarray(Wk[:, 128 * c:128 * (c + 1)]),
            "wv_m": np.ascontiguousarray(Wv[:, 128 * c:128 * (c + 1)]),
            "wo": Wo,
            "qw_m": np.ascontiguousarray(qw[:, OROWS * c:OROWS * (c + 1), :]),
            "lut4_m": np.ascontiguousarray(lut4[OROWS * c:OROWS * (c + 1)]),
            "lut8_m": np.ascontiguousarray(lut8[OROWS * c:OROWS * (c + 1)]),
            "bias": bias,
            "cosT": cosT,
            "sinTs": sinTs,
        })
    res = run_bass_kernel_spmd(nc, in_maps, core_ids=list(range(NCORES)))
    out = np.concatenate([res.results[c]["out_m"] for c in range(NCORES)], axis=0)
    if _dbg:
        return out[None], res
    return out[None]


# version 04
# speedup vs baseline: 1.1879x; 1.1879x over previous
import numpy as np
import ml_dtypes
import concourse.bass as bass
import concourse.mybir as mybir
from concourse.bass_utils import run_bass_kernel_spmd
from contextlib import ExitStack

F32 = mybir.dt.float32
BF16 = mybir.dt.bfloat16
I32 = mybir.dt.int32
AF = mybir.ActivationFunctionType
ALU = mybir.AluOpType
BF16NP = ml_dtypes.bfloat16

NCORES = 8
S = 8192
D = 4096
HD = 128
NQH = 4
TOK = 1024
QC = 512
EPS = 1e-5
SCALE = 1.0 / float(np.sqrt(128.0))
OROWS = 512
RG = [list(range(NCORES))]


def _host_consts():
    inv_freq = 1.0 / (10000.0 ** (np.arange(0, HD, 2, dtype=np.float32) / HD))
    t = np.arange(S, dtype=np.float32)
    freqs = np.outer(t, inv_freq)
    emb = np.concatenate([freqs, freqs], axis=-1)
    cosT = np.ascontiguousarray(np.cos(emb).astype(np.float32).T)
    sinT = np.ascontiguousarray(np.sin(emb).astype(np.float32).T)
    sinTs = sinT.copy()
    sinTs[:64] = -sinTs[:64]
    m01 = np.zeros((4, 128, QC), dtype=np.float32)
    kk = np.arange(128)[:, None]
    qq = np.arange(QC)[None, :]
    for i in range(4):
        m01[i] = np.where(128 * i + kk <= qq, 0.0, -2000.0)
    masks = m01.astype(BF16NP)
    ident = np.eye(128, dtype=BF16NP)
    return cosT, sinTs, masks, ident


class Ctr:
    def __init__(self):
        self.v = 0

    def bump(self, n=1):
        self.v += n
        return self.v


def build(dbg=False):
    nc = bass.Bass("TRN2", target_bir_lowering=False, debug=False)
    cosT_np, sinTs_np, masks_np, ident_np = _host_consts()

    d_x = nc.dram_tensor("x_mine", [TOK, D], F32, kind="ExternalInput")
    d_wq = nc.dram_tensor("wq_m", [D, NQH * HD], F32, kind="ExternalInput")
    d_wk = nc.dram_tensor("wk_m", [D, HD], F32, kind="ExternalInput")
    d_wv = nc.dram_tensor("wv_m", [D, HD], F32, kind="ExternalInput")
    d_wo = nc.dram_tensor("wo", [D, D], F32, kind="ExternalInput")
    d_qw = nc.dram_tensor("qw_m", [8, OROWS, 128], I32, kind="ExternalInput")
    d_lut4 = nc.dram_tensor("lut4_m", [OROWS, 16], F32, kind="ExternalInput")
    d_lut8 = nc.dram_tensor("lut8_m", [OROWS, 256], F32, kind="ExternalInput")
    d_bias = nc.dram_tensor("bias", [1, D], F32, kind="ExternalInput")
    d_cos = nc.dram_tensor("cosT", [HD, S], F32, kind="ExternalInput")
    d_sin = nc.dram_tensor("sinTs", [HD, S], F32, kind="ExternalInput")
    d_out = nc.dram_tensor("out_m", [TOK, D], F32, kind="ExternalOutput")

    d_masks = nc.inline_tensor(masks_np.reshape(4 * 128, QC), name="masks")
    d_ident = nc.inline_tensor(ident_np, name="ident")

    agx_in = nc.dram_tensor("agx_in", [D, TOK], BF16)
    agx = nc.dram_tensor("agx", [NCORES * D, TOK], BF16)
    qspill = nc.dram_tensor("qspill", [NQH * HD, S], BF16)
    ago_in = nc.dram_tensor("ago_in", [NQH * HD, S], BF16)
    ago = nc.dram_tensor("ago", [NCORES * NQH * HD, S], BF16)
    agom = nc.dram_tensor("agom", [NCORES * NQH * HD, TOK], BF16)
    agw_in = nc.dram_tensor("agw_in", [D, 2 * OROWS], BF16)
    agw = nc.dram_tensor("agw", [NCORES * D, 2 * OROWS], BF16)
    x2sp = nc.dram_tensor("x2sp", [TOK, D], F32)
    wsp = nc.dram_tensor("wsp", [2 * OROWS, D], BF16)

    dbg_t = {}
    if dbg:
        for nm, shp in [("z1", [128, D]), ("kT", [128, 512]), ("expS", [128, 1024]),
                        ("x2", [128, D]), ("wl", [128, 512]), ("idx8", [128, 512]),
                        ("mcol", [128, 8]), ("oT", [128, 512]), ("den", [1, 512]),
                        ("scr", [128, 512]), ("rot", [128, 512]), ("cs", [128, 512]),
                        ("sn", [128, 512]), ("kTlate", [128, 512]),
                        ("ag6", [128, 1024]), ("o2", [128, 512]),
                        ("axin", [128, 512]), ("agx0", [128, 512]),
                        ("x2sp", [128, D]), ("x2nT", [128, 1024]),
                        ("x0", [128, D]), ("st0", [128, 20]),
                        ("mfin", [128, 8]), ("m1fin", [128, 8]),
                        ("xnt7", [128, 1024]), ("x2sp7a", [128, D]),
                        ("x2sp7b", [128, D]), ("xntall", [128, 32 * TOK]),
                        ("agw0", [128, 512]), ("x2all", [1024, D]),
                        ("stB", [128, 20]), ("ago9", [4096, 512]),
                        ("oT9", [128, 1024]), ("den9", [2, 512]),
                        ("es9", [128, 1024])]:
            dbg_t[nm] = nc.dram_tensor("dbg_" + nm, shp, F32, kind="ExternalOutput")

    ops = {k: [] for k in "gtvsy"}

    def p(eng, fn):
        ops[eng].append(fn)

    sems = {}
    ctrs = {}
    for nm in ["ld", "x1", "v1", "sc", "tp", "ag", "xn", "xnc", "ev", "rot",
               "g1", "q4", "exp", "msk", "av", "nrm", "rec", "bc", "ao", "wo",
               "mm6", "ep6", "x2", "u", "sw", "wst", "ff", "fe", "st", "cs",
               "vt", "qs", "x2s", "x7",
               "x1b", "sx", "xn1", "xn2", "xn3", "csb", "rotb", "qp",
               "q4b", "x7b", "wt6", "wo2"]:
        sems[nm] = nc.alloc_semaphore(nm)
        ctrs[nm] = Ctr()

    SB = ExitStack()

    def sb(name, shape, dt, stack):
        return stack.enter_context(nc.sbuf_tensor(name, shape, dt))

    def ps(name, shape, dt, stack):
        return stack.enter_context(nc.psum_tensor(name, shape, dt))

    with SB:
        s_ident = sb("s_ident", [128, 128], BF16, SB)
        s_ones_col = sb("s_ones_col", [128, 1], BF16, SB)
        s_ones_row = sb("s_ones_row", [1, 128], F32, SB)
        s_bias = sb("s_bias", [1, D], F32, SB)
        s_m = sb("s_m", [128, 8], F32, SB)
        s_m1 = sb("s_m1", [128, 8], F32, SB)
        stat2b = [sb("statA", [128, 20], F32, SB),
                  sb("statB", [128, 20], F32, SB)]
        s_recip = [sb("s_recip0", [1, 512], F32, SB),
                   sb("s_recip1", [1, 512], F32, SB)]

        def setup_y(y):
            y.dma_start(out=s_ident[:], in_=d_ident.ap()).then_inc(sems["ld"], 16)
            y.dma_start(out=s_bias[:], in_=d_bias.ap()).then_inc(sems["ld"], 16)

        p("y", setup_y)
        ld_setup = ctrs["ld"].bump(32)

        def setup_g(g):
            g.memset(s_ones_col[:], 1.0)
            g.memset(s_ones_row[:], 1.0).then_inc(sems["g1"], 1)

        p("g", setup_g)
        g1_setup = ctrs["g1"].bump(1)

        def emit_lnstats(stat, buf, ldwait, wdbg, sqscr, p7):
            wsem, wval = ldwait

            def v0(v, wval=wval):
                v.wait_ge(sems[wsem], wval)
                v.reduce_sum(stat[:, 0:1], buf[:],
                             axis=mybir.AxisListType.X).then_inc(sems["v1"], 1)

            p("v", v0)
            tv0 = ctrs["v1"].bump(1)

            def s0(s, wval=wval, wdbg=wdbg, tv0=tv0):
                s.wait_ge(sems[wsem], wval)
                if wdbg:
                    s.wait_ge(sems["st"], wdbg)
                s.activation(sqscr[:], buf[:], AF.Square, accum_out=stat[:, 2:3])
                s.wait_ge(sems["v1"], tv0)
                s.activation(stat[:, 1:2], stat[:, 0:1], AF.Copy,
                             scale=1.0 / D).then_inc(sems["sc"], 1)

            p("s", s0)
            ts0 = ctrs["sc"].bump(1)

            def v1f(v, ts0=ts0):
                v.wait_ge(sems["sc"], ts0)
                v.tensor_scalar(stat[:, 3:4], stat[:, 2:3], 1.0 / D, None, ALU.mult)
                v.scalar_tensor_tensor(stat[:, 4:5], stat[:, 1:2], -1.0,
                                       stat[:, 1:2], ALU.mult,
                                       ALU.mult).then_inc(sems["v1"], 1)

            p("v", v1f)
            tv1 = ctrs["v1"].bump(1)

            def s1(s, tv1=tv1):
                s.wait_ge(sems["v1"], tv1)
                s.add(stat[:, 6:7], stat[:, 3:4], stat[:, 4:5]).then_inc(sems["sc"], 1)

            p("s", s1)
            ts1 = ctrs["sc"].bump(1)

            def v2(v, ts1=ts1):
                v.wait_ge(sems["sc"], ts1)
                v.tensor_scalar(stat[:, 7:8], stat[:, 6:7], EPS, None,
                                ALU.add).then_inc(sems["v1"], 1)

            p("v", v2)
            tv2 = ctrs["v1"].bump(1)

            def s2(s, tv2=tv2):
                s.wait_ge(sems["v1"], tv2)
                s.activation(stat[:, 8:9], stat[:, 7:8], AF.Ln).then_inc(sems["sc"], 1)

            p("s", s2)
            ts2 = ctrs["sc"].bump(1)

            def v3(v, ts2=ts2):
                v.wait_ge(sems["sc"], ts2)
                v.tensor_copy(stat[:, 9:10], stat[:, 8:9]).then_inc(sems["v1"], 1)

            p("v", v3)
            tv3 = ctrs["v1"].bump(1)

            def s3(s, tv3=tv3):
                s.wait_ge(sems["v1"], tv3)
                s.activation(stat[:, 10:11], stat[:, 9:10], AF.Exp,
                             scale=-0.5).then_inc(sems["sc"], 1)

            p("s", s3)
            ts3 = ctrs["sc"].bump(1)

            def v4(v, ts3=ts3):
                v.wait_ge(sems["sc"], ts3)
                v.tensor_mul(stat[:, 11:12], stat[:, 10:11], stat[:, 10:11])
                v.tensor_copy(stat[:, 5:6], stat[:, 10:11]).then_inc(sems["v1"], 1)

            p("v", v4)
            tv4 = ctrs["v1"].bump(1)

            def s4(s, tv4=tv4):
                s.wait_ge(sems["v1"], tv4)
                s.mul(stat[:, 12:13], stat[:, 11:12], stat[:, 7:8]).then_inc(sems["sc"], 1)

            p("s", s4)
            ts4 = ctrs["sc"].bump(1)

            def v5(v, ts4=ts4):
                v.wait_ge(sems["sc"], ts4)
                v.tensor_scalar(stat[:, 14:15], stat[:, 12:13], -0.5, 1.5,
                                ALU.mult, ALU.add).then_inc(sems["v1"], 1)

            p("v", v5)
            tv5 = ctrs["v1"].bump(1)

            def s5(s, tv5=tv5):
                s.wait_ge(sems["v1"], tv5)
                s.mul(stat[:, 13:14], stat[:, 14:15],
                      stat[:, 5:6]).then_inc(sems["sc"], 1)

            p("s", s5)
            ts5 = ctrs["sc"].bump(1)
            if not p7:
                return ts5

            def v6(v, ts5=ts5):
                v.wait_ge(sems["sc"], ts5)
                v.tensor_mul(stat[:, 15:16], stat[:, 13:14], stat[:, 13:14])
                v.tensor_scalar(stat[:, 17:18], stat[:, 6:7], float(D), None,
                                ALU.mult).then_inc(sems["v1"], 1)

            p("v", v6)
            tv6 = ctrs["v1"].bump(1)

            def s6(s, tv6=tv6):
                s.wait_ge(sems["v1"], tv6)
                s.mul(stat[:, 16:17], stat[:, 15:16],
                      stat[:, 17:18]).then_inc(sems["sc"], 1)

            p("s", s6)
            return ctrs["sc"].bump(1)

        with ExitStack() as DQ:
            HW = 2048
            s_lut4 = sb("s_lut4", [128, 4 * 16], F32, DQ)
            s_lut8 = sb("s_lut8", [128, 4 * 256], F32, DQ)
            s_qw = sb("s_qw", [128, 8 * 64], I32, DQ)
            s_ctrI = [sb("s_ctrI0", [128, HW], I32, DQ),
                      sb("s_ctrI1", [128, HW], I32, DQ)]
            s_idx8I = sb("s_idx8I", [128, HW], I32, DQ)
            s_ctr = sb("s_ctr", [128, HW], BF16, DQ)
            s_idx8 = sb("s_idx8", [128, HW], BF16, DQ)
            s_idx4 = sb("s_idx4", [128, HW], BF16, DQ)
            s_wl = sb("s_wl", [128, HW], BF16, DQ)
            s_wh = sb("s_wh", [128, HW], BF16, DQ)

            def lut_loads(y):
                for j in range(4):
                    y.dma_start(out=s_lut4[:, 16 * j:16 * (j + 1)],
                                in_=d_lut4[128 * j:128 * (j + 1), :]).then_inc(sems["ld"], 16)
                    y.dma_start(out=s_lut8[:, 256 * j:256 * (j + 1)],
                                in_=d_lut8[128 * j:128 * (j + 1), :]).then_inc(sems["ld"], 16)

            p("y", lut_loads)
            ld_luts = ctrs["ld"].bump(8 * 16)

            def emit_sweep():
                last_store = [0]
                for j in range(4):
                    for hf in range(2):
                        uprev = ctrs["u"].v

                        def ldqw(y, j=j, hf=hf, uprev=uprev):
                            y.wait_ge(sems["u"], uprev)
                            src = bass.AP(d_qw, 128 * j * 128 + 64 * hf,
                                          [[128, 128], [OROWS * 128, 8], [1, 64]])
                            dst = bass.AP(s_qw, 0, [[8 * 64, 128], [64, 8], [1, 64]])
                            y.dma_start(out=dst, in_=src).then_inc(sems["qs"], 16)

                        yield ("y", ldqw)
                        qs_t = ctrs["qs"].bump(16)

                        gor = {}
                        for pl in range(8):
                            def vext(v, pl=pl, qs_t=qs_t, ld_luts=ld_luts,
                                     wprev=last_store[0], wg=gor.get(pl - 2, 0)):
                                if pl == 0:
                                    v.wait_ge(sems["qs"], qs_t)
                                    v.wait_ge(sems["ld"], ld_luts)
                                    v.wait_ge(sems["wst"], wprev)
                                v.wait_ge(sems["g1"], wg)
                                tgt = s_idx8I if pl == 0 else s_ctrI[pl % 2]
                                ins = None
                                for b in range(32):
                                    sh = b - pl
                                    op0 = (ALU.logical_shift_right if sh >= 0
                                           else ALU.logical_shift_left)
                                    dst = bass.AP(tgt, b, [[HW, 128], [32, 64]])
                                    ins = v.tensor_scalar(dst,
                                                          s_qw[:, 64 * pl:64 * (pl + 1)],
                                                          abs(sh), 1 << pl, op0,
                                                          ALU.bitwise_and)
                                ins.then_inc(sems["msk"], 1)

                            yield ("v", vext)
                            vet = ctrs["msk"].bump(1)

                            if pl >= 1:
                                def gor_fn(g, pl=pl, vet=vet):
                                    g.wait_ge(sems["msk"], vet)
                                    g.tensor_tensor(s_idx8I[:], s_idx8I[:],
                                                    s_ctrI[pl % 2][:],
                                                    ALU.add).then_inc(sems["g1"], 1)

                                yield ("g", gor_fn)
                                gor[pl] = ctrs["g1"].bump(1)

                        def vun(v, wg=gor[7]):
                            v.wait_ge(sems["g1"], wg)
                            v.tensor_copy(s_idx8[:], s_idx8I[:])
                            v.tensor_scalar(s_ctrI[0][:], s_idx8I[:], 15, None,
                                            ALU.bitwise_and)
                            v.tensor_copy(s_idx4[:], s_ctrI[0][:])
                            v.memset(s_wl[:], 0.0)
                            v.memset(s_wh[:], 0.0).then_inc(sems["u"], 1)

                        yield ("v", vun)
                        ut = ctrs["u"].bump(1)
                        if dbg and j == 0 and hf == 0:
                            def di(g, ut=ut):
                                g.wait_ge(sems["u"], ut)
                                g.dma_start(out=dbg_t["idx8"].ap(),
                                            in_=s_idx8[:, 0:512]).then_inc(sems["st"], 16)

                            yield ("g", di)
                            ctrs["st"].bump(16)

                        for c in range(16):
                            def vc4(v, c=c, j=j):
                                v.tensor_scalar(s_ctr[:], s_idx4[:], float(c),
                                                s_lut4[:, 16 * j + c:16 * j + c + 1],
                                                ALU.is_equal, ALU.mult)
                                v.tensor_add(s_wl[:], s_wl[:], s_ctr[:])

                            yield ("v", vc4)
                        for c in range(256):
                            def vc8(v, c=c, j=j):
                                v.tensor_scalar(s_ctr[:], s_idx8[:], float(c),
                                                s_lut8[:, 256 * j + c:256 * j + c + 1],
                                                ALU.is_equal, ALU.mult)
                                v.tensor_add(s_wh[:], s_wh[:], s_ctr[:])

                            yield ("v", vc8)

                        def vdone(v):
                            v.tensor_scalar(s_wh[:], s_wh[:], 0.0, None,
                                            ALU.add).then_inc(sems["sw"], 1)

                        yield ("v", vdone)
                        swt = ctrs["sw"].bump(1)
                        if dbg and j == 0 and hf == 0:
                            def dw(g, swt=swt):
                                g.wait_ge(sems["sw"], swt)
                                g.dma_start(out=dbg_t["wl"].ap(),
                                            in_=s_wl[:, 0:512]).then_inc(sems["st"], 16)

                            yield ("g", dw)
                            ctrs["st"].bump(16)

                        def stw(y, j=j, hf=hf, swt=swt):
                            y.wait_ge(sems["sw"], swt)
                            y.dma_start(out=wsp[128 * j:128 * (j + 1),
                                                HW * hf:HW * (hf + 1)],
                                        in_=s_wl[:]).then_inc(sems["wst"], 16)
                            y.dma_start(out=wsp[OROWS + 128 * j:OROWS + 128 * (j + 1),
                                                HW * hf:HW * (hf + 1)],
                                        in_=s_wh[:]).then_inc(sems["wst"], 16)

                        yield ("y", stw)
                        last_store[0] = ctrs["wst"].bump(32)

            sweep_gen = emit_sweep()
            sweep_live = [True]

            def drain_sweep(n):
                while sweep_live[0] and n > 0:
                    try:
                        eng, fn = next(sweep_gen)
                        p(eng, fn)
                        n -= 1
                    except StopIteration:
                        sweep_live[0] = False

            with ExitStack() as KV:
                s_kT = sb("s_kT", [128, S], BF16, KV)
                s_V = sb("s_V", [128, S], BF16, KV)

                with ExitStack() as P1S:
                    s_x = [sb(f"s_x{i}", [128, D], F32, P1S) for i in range(2)]
                    s_z = sb("s_z", [128, D], BF16, P1S)
                    s_sqscr = sb("s_sqscr", [128, D], F32, P1S)
                    s_xnstage = sb("s_xnstage", [128, D], BF16, P1S)
                    p_tp = [ps(f"p_tp{i}", [128, 1024], BF16, P1S) for i in range(2)]

                    apply_t = {}
                    dbg_gate = [0]
                    buf0_ref = s_x[0]
                    sq_t = {}
                    ev_t = {}
                    ttr_t = {}
                    stx_t = {}

                    for i in range(8):
                        buf = s_x[i % 2]
                        xsem = "x1" if i % 2 == 0 else "x1b"

                        def ldx(y, i=i, buf=buf, xsem=xsem,
                                wv=apply_t.get(i - 2, 0), ws=sq_t.get(i - 2, 0)):
                            y.wait_ge(sems["v1"], wv)
                            y.wait_ge(sems["sc"], ws)
                            y.dma_start(out=buf[:],
                                        in_=d_x[128 * i:128 * (i + 1), :]).then_inc(sems[xsem], 16)

                        p("y", ldx)
                        x1t = ctrs[xsem].bump(16)

                        stat = stat2b[i % 2]
                        sc_b = emit_lnstats(stat, buf, (xsem, x1t),
                                            (dbg_gate[0] if (dbg and i == 1) else 0),
                                            s_sqscr, p7=False)
                        sq_t[i] = sc_b

                        def vapply(v, buf=buf, stat=stat, sc_b=sc_b,
                                   wtp=ttr_t.get(i - 1, 0),
                                   wdbg=(dbg_gate[0] if (dbg and i == 1) else 0)):
                            v.wait_ge(sems["sc"], sc_b)
                            v.wait_ge(sems["tp"], wtp)
                            if wdbg:
                                v.wait_ge(sems["st"], wdbg)
                            v.tensor_scalar(s_z[:], buf[:], stat[:, 1:2], stat[:, 13:14],
                                            ALU.subtract, ALU.mult).then_inc(sems["v1"], 1)

                        p("v", vapply)
                        apply_t[i] = ctrs["v1"].bump(1)

                        if dbg and i == 0:
                            def dz(g, w=apply_t[i], stat=stat):
                                g.wait_ge(sems["v1"], w)
                                g.dma_start(out=dbg_t["z1"].ap(),
                                            in_=s_z[:]).then_inc(sems["st"], 16)
                                g.dma_start(out=dbg_t["x0"].ap(),
                                            in_=buf0_ref[:]).then_inc(sems["st"], 16)
                                g.dma_start(out=dbg_t["st0"].ap(),
                                            in_=stat[:]).then_inc(sems["st"], 16)

                            p("g", dz)
                            dz_t = ctrs["st"].bump(48)
                            dbg_gate[0] = dz_t

                        for gidx in range(4):
                            pb = p_tp[gidx % 2]
                            gi = 4 * i + gidx

                            def ttr(t, gidx=gidx, pb=pb, wv=apply_t[i], i=i,
                                    wev=ev_t.get((i, gidx - 2) if gidx >= 2 else (i - 1, gidx + 2), 0)):
                                if i == 0 and gidx == 0:
                                    t.wait_ge(sems["ld"], 160)
                                t.wait_ge(sems["v1"], wv)
                                t.wait_ge(sems["ev"], wev)
                                ins = None
                                for f in range(8):
                                    ft = 8 * gidx + f
                                    ins = t.transpose(pb[:, 128 * f:128 * (f + 1)],
                                                      s_z[:, 128 * ft:128 * (ft + 1)], s_ident[:])
                                ins.then_inc(sems["tp"], 1)

                            p("t", ttr)
                            tpt = ctrs["tp"].bump(1)
                            ttr_t[i] = tpt

                            def sev(s, gidx=gidx, pb=pb, tpt=tpt, wst=stx_t.get(i - 1, 0)):
                                s.wait_ge(sems["tp"], tpt)
                                if gidx == 0:
                                    s.wait_ge(sems["sx"], wst)
                                s.copy(s_xnstage[:, 1024 * gidx:1024 * (gidx + 1)],
                                       pb[:]).then_inc(sems["ev"], 1)

                            p("s", sev)
                            ev_t[(i, gidx)] = ctrs["ev"].bump(1)

                        def stx(y, i=i, w=ev_t[(i, 3)]):
                            y.wait_ge(sems["ev"], w)
                            src = bass.AP(s_xnstage, 0, [[D, 128], [128, 32], [1, 128]])
                            dst = bass.AP(agx_in, 128 * i,
                                          [[TOK, 128], [128 * TOK, 32], [1, 128]])
                            y.dma_start(out=dst, in_=src).then_inc(sems["sx"], 16)

                        p("y", stx)
                        stx_t[i] = ctrs["sx"].bump(16)

                    emit_lnstats(stat2b[0], s_x[1], ("x1b", ctrs["x1b"].v), 0,
                                 s_sqscr, p7=False)

                    ag1_after = (ctrs["x1"].v, ctrs["x1b"].v, ctrs["sx"].v)

                    def ag1(g, w=ag1_after):
                        g.wait_ge(sems["x1"], w[0])
                        g.wait_ge(sems["x1b"], w[1])
                        g.wait_ge(sems["sx"], w[2])
                        g.collective_compute("AllGather", ALU.bypass, replica_groups=RG,
                                             ins=[agx_in.ap().opt()],
                                             outs=[agx.ap().opt()]).then_inc(sems["ag"], 1)

                    p("g", ag1)
                    ag1_t = ctrs["ag"].bump(1)

                    if dbg:
                        def daxin(g, w=ag1_t):
                            g.wait_ge(sems["ag"], w)
                            g.dma_start(out=dbg_t["axin"].ap(),
                                        in_=agx_in[0:128, 0:512]).then_inc(sems["st"], 16)
                            g.dma_start(out=dbg_t["agx0"].ap(),
                                        in_=agx[0:128, 0:512]).then_inc(sems["st"], 16)

                        p("g", daxin)
                        ctrs["st"].bump(32)

                with ExitStack() as P3S:
                    s_wqkv = sb("s_wqkv", [128, 32 * 6 * 128], BF16, P3S)
                    s_xnr = [sb(f"s_xnr{i}", [128, 512], BF16, P3S) for i in range(4)]
                    s_cs = [sb(f"s_cs{i}", [128, 512], F32, P3S) for i in range(2)]
                    s_sn = [sb(f"s_sn{i}", [128, 512], F32, P3S) for i in range(2)]
                    s_scr = [sb(f"s_scr{i}", [128, 512], F32, P3S) for i in range(2)]
                    s_rot = [sb(f"s_rot{i}", [128, 512], F32, P3S) for i in range(2)]
                    s_qstage = sb("s_qstage", [128, 512], BF16, P3S)
                    s_vtb = sb("s_vtb", [128, 512], BF16, P3S)
                    p_qkv = [ps(f"p_qkv{i}", [128, 512], F32, P3S) for i in range(6)]
                    p_vt = ps("p_vt", [128, 512], BF16, P3S)

                    def wload(g, w=g1_setup, wp1=ag1_after):
                        g.wait_ge(sems["g1"], w)
                        g.wait_ge(sems["x1"], wp1[0])
                        g.wait_ge(sems["x1b"], wp1[1])
                        g.wait_ge(sems["sx"], wp1[2])
                        for ft in range(32):
                            base = ft * 6 * 128
                            g.dma_start(out=s_wqkv[:, base:base + 512],
                                        in_=d_wq[128 * ft:128 * (ft + 1), :]).then_inc(sems["ld"], 16)
                            g.dma_start(out=s_wqkv[:, base + 512:base + 640],
                                        in_=d_wk[128 * ft:128 * (ft + 1), :]).then_inc(sems["ld"], 16)
                            g.dma_start(out=s_wqkv[:, base + 640:base + 768],
                                        in_=d_wv[128 * ft:128 * (ft + 1), :]).then_inc(sems["ld"], 16)

                    p("g", wload)
                    wld_t = ctrs["ld"].bump(96 * 16)

                    scr_war = {}
                    qst_war = [0]
                    cs_war = {}
                    vt_war = [0]
                    evac3_t = {}

                    for tc in range(16):
                        rnk, ch = tc // 2, tc % 2
                        for ft in range(32):
                            ring = s_xnr[ft % 4]
                            gidx_xn = 32 * tc + ft
                            xnsem = ["xn", "xn1", "xn2", "xn3"][ft % 4]

                            def ldxn(y, tc=tc, ft=ft, ring=ring, rnk=rnk, ch=ch,
                                     xnsem=xnsem,
                                     agw_=ag1_t, xw=max(0, 32 * tc + ft - 3)):
                                y.wait_ge(sems["ag"], agw_)
                                y.wait_ge(sems["xnc"], xw)
                                y.dma_start(out=ring[:],
                                            in_=agx[D * rnk + 128 * ft:D * rnk + 128 * (ft + 1),
                                                    512 * ch:512 * (ch + 1)]).then_inc(sems[xnsem], 16)

                            p("y", ldxn)
                            xnt = ctrs[xnsem].bump(16)

                            def mm6(t, tc=tc, ft=ft, ring=ring, xnt=xnt, wld_t=wld_t,
                                    xnsem=xnsem,
                                    wev=evac3_t.get(tc - 1, 0)):
                                t.wait_ge(sems[xnsem], xnt)
                                if ft == 0:
                                    t.wait_ge(sems["ev"], wev)
                                if tc == 0 and ft == 0:
                                    t.wait_ge(sems["ld"], wld_t)
                                base = ft * 6 * 128
                                for ti in range(6):
                                    ins = t.matmul(p_qkv[ti][:],
                                                   s_wqkv[:, base + 128 * ti:base + 128 * (ti + 1)],
                                                   ring[:], start=(ft == 0), stop=(ft == 31))
                                ins.then_inc(sems["xnc"], 1)

                            p("t", mm6)
                            ctrs["xnc"].bump(1)
                        vtt = ctrs["xnc"].v

                        cssem = "cs" if tc % 2 == 0 else "csb"

                        def ldcs(y, tc=tc, w=cs_war.get(tc % 2, 0), wag=ag1_t,
                                 cssem=cssem):
                            y.wait_ge(sems["g1"], w)
                            y.wait_ge(sems["ag"], wag)
                            y.dma_start(out=s_cs[tc % 2][:],
                                        in_=d_cos[:, 512 * tc:512 * (tc + 1)]).then_inc(sems[cssem], 16)
                            y.dma_start(out=s_sn[tc % 2][:],
                                        in_=d_sin[:, 512 * tc:512 * (tc + 1)]).then_inc(sems[cssem], 16)

                        p("y", ldcs)
                        cst = ctrs[cssem].bump(32)

                        for ti in range(5):
                            scr = s_scr[ti % 2]
                            rot = s_rot[ti % 2]

                            def sev2(s, ti=ti, scr=scr, vtt=vtt, w=scr_war.get(ti % 2, 0)):
                                s.wait_ge(sems["xnc"], vtt)
                                s.wait_ge(sems["g1"], w)
                                s.copy(scr[:], p_qkv[ti][:]).then_inc(sems["ev"], 1)

                            p("s", sev2)
                            evt2 = ctrs["ev"].bump(1)

                            rotsem = "rot" if ti % 2 == 0 else "rotb"

                            def rotd(y, scr=scr, rot=rot, evt2=evt2, rotsem=rotsem):
                                y.wait_ge(sems["ev"], evt2)
                                y.dma_start(out=rot[0:64, :], in_=scr[64:128, :]).then_inc(sems[rotsem], 16)
                                y.dma_start(out=rot[64:128, :], in_=scr[0:64, :]).then_inc(sems[rotsem], 16)

                            p("y", rotd)
                            rott = ctrs[rotsem].bump(32)

                            def grope(g, ti=ti, tc=tc, scr=scr, rot=rot, rott=rott,
                                      cst=cst, evt2=evt2, qw=qst_war[0],
                                      rotsem=rotsem, cssem=cssem):
                                g.wait_ge(sems[rotsem], rott)
                                g.wait_ge(sems[cssem], cst)
                                g.wait_ge(sems["ev"], evt2)
                                if ti < 4:
                                    g.wait_ge(sems["qp"], qw)
                                cs_ = s_cs[tc % 2]
                                sn_ = s_sn[tc % 2]
                                g.tensor_mul(scr[:], scr[:], cs_[:])
                                g.tensor_mul(rot[:], rot[:], sn_[:])
                                if ti < 4:
                                    g.tensor_add(s_qstage[:], scr[:], rot[:]).then_inc(sems["g1"], 1)
                                else:
                                    g.tensor_add(s_kT[:, 512 * tc:512 * (tc + 1)],
                                                 scr[:], rot[:]).then_inc(sems["g1"], 1)

                            p("g", grope)
                            g1t = ctrs["g1"].bump(1)
                            scr_war[ti % 2] = g1t
                            cs_war[tc % 2] = g1t

                            if ti < 4:
                                def stq(y, ti=ti, tc=tc, g1t=g1t):
                                    y.wait_ge(sems["g1"], g1t)
                                    y.dma_start(out=qspill[128 * ti:128 * (ti + 1),
                                                           512 * tc:512 * (tc + 1)],
                                                in_=s_qstage[:]).then_inc(sems["qp"], 16)

                                p("y", stq)
                                qst_war[0] = ctrs["qp"].bump(16)
                            if dbg and tc == 0 and ti == 4:
                                def dk(g, g1t=g1t, scr=scr, rot=rot, tc=tc):
                                    g.wait_ge(sems["g1"], g1t)
                                    g.dma_start(out=dbg_t["kT"].ap(),
                                                in_=s_kT[:, 0:512]).then_inc(sems["st"], 16)
                                    g.dma_start(out=dbg_t["scr"].ap(),
                                                in_=scr[:]).then_inc(sems["st"], 16)
                                    g.dma_start(out=dbg_t["rot"].ap(),
                                                in_=rot[:]).then_inc(sems["st"], 16)
                                    g.dma_start(out=dbg_t["cs"].ap(),
                                                in_=s_cs[tc % 2][:]).then_inc(sems["st"], 16)
                                    g.dma_start(out=dbg_t["sn"].ap(),
                                                in_=s_sn[tc % 2][:]).then_inc(sems["st"], 16)

                                p("g", dk)
                                ctrs["st"].bump(16 * 5)

                        def sev3(s, vtt=vtt, w=vt_war[0]):
                            s.wait_ge(sems["xnc"], vtt)
                            s.wait_ge(sems["ev"], w)
                            s.copy(s_vtb[:], p_qkv[5][:]).then_inc(sems["ev"], 1)

                        p("s", sev3)
                        evt3 = ctrs["ev"].bump(1)
                        evac3_t[tc] = evt3

                        def tvt(t, evt3=evt3, wvt=vt_war[0]):
                            t.wait_ge(sems["ev"], evt3)
                            t.wait_ge(sems["ev"], wvt)
                            ins = None
                            for f in range(4):
                                ins = t.transpose(p_vt[:, 128 * f:128 * (f + 1)],
                                                  s_vtb[:, 128 * f:128 * (f + 1)], s_ident[:])
                            ins.then_inc(sems["tp"], 1)

                        p("t", tvt)
                        tpt2 = ctrs["tp"].bump(1)

                        def sev4(s, tpt2=tpt2, tc=tc):
                            s.wait_ge(sems["tp"], tpt2)
                            s.copy(s_V[:, 512 * tc:512 * (tc + 1)],
                                   p_vt[:]).then_inc(sems["ev"], 1)

                        p("s", sev4)
                        vt_war[0] = ctrs["ev"].bump(1)

                        drain_sweep(20)

                    p3_ev = ctrs["ev"].v
                    p3_g1 = ctrs["g1"].v
                    p3_qs = ctrs["qs"].v
                    p3_qp = ctrs["qp"].v

                with ExitStack() as P4S:
                    s_masks = sb("s_masks", [128, 4 * 512], BF16, P4S)
                    s_qt = [sb(f"s_qt{i}", [128, 1024], BF16, P4S) for i in range(2)]
                    s_es = [sb(f"s_es{i}", [128, 1024], BF16, P4S) for i in range(2)]
                    s_ot = sb("s_ot", [128, 1024], BF16, P4S)
                    s_rb = sb("s_rb", [128, 512], F32, P4S)
                    s_dbgk = sb("s_dbgk", [128, 512], BF16, P4S) if dbg else None
                    p_S = ps("p_S", [128, 1024], F32, P4S)
                    p_oT = [ps(f"p_oT{i}", [128, 1024], F32, P4S) for i in range(2)]
                    p_den = ps("p_den", [128, 512], F32, P4S)
                    p_bc = ps("p_bc", [128, 512], F32, P4S)

                    def ldmask(y, wev=p3_ev, wg=p3_g1, wq=p3_qs, wqp=p3_qp):
                        y.wait_ge(sems["ev"], wev)
                        y.wait_ge(sems["g1"], wg)
                        y.wait_ge(sems["qs"], wq)
                        y.wait_ge(sems["qp"], wqp)
                        src = bass.AP(d_masks, 0, [[512, 128], [128 * 512, 4], [1, 512]])
                        dst = bass.AP(s_masks, 0, [[4 * 512, 128], [512, 4], [1, 512]])
                        y.dma_start(out=dst, in_=src).then_inc(sems["ld"], 16)

                    p("y", ldmask)
                    mask_t = ctrs["ld"].bump(16)

                    if dbg:
                        def dkcap(s, w=p3_g1):
                            s.wait_ge(sems["g1"], w)
                            s.copy(s_dbgk[:], s_kT[:, 0:512]).then_inc(sems["sc"], 1)

                        p("s", dkcap)
                        dk_t = ctrs["sc"].bump(1)

                        def dkd(g, w=dk_t):
                            g.wait_ge(sems["sc"], w)
                            g.dma_start(out=dbg_t["kTlate"].ap(),
                                        in_=s_dbgk[:]).then_inc(sems["st"], 16)

                        p("g", dkd)
                        ctrs["st"].bump(16)

                    q_war = {0: 0, 1: 0}
                    es_war = {0: 0, 1: 0}
                    nrm_war = {0: 0, 1: 0}
                    rec_war = [0]
                    bc_war = [0]
                    ot_war = [0]
                    rb_war = [0]

                    for qc in range(16):
                        ntile = 4 * qc + 4
                        for pr in range(2):
                            qb = s_qt[pr]

                            q4sem = "q4" if pr == 0 else "q4b"

                            def ldq(y, qc=qc, pr=pr, qb=qb, p3_qp=p3_qp, w=q_war[pr],
                                    wev=p3_ev, q4sem=q4sem):
                                y.wait_ge(sems["qp"], p3_qp)
                                y.wait_ge(sems["ev"], wev)
                                y.wait_ge(sems["av"], w)
                                for hl in range(2):
                                    h = 2 * pr + hl
                                    y.dma_start(out=qb[:, 512 * hl:512 * (hl + 1)],
                                                in_=qspill[128 * h:128 * (h + 1),
                                                           512 * qc:512 * (qc + 1)]).then_inc(sems[q4sem], 16)

                            p("y", ldq)
                            q4t = ctrs[q4sem].bump(32)

                            pend_av = [None]
                            pend_par = [0]
                            for t_i in range(ntile):
                                es = s_es[t_i % 2]
                                diag = t_i >= 4 * qc
                                di = t_i - 4 * qc

                                def smm(t, t_i=t_i, qb=qb, q4t=q4t, qc=qc, pr=pr,
                                        p3_ev=p3_ev, p3_g1=p3_g1, expw=ctrs["exp"].v,
                                        q4sem=q4sem, diag=diag, di=di, mask_t=mask_t):
                                    if t_i == 0:
                                        t.wait_ge(sems[q4sem], q4t)
                                        if qc == 0 and pr == 0:
                                            t.wait_ge(sems["g1"], p3_g1)
                                            t.wait_ge(sems["ev"], p3_ev)
                                    if diag:
                                        t.wait_ge(sems["ld"], mask_t)
                                    t.wait_ge(sems["exp"], expw)
                                    ins = None
                                    for hl in range(2):
                                        ins = t.matmul(p_S[:, 512 * hl:512 * (hl + 1)],
                                                       s_kT[:, 128 * t_i:128 * (t_i + 1)],
                                                       qb[:, 512 * hl:512 * (hl + 1)],
                                                       start=True, stop=not diag)
                                        if diag:
                                            ins = t.matmul(p_S[:, 512 * hl:512 * (hl + 1)],
                                                           s_ident[:],
                                                           s_masks[:, 512 * di:512 * (di + 1)],
                                                           start=False, stop=True)
                                    ins.then_inc(sems["av"], 1)

                                p("t", smm)
                                a1 = ctrs["av"].bump(1)
                                q_war[pr] = a1

                                def sexp(s, a1=a1, es=es, esw=es_war[t_i % 2]):
                                    s.wait_ge(sems["av"], a1)
                                    s.wait_ge(sems["av"], esw)
                                    s.activation(es[:], p_S[:], AF.Exp,
                                                 scale=SCALE).then_inc(sems["exp"], 1)

                                p("s", sexp)
                                expt = ctrs["exp"].bump(1)

                                def avmm(t, t_i=t_i, es=es, expt=expt, pr=pr, qc=qc,
                                         wn=nrm_war[pr], wr=rec_war[0]):
                                    t.wait_ge(sems["exp"], expt)
                                    if t_i == 0:
                                        t.wait_ge(sems["nrm"], wn)
                                        t.wait_ge(sems["rec"], wr)
                                    first = t_i == 0
                                    last = t_i == 4 * qc + 3
                                    ins = None
                                    for hl in range(2):
                                        base = 32 * hl
                                        t.matmul(p_oT[pr][:, 512 * hl:512 * (hl + 1)],
                                                 s_V[:, 128 * t_i:128 * (t_i + 1)],
                                                 es[:, 512 * hl:512 * (hl + 1)],
                                                 start=first, stop=last)
                                        ins = t.matmul(p_den[base:base + 1, :],
                                                       s_ones_col[:],
                                                       es[:, 512 * hl:512 * (hl + 1)],
                                                       start=first, stop=last,
                                                       tile_position=(0, base))
                                    ins.then_inc(sems["av"], 1)

                                if pend_av[0] is not None:
                                    p("t", pend_av[0])
                                    es_war[pend_par[0]] = ctrs["av"].bump(1)
                                pend_av[0] = avmm
                                pend_par[0] = t_i % 2

                            p("t", pend_av[0])
                            es_war[pend_par[0]] = ctrs["av"].bump(1)
                            pend_av[0] = None

                            av_end = ctrs["av"].v
                            if dbg and qc == 9 and pr == 1:
                                def dexp9(g, av_end=av_end, par=(4 * 9 + 4 - 1) % 2):
                                    g.wait_ge(sems["av"], av_end)
                                    g.dma_start(out=dbg_t["es9"].ap(),
                                                in_=s_es[par][:]).then_inc(sems["st"], 16)

                                p("g", dexp9)
                                ctrs["st"].bump(16)
                            if dbg and qc == 1 and pr == 0:
                                def dexp(g, av_end=av_end, par=(4 * 1 + 4 - 1) % 2):
                                    g.wait_ge(sems["av"], av_end)
                                    g.dma_start(out=dbg_t["expS"].ap(),
                                                in_=s_es[par][:]).then_inc(sems["st"], 16)

                                p("g", dexp)
                                ctrs["st"].bump(16)

                            nrm0 = ctrs["nrm"].v
                            for hl in range(2):
                                def srec(s, hl=hl, av_end=av_end):
                                    s.wait_ge(sems["av"], av_end)
                                    base = 32 * hl
                                    s.activation(s_recip[hl][:],
                                                 p_den[base:base + 1, :], AF.Ln)
                                    s.activation(s_recip[hl][:],
                                                 s_recip[hl][:], AF.Exp,
                                                 scale=-1.0).then_inc(sems["rec"], 1)

                                p("s", srec)
                                rect = ctrs["rec"].bump(1)
                                rec_war[0] = rect
                                if dbg and qc == 1 and pr == 0 and hl == 0:
                                    def dden(g, rect=rect):
                                        g.wait_ge(sems["rec"], rect)
                                        g.dma_start(out=dbg_t["den"].ap(),
                                                    in_=s_recip[0][:]).then_inc(sems["st"], 16)

                                    p("g", dden)
                                    ctrs["st"].bump(16)

                                def tbc(t, hl=hl, rect=rect, w=bc_war[0]):
                                    t.wait_ge(sems["rec"], rect)
                                    t.wait_ge(sems["rec"], w)
                                    t.matmul(p_bc[:], s_ones_row[:],
                                             s_recip[hl][:], start=True,
                                             stop=True).then_inc(sems["bc"], 1)

                                p("t", tbc)
                                bct = ctrs["bc"].bump(1)

                                def sbcc(s, bct=bct, wrb=rb_war[0]):
                                    s.wait_ge(sems["bc"], bct)
                                    s.wait_ge(sems["nrm"], wrb)
                                    s.copy(s_rb[:], p_bc[:]).then_inc(sems["rec"], 1)

                                p("s", sbcc)
                                rbt = ctrs["rec"].bump(1)
                                bc_war[0] = rbt

                                def vnrm(v, hl=hl, pr=pr, rbt=rbt, w=ot_war[0]):
                                    v.wait_ge(sems["rec"], rbt)
                                    if hl == 0:
                                        v.wait_ge(sems["ao"], w)
                                    v.tensor_mul(s_ot[:, 512 * hl:512 * (hl + 1)],
                                                 p_oT[pr][:, 512 * hl:512 * (hl + 1)],
                                                 s_rb[:]).then_inc(sems["nrm"], 1)

                                p("v", vnrm)
                                rb_war[0] = ctrs["nrm"].bump(1)
                            nrm_war[pr] = ctrs["nrm"].v

                            nrmt = ctrs["nrm"].v

                            def sto(y, qc=qc, pr=pr, nrmt=nrmt):
                                y.wait_ge(sems["nrm"], nrmt)
                                for hl in range(2):
                                    h = 2 * pr + hl
                                    y.dma_start(out=ago_in[128 * h:128 * (h + 1),
                                                           512 * qc:512 * (qc + 1)],
                                                in_=s_ot[:, 512 * hl:512 * (hl + 1)]).then_inc(sems["ao"], 16)

                            p("y", sto)
                            ot_war[0] = ctrs["ao"].bump(32)
                            if dbg and qc == 9 and pr == 1:
                                def dot9(g, nrmt=nrmt):
                                    g.wait_ge(sems["nrm"], nrmt)
                                    g.dma_start(out=dbg_t["oT9"].ap(),
                                                in_=s_ot[:]).then_inc(sems["st"], 16)
                                    g.dma_start(out=dbg_t["den9"][0:1, :],
                                                in_=s_recip[0][:]).then_inc(sems["st"], 16)
                                    g.dma_start(out=dbg_t["den9"][1:2, :],
                                                in_=s_recip[1][:]).then_inc(sems["st"], 16)

                                p("g", dot9)
                                ctrs["st"].bump(48)
                            if dbg and qc == 0 and pr == 0:
                                def dot(g, nrmt=nrmt):
                                    g.wait_ge(sems["nrm"], nrmt)
                                    g.dma_start(out=dbg_t["oT"].ap(),
                                                in_=s_ot[:, 0:512]).then_inc(sems["st"], 16)

                                p("g", dot)
                                ctrs["st"].bump(16)

                            drain_sweep(45)

                    ao_all = ctrs["ao"].v

                    def ag2(g, w=ao_all):
                        g.wait_ge(sems["ao"], w)
                        g.collective_compute("AllGather", ALU.bypass, replica_groups=RG,
                                             ins=[ago_in.ap().opt()],
                                             outs=[ago.ap().opt()]).then_inc(sems["ag"], 1)

                    p("g", ag2)
                    ag2_t = ctrs["ag"].bump(1)

            with ExitStack() as P6S:
                s_woH = [sb(f"s_wo{i}", [128, 16 * 512], BF16, P6S) for i in range(2)]
                s_xc = sb("s_xc", [128, 512], F32, P6S)
                s_x2st = sb("s_x2st", [128, 512], F32, P6S)
                s_do2 = sb("s_do2", [128, 512], F32, P6S) if dbg else None
                AG6S = ExitStack()
                s_ag6 = sb("s_ag6", [128, 32 * 1024], BF16, AG6S)
                p_o2 = [ps(f"p_o2{i}", [128, 512], F32, P6S) for i in range(2)]
                p_wt = [ps(f"p_wt{i}", [128, 1024], BF16, P6S) for i in range(2)]

                def ldago(g, w=ag2_t):
                    g.wait_ge(sems["ag"], w)
                    pid = nc.partition_id([mybir.EngineType.Pool])
                    with g.register("agoff0") as r0:
                        g.reg_alu(r0, pid, TOK, ALU.mult)
                        with g.register("agoff") as r:
                            for ch in range(8):
                                g.reg_alu(r, r0, 512 * ch * S, ALU.add)
                                g.dma_start(
                                    out=bass.AP(agom, 512 * ch * TOK, [[TOK, 512], [1, TOK]]),
                                    in_=bass.AP(ago, r, [[S, 512], [1, TOK]]),
                                ).then_inc(sems["ao"], 16)
                    g.wait_ge(sems["ao"], stage_done)
                    for ct in range(32):
                        row0 = 512 * (ct // 4) + 128 * (ct % 4)
                        g.dma_start(out=s_ag6[:, 1024 * ct:1024 * (ct + 1)],
                                    in_=agom[row0:row0 + 128, :]).then_inc(sems["ao"], 16)

                stage_done = ctrs["ao"].v + 8 * 16
                p("g", ldago)
                ago_ld = ctrs["ao"].bump(40 * 16)

                if dbg:
                    def dag6(g, w=ago_ld):
                        g.wait_ge(sems["ao"], w)
                        g.dma_start(out=dbg_t["ag6"].ap(),
                                    in_=s_ag6[:, 0:1024]).then_inc(sems["st"], 16)

                    p("g", dag6)
                    ctrs["st"].bump(16)

                woh_war = {}
                ep6_war = {}
                xc_war = [0]
                x2st_war = [0]

                for F in range(8):
                    woth = {}
                    for h in range(2):
                        wsem = "wo" if h == 0 else "wo2"

                        def ldwoH(g, F=F, h=h, wsem=wsem,
                                  w=woh_war.get((F - 1, h), 0), wag=ag2_t):
                            g.wait_ge(sems["ag"], wag)
                            g.wait_ge(sems["mm6"], w)
                            src = bass.AP(d_wo, 512 * F + 16 * h * 128 * D,
                                          [[D, 128], [128 * D, 16], [1, 512]])
                            dst = bass.AP(s_woH[h], 0,
                                          [[16 * 512, 128], [512, 16], [1, 512]])
                            g.dma_start(out=dst, in_=src).then_inc(sems[wsem], 16)

                        p("g", ldwoH)
                        woth[h] = ctrs[wsem].bump(16)

                    for TC in range(8):
                        pb = p_o2[TC % 2]
                        g6 = 8 * F + TC

                        def mmwoA(t, F=F, TC=TC, pb=pb, wot=woth[0], ago_ld=ago_ld,
                                  wep=max(0, 8 * F + TC - 1)):
                            t.wait_ge(sems["wo"], wot)
                            if F == 0 and TC == 0:
                                t.wait_ge(sems["ao"], ago_ld)
                            t.wait_ge(sems["ep6"], wep)
                            for ct in range(16):
                                ins = t.matmul(pb[:],
                                               s_ag6[:, 1024 * ct + 128 * TC:1024 * ct + 128 * (TC + 1)],
                                               s_woH[0][:, 512 * ct:512 * (ct + 1)],
                                               start=(ct == 0), stop=False)
                            ins.then_inc(sems["mm6"], 1)

                        p("t", mmwoA)
                        mm6a = ctrs["mm6"].bump(1)

                        def mmwoB(t, F=F, TC=TC, pb=pb, wot=woth[1]):
                            t.wait_ge(sems["wo2"], wot)
                            for ct in range(16):
                                ins = t.matmul(pb[:],
                                               s_ag6[:, 1024 * (16 + ct) + 128 * TC:1024 * (16 + ct) + 128 * (TC + 1)],
                                               s_woH[1][:, 512 * ct:512 * (ct + 1)],
                                               start=False, stop=(ct == 15))
                            ins.then_inc(sems["mm6"], 1)

                        p("t", mmwoB)
                        mm6t = ctrs["mm6"].bump(1)
                        if TC == 7:
                            woh_war[(F, 0)] = mm6a
                            woh_war[(F, 1)] = mm6t

                        if False and dbg and F == 0 and TC == 0:
                            def do2(s, pb=pb, w=mm6t):
                                s.wait_ge(sems["mm6"], w)
                                s.copy(s_do2[:], pb[:]).then_inc(sems["sc"], 1)

                            p("s", do2)
                            do2t = ctrs["sc"].bump(1)

                            def do2d(g, w=do2t):
                                g.wait_ge(sems["sc"], w)
                                g.dma_start(out=dbg_t["o2"].ap(),
                                            in_=s_do2[:]).then_inc(sems["st"], 16)

                            p("g", do2d)
                            ctrs["st"].bump(16)

                        def ldxc(y, F=F, TC=TC, w=g6):
                            y.wait_ge(sems["ep6"], w)
                            y.dma_start(out=s_xc[:],
                                        in_=bass.AP(d_x, 128 * TC * D + 512 * F,
                                                    [[D, 128], [1, 512]])).then_inc(sems["x2"], 16)

                        p("y", ldxc)
                        xct = ctrs["x2"].bump(16)

                        def vep6(v, pb=pb, mm6t=mm6t, xct=xct, w=x2st_war[0]):
                            v.wait_ge(sems["mm6"], mm6t)
                            v.wait_ge(sems["x2"], xct)
                            v.wait_ge(sems["x2s"], w)
                            v.tensor_add(s_x2st[:], pb[:], s_xc[:]).then_inc(sems["ep6"], 1)

                        p("v", vep6)
                        ep6t = ctrs["ep6"].bump(1)

                        def stx2(y, F=F, TC=TC, ep6t=ep6t):
                            y.wait_ge(sems["ep6"], ep6t)
                            y.dma_start(out=bass.AP(x2sp, 128 * TC * D + 512 * F,
                                                    [[D, 128], [1, 512]]),
                                        in_=s_x2st[:]).then_inc(sems["x2s"], 16)

                        p("y", stx2)
                        x2st_war[0] = ctrs["x2s"].bump(16)

                        drain_sweep(14)

                drain_sweep(10 ** 9)
                sw_all = ctrs["sw"].v
                wst_all = ctrs["wst"].v
                mm6_all = ctrs["mm6"].v
                AG6S.close()
                s_wts = sb("s_wts", [128, D], BF16, P6S)
                s_wtst = sb("s_wtst", [128, D], BF16, P6S)

                wts_war = [0]
                tp6_war = [0]
                ev6_war = {}
                for wi in range(2):
                    for j in range(4):
                        def ldwt(y, wi=wi, j=j, w=wst_all, wtp=tp6_war[0], wag=ag2_t,
                                 wmm=mm6_all):
                            y.wait_ge(sems["ag"], wag)
                            y.wait_ge(sems["wst"], w)
                            y.wait_ge(sems["tp"], wtp)
                            y.wait_ge(sems["mm6"], wmm)
                            y.dma_start(out=s_wts[:],
                                        in_=wsp[OROWS * wi + 128 * j:OROWS * wi + 128 * (j + 1),
                                                :]).then_inc(sems["wt6"], 16)

                        p("y", ldwt)
                        wtt = ctrs["wt6"].bump(16)

                        for gidx in range(4):
                            pbw = p_wt[gidx % 2]
                            key = (wi, j, gidx)
                            prev_key = (wi, j, gidx - 2) if gidx >= 2 else \
                                ((wi, j - 1, gidx + 2) if j >= 1 else
                                 ((wi - 1, 3, gidx + 2) if wi >= 1 else None))

                            def twt(t, gidx=gidx, pbw=pbw, wtt=wtt,
                                    wev=ev6_war.get(prev_key, 0)):
                                t.wait_ge(sems["wt6"], wtt)
                                t.wait_ge(sems["ev"], wev)
                                ins = None
                                for f in range(8):
                                    ft = 8 * gidx + f
                                    ins = t.transpose(pbw[:, 128 * f:128 * (f + 1)],
                                                      s_wts[:, 128 * ft:128 * (ft + 1)], s_ident[:])
                                ins.then_inc(sems["tp"], 1)

                            p("t", twt)
                            tptw = ctrs["tp"].bump(1)
                            tp6_war[0] = tptw

                            def swev(s, gidx=gidx, pbw=pbw, tptw=tptw, w=wts_war[0]):
                                s.wait_ge(sems["tp"], tptw)
                                if gidx == 0:
                                    s.wait_ge(sems["wst"], w)
                                s.copy(s_wtst[:, 1024 * gidx:1024 * (gidx + 1)],
                                       pbw[:]).then_inc(sems["ev"], 1)

                            p("s", swev)
                            ev6_war[key] = ctrs["ev"].bump(1)

                        def stwT(y, wi=wi, j=j, w=ev6_war[(wi, j, 3)]):
                            y.wait_ge(sems["ev"], w)
                            src = bass.AP(s_wtst, 0, [[D, 128], [128, 32], [1, 128]])
                            dst = bass.AP(agw_in, OROWS * wi + 128 * j,
                                          [[2 * OROWS, 128], [128 * 2 * OROWS, 32], [1, 128]])
                            y.dma_start(out=dst, in_=src).then_inc(sems["wst"], 16)

                        p("y", stwT)
                        wts_war[0] = ctrs["wst"].bump(16)

                agw_done = ctrs["wst"].v

                def ag3(g, w=agw_done):
                    g.wait_ge(sems["wst"], w)
                    g.collective_compute("AllGather", ALU.bypass, replica_groups=RG,
                                         ins=[agw_in.ap().opt()],
                                         outs=[agw.ap().opt()]).then_inc(sems["ag"], 1)

                p("g", ag3)
                ag3_t = ctrs["ag"].bump(1)
                p6_x2s = ctrs["x2s"].v

                if dbg:
                    def dx2sp(g, w=p6_x2s):
                        g.wait_ge(sems["x2s"], w)
                        g.dma_start(out=dbg_t["x2sp"].ap(),
                                    in_=x2sp[0:128, :]).then_inc(sems["st"], 16)
                        g.dma_start(out=dbg_t["x2sp7a"].ap(),
                                    in_=x2sp[896:1024, :]).then_inc(sems["st"], 16)

                    p("g", dx2sp)
                    ctrs["st"].bump(32)

        with ExitStack() as X2N:
            s_x2nT = sb("s_x2nT", [128, 32 * TOK], BF16, X2N)
            x2nT_ref = s_x2nT
            with ExitStack() as P7S:
                s_x2r = [sb(f"s_x2r{i}", [128, D], F32, P7S) for i in range(2)]
                s_z2 = sb("s_z2", [128, D], BF16, P7S)
                s_sq2 = sb("s_sq2", [128, D], F32, P7S)
                p_t7 = [ps(f"p_t7{i}", [128, 1024], BF16, P7S) for i in range(2)]

                v7_done = {}
                s7_done = {}
                ev7 = {}
                t7_t = {}
                for TC in range(8):
                    buf = s_x2r[TC % 2]
                    x7sem = "x7" if TC % 2 == 0 else "x7b"

                    def ldx2(y, TC=TC, buf=buf, w=p6_x2s, wagw=agw_done, x7sem=x7sem,
                             wv=v7_done.get(TC - 2, 0), ws=s7_done.get(TC - 2, 0)):
                        y.wait_ge(sems["x2s"], w)
                        y.wait_ge(sems["wst"], wagw)
                        y.wait_ge(sems["v1"], wv)
                        y.wait_ge(sems["sc"], ws)
                        y.dma_start(out=buf[:],
                                    in_=x2sp[128 * TC:128 * (TC + 1), :]).then_inc(sems[x7sem], 16)

                    p("y", ldx2)
                    x2t = ctrs[x7sem].bump(16)

                    stat = stat2b[TC % 2]
                    s7rt = emit_lnstats(stat, buf, (x7sem, x2t), 0, s_sq2, p7=True)
                    s7_done[TC] = s7rt

                    def v7c(v, buf=buf, stat=stat, s7rt=s7rt, TC=TC,
                            wtp=t7_t.get(TC - 1, 0)):
                        v.wait_ge(sems["sc"], s7rt)
                        v.wait_ge(sems["tp"], wtp)
                        v.tensor_scalar(s_m[:, TC:TC + 1], stat[:, 16:17], 4096.0, None, ALU.is_gt)
                        v.tensor_scalar(s_m1[:, TC:TC + 1], stat[:, 16:17], 4096.0, None, ALU.is_le)
                        v.tensor_scalar(s_z2[:], buf[:], stat[:, 1:2], stat[:, 13:14],
                                        ALU.subtract, ALU.mult).then_inc(sems["v1"], 1)

                    p("v", v7c)
                    v7_done[TC] = ctrs["v1"].bump(1)
                    if dbg and TC == 0:
                        def dx2(g, w=v7_done[TC], buf=buf):
                            g.wait_ge(sems["v1"], w)
                            g.dma_start(out=dbg_t["x2"].ap(), in_=buf[:]).then_inc(sems["st"], 16)
                            g.dma_start(out=dbg_t["mcol"].ap(), in_=s_m[:]).then_inc(sems["st"], 16)

                        p("g", dx2)
                        ctrs["st"].bump(32)

                    for gidx in range(4):
                        pb7 = p_t7[gidx % 2]
                        key7 = (TC, gidx)
                        prev7 = (TC, gidx - 2) if gidx >= 2 else (TC - 1, gidx + 2)

                        def t7(t, gidx=gidx, pb7=pb7, w=v7_done[TC],
                               wev=ev7.get(prev7, 0)):
                            t.wait_ge(sems["v1"], w)
                            t.wait_ge(sems["ev"], wev)
                            ins = None
                            for f in range(8):
                                ft = 8 * gidx + f
                                ins = t.transpose(pb7[:, 128 * f:128 * (f + 1)],
                                                  s_z2[:, 128 * ft:128 * (ft + 1)], s_ident[:])
                            ins.then_inc(sems["tp"], 1)

                        p("t", t7)
                        t7t = ctrs["tp"].bump(1)
                        t7_t[TC] = t7t

                        def s7e(s, gidx=gidx, pb7=pb7, t7t=t7t, TC=TC):
                            s.wait_ge(sems["tp"], t7t)
                            ins = None
                            for f in range(8):
                                ft = 8 * gidx + f
                                ins = s.copy(s_x2nT[:, TOK * ft + 128 * TC:TOK * ft + 128 * (TC + 1)],
                                             pb7[:, 128 * f:128 * (f + 1)])
                            ins.then_inc(sems["ev"], 1)

                        p("s", s7e)
                        ev7[key7] = ctrs["ev"].bump(1)

                emit_lnstats(stat2b[0], s_x2r[1], ("x7b", ctrs["x7b"].v), 0,
                             s_sq2, p7=True)

            p7_ev = ctrs["ev"].v
            p7_v1 = ctrs["v1"].v

            if dbg:
                def dx2nT(g, w=p7_ev):
                    g.wait_ge(sems["ev"], w)
                    g.dma_start(out=dbg_t["x2nT"].ap(),
                                in_=s_x2nT[:, 0:1024]).then_inc(sems["st"], 16)

                p("g", dx2nT)
                ctrs["st"].bump(16)

            with ExitStack() as P9S:
                s_wbl = sb("s_wbl", [128, 32 * 512], BF16, P9S)
                s_wbh = sb("s_wbh", [128, 32 * 512], BF16, P9S)
                s_ylev = sb("s_ylev", [128, 512], F32, P9S)
                s_t1 = sb("s_t1", [128, 512], F32, P9S)
                s_t2 = sb("s_t2", [128, 512], F32, P9S)
                s_t3 = sb("s_t3", [128, 512], F32, P9S)
                s_t4 = sb("s_t4", [128, 512], F32, P9S)
                s_x2c = sb("s_x2c", [128, 512], F32, P9S)
                s_bb = sb("s_bb", [128, 512], F32, P9S)
                s_ost = sb("s_ost", [128, 512], F32, P9S)
                p_y = [ps(f"p_y{i}", [128, 1024], F32, P9S) for i in range(2)]
                p_bb = ps("p_bb", [128, 512], F32, P9S)

                ff_rank_done = {}
                fe_war = {}
                ye_war = [0]
                ye_war2 = [0]
                ge_war = [0]
                bb_war = [0]
                x2c_war = [0]
                ost_war = [0]

                for r in range(NCORES):
                    def ldw(g, r=r, w=ag3_t, wff=ff_rank_done.get(r - 1, 0),
                            wev=p7_ev):
                        g.wait_ge(sems["ag"], w)
                        g.wait_ge(sems["ev"], wev)
                        g.wait_ge(sems["ff"], wff)
                        srcl = bass.AP(agw, (D * r) * (2 * OROWS),
                                       [[2 * OROWS, 128], [128 * 2 * OROWS, 32], [1, 512]])
                        srch = bass.AP(agw, (D * r) * (2 * OROWS) + OROWS,
                                       [[2 * OROWS, 128], [128 * 2 * OROWS, 32], [1, 512]])
                        dstl = bass.AP(s_wbl, 0, [[32 * 512, 128], [512, 32], [1, 512]])
                        dsth = bass.AP(s_wbh, 0, [[32 * 512, 128], [512, 32], [1, 512]])
                        g.dma_start(out=dstl, in_=srcl).then_inc(sems["wo"], 16)
                        g.dma_start(out=dsth, in_=srch).then_inc(sems["wo"], 16)

                    p("g", ldw)
                    wt9 = ctrs["wo"].bump(32)

                    def tb9(t, r=r, wld=ld_setup, w=bb_war[0], wev=p7_ev):
                        t.wait_ge(sems["ld"], wld)
                        t.wait_ge(sems["ev"], wev)
                        t.wait_ge(sems["fe"], w)
                        t.matmul(p_bb[:], s_ones_row[:],
                                 s_bias[0:1, 512 * r:512 * (r + 1)],
                                 start=True, stop=True).then_inc(sems["bc"], 1)

                    p("t", tb9)
                    bct9 = ctrs["bc"].bump(1)

                    def sb9(s, bct9=bct9):
                        s.wait_ge(sems["bc"], bct9)
                        s.copy(s_bb[:], p_bb[:]).then_inc(sems["rec"], 1)

                    p("s", sb9)
                    sb9t = ctrs["rec"].bump(1)

                    for TCi, TC in enumerate(range(8)):
                        py = p_y[TC % 2]
                        g9 = 8 * r + TCi

                        def mf(t, r=r, TC=TC, py=py, wt9=wt9, p7_ev=p7_ev,
                               wfe=fe_war.get(g9 - 2, 0)):
                            t.wait_ge(sems["wo"], wt9)
                            if r == 0 and TC == 0:
                                t.wait_ge(sems["ev"], p7_ev)
                            t.wait_ge(sems["fe"], wfe)
                            for it in range(32):
                                xt = s_x2nT[:, TOK * it + 128 * TC:TOK * it + 128 * (TC + 1)]
                                t.matmul(py[:, 0:512], xt,
                                         s_wbl[:, 512 * it:512 * (it + 1)],
                                         start=(it == 0), stop=(it == 31))
                                ins = t.matmul(py[:, 512:1024], xt,
                                               s_wbh[:, 512 * it:512 * (it + 1)],
                                               start=(it == 0), stop=(it == 31))
                            ins.then_inc(sems["ff"], 1)

                        p("t", mf)
                        fft = ctrs["ff"].bump(1)
                        if TC == 7:
                            ff_rank_done[r] = fft

                        def ldx2c(y, r=r, TC=TC, w=p6_x2s, wfe=x2c_war[0]):
                            y.wait_ge(sems["x2s"], w)
                            y.wait_ge(sems["fe"], wfe)
                            y.dma_start(out=s_x2c[:],
                                        in_=bass.AP(x2sp, 128 * TC * D + 512 * r,
                                                    [[D, 128], [1, 512]])).then_inc(sems["x2"], 16)

                        p("y", ldx2c)
                        x2ct = ctrs["x2"].bump(16)

                        def sy9b(s, fft=fft, py=py, w=ye_war2[0], TC=TC, wv=p7_v1):
                            s.wait_ge(sems["ff"], fft)
                            s.wait_ge(sems["v1"], w)
                            s.wait_ge(sems["v1"], wv)
                            s.copy(s_ylev[:], py[:, 0:512])
                            s.mul(s_t1[:], s_ylev[:], s_m1[:, TC:TC + 1])
                            s.mul(s_t2[:], py[:, 512:1024],
                                  s_m[:, TC:TC + 1]).then_inc(sems["sc"], 1)

                        p("s", sy9b)
                        syt = ctrs["sc"].bump(1)

                        def ve9(v, syt=syt, x2ct=x2ct, sb9t=sb9t, wge=ge_war[0]):
                            v.wait_ge(sems["sc"], syt)
                            v.wait_ge(sems["x2"], x2ct)
                            v.wait_ge(sems["rec"], sb9t)
                            v.wait_ge(sems["fe"], wge)
                            v.tensor_add(s_t3[:], s_t1[:], s_t2[:])
                            v.tensor_add(s_t4[:], s_bb[:], s_x2c[:]).then_inc(sems["v1"], 1)

                        p("v", ve9)
                        vet9 = ctrs["v1"].bump(1)
                        ye_war2[0] = vet9

                        def ge9(g, vet9=vet9, wost=ost_war[0]):
                            g.wait_ge(sems["v1"], vet9)
                            g.wait_ge(sems["st"], wost)
                            g.tensor_add(s_ost[:], s_t3[:], s_t4[:]).then_inc(sems["fe"], 1)

                        p("g", ge9)
                        fet = ctrs["fe"].bump(1)
                        ge_war[0] = fet
                        fe_war[g9] = fet
                        ye_war[0] = ctrs["sc"].v
                        x2c_war[0] = fet
                        bb_war[0] = fet if TC == 7 else bb_war[0]

                        def sto9(y, r=r, TC=TC, fet=fet):
                            y.wait_ge(sems["fe"], fet)
                            y.dma_start(out=bass.AP(d_out, 128 * TC * D + 512 * r,
                                                    [[D, 128], [1, 512]]),
                                        in_=s_ost[:]).then_inc(sems["st"], 16)

                        p("y", sto9)
                        ost_war[0] = ctrs["st"].bump(16)

        if dbg:
            def dfin(g, w=ctrs["fe"].v):
                g.wait_ge(sems["fe"], w)
                g.dma_start(out=dbg_t["mfin"].ap(), in_=s_m[:]).then_inc(sems["st"], 16)
                g.dma_start(out=dbg_t["m1fin"].ap(), in_=s_m1[:]).then_inc(sems["st"], 16)
                g.dma_start(out=dbg_t["xnt7"].ap(),
                            in_=x2nT_ref[:, 0:1024]).then_inc(sems["st"], 16)
                g.dma_start(out=dbg_t["x2sp7b"].ap(),
                            in_=x2sp[896:1024, :]).then_inc(sems["st"], 16)
                g.dma_start(out=dbg_t["xntall"].ap(),
                            in_=x2nT_ref[:]).then_inc(sems["st"], 16)
                g.dma_start(out=dbg_t["agw0"].ap(),
                            in_=agw[0:128, 0:512]).then_inc(sems["st"], 16)
                g.dma_start(out=dbg_t["x2all"].ap(),
                            in_=x2sp.ap()).then_inc(sems["st"], 16)
                g.dma_start(out=dbg_t["stB"].ap(),
                            in_=stat2b[1][:]).then_inc(sems["st"], 16)
                g.dma_start(out=dbg_t["ago9"].ap(),
                            in_=ago[0:4096, 4608:5120]).then_inc(sems["st"], 16)

            p("g", dfin)
            ctrs["st"].bump(144)

        st_all = ctrs["st"].v

        def fin(y, w=st_all):
            y.wait_ge(sems["st"], w)

        p("y", fin)

        with nc.Block() as block:
            @block.gpsimd
            def _(g):
                for fn in ops["g"]:
                    fn(g)

            @block.tensor
            def _(t):
                for fn in ops["t"]:
                    fn(t)

            @block.vector
            def _(v):
                for fn in ops["v"]:
                    fn(v)

            @block.scalar
            def _(s):
                for fn in ops["s"]:
                    fn(s)

            @block.sync
            def _(y):
                for fn in ops["y"]:
                    fn(y)

    return nc


def kernel(L0_attn_hidden_states, L0_attn_Wq, L0_attn_Wk, L0_attn_Wv, L0_attn_Wo,
           L0_ffn_qweight, L0_ffn_lut4, L0_ffn_lut8, L0_ffn_bias, _dbg=False):
    x = np.ascontiguousarray(np.asarray(L0_attn_hidden_states)[0], dtype=np.float32)
    Wq = np.asarray(L0_attn_Wq, dtype=np.float32)
    Wk = np.asarray(L0_attn_Wk, dtype=np.float32)
    Wv = np.asarray(L0_attn_Wv, dtype=np.float32)
    Wo = np.ascontiguousarray(np.asarray(L0_attn_Wo, dtype=np.float32))
    qw = np.ascontiguousarray(np.asarray(L0_ffn_qweight, dtype=np.int32))
    lut4 = np.asarray(L0_ffn_lut4, dtype=np.float32)
    lut8 = np.asarray(L0_ffn_lut8, dtype=np.float32)
    bias = np.ascontiguousarray(np.asarray(L0_ffn_bias, dtype=np.float32).reshape(1, D))
    cosT, sinTs, _, _ = _host_consts()

    nc = build(dbg=_dbg)
    in_maps = []
    for c in range(NCORES):
        in_maps.append({
            "x_mine": np.ascontiguousarray(x[TOK * c:TOK * (c + 1)]),
            "wq_m": np.ascontiguousarray(Wq[:, 512 * c:512 * (c + 1)]),
            "wk_m": np.ascontiguousarray(Wk[:, 128 * c:128 * (c + 1)]),
            "wv_m": np.ascontiguousarray(Wv[:, 128 * c:128 * (c + 1)]),
            "wo": Wo,
            "qw_m": np.ascontiguousarray(qw[:, OROWS * c:OROWS * (c + 1), :]),
            "lut4_m": np.ascontiguousarray(lut4[OROWS * c:OROWS * (c + 1)]),
            "lut8_m": np.ascontiguousarray(lut8[OROWS * c:OROWS * (c + 1)]),
            "bias": bias,
            "cosT": cosT,
            "sinTs": sinTs,
        })
    res = run_bass_kernel_spmd(nc, in_maps, core_ids=list(range(NCORES)))
    out = np.concatenate([res.results[c]["out_m"] for c in range(NCORES)], axis=0)
    if _dbg:
        return out[None], res
    return out[None]


# version 06 (final)
# speedup vs baseline: 1.2087x; 1.0174x over previous
import numpy as np
import ml_dtypes
import concourse.bass as bass
import concourse.mybir as mybir
from concourse.bass_utils import run_bass_kernel_spmd
from contextlib import ExitStack

F32 = mybir.dt.float32
BF16 = mybir.dt.bfloat16
I32 = mybir.dt.int32
AF = mybir.ActivationFunctionType
ALU = mybir.AluOpType
BF16NP = ml_dtypes.bfloat16

NCORES = 8
S = 8192
D = 4096
HD = 128
NQH = 4
TOK = 1024
QC = 512
EPS = 1e-5
SCALE = 1.0 / float(np.sqrt(128.0))
OROWS = 512
RG = [list(range(NCORES))]


def _host_consts():
    inv_freq = 1.0 / (10000.0 ** (np.arange(0, HD, 2, dtype=np.float32) / HD))
    t = np.arange(S, dtype=np.float32)
    freqs = np.outer(t, inv_freq)
    emb = np.concatenate([freqs, freqs], axis=-1)
    cosT = np.ascontiguousarray(np.cos(emb).astype(np.float32).T)
    sinT = np.ascontiguousarray(np.sin(emb).astype(np.float32).T)
    sinTs = sinT.copy()
    sinTs[:64] = -sinTs[:64]
    m01 = np.zeros((4, 128, QC), dtype=np.float32)
    kk = np.arange(128)[:, None]
    qq = np.arange(QC)[None, :]
    for i in range(4):
        m01[i] = np.where(128 * i + kk <= qq, 0.0, -2000.0)
    masks = m01.astype(BF16NP)
    ident = np.eye(128, dtype=BF16NP)
    return cosT, sinTs, masks, ident


class Ctr:
    def __init__(self):
        self.v = 0

    def bump(self, n=1):
        self.v += n
        return self.v


def build(dbg=False):
    nc = bass.Bass("TRN2", target_bir_lowering=False, debug=False)
    cosT_np, sinTs_np, masks_np, ident_np = _host_consts()

    d_x = nc.dram_tensor("x_mine", [TOK, D], F32, kind="ExternalInput")
    d_wq = nc.dram_tensor("wq_m", [D, NQH * HD], F32, kind="ExternalInput")
    d_wk = nc.dram_tensor("wk_m", [D, HD], F32, kind="ExternalInput")
    d_wv = nc.dram_tensor("wv_m", [D, HD], F32, kind="ExternalInput")
    d_wo = nc.dram_tensor("wo", [D, D], F32, kind="ExternalInput")
    d_qw = nc.dram_tensor("qw_m", [8, OROWS, 128], I32, kind="ExternalInput")
    d_lut4 = nc.dram_tensor("lut4_m", [OROWS, 16], F32, kind="ExternalInput")
    d_lut8 = nc.dram_tensor("lut8_m", [OROWS, 256], F32, kind="ExternalInput")
    d_bias = nc.dram_tensor("bias", [1, D], F32, kind="ExternalInput")
    d_cos = nc.dram_tensor("cosT", [HD, S], F32, kind="ExternalInput")
    d_sin = nc.dram_tensor("sinTs", [HD, S], F32, kind="ExternalInput")
    d_out = nc.dram_tensor("out_m", [TOK, D], F32, kind="ExternalOutput")

    d_masks = nc.inline_tensor(masks_np.reshape(4 * 128, QC), name="masks")
    d_ident = nc.inline_tensor(ident_np, name="ident")

    agx_in = nc.dram_tensor("agx_in", [D, TOK], BF16)
    agx = nc.dram_tensor("agx", [NCORES * D, TOK], BF16)
    qspill = nc.dram_tensor("qspill", [NQH * HD, S], BF16)
    ago_in = nc.dram_tensor("ago_in", [NQH * HD, S], BF16)
    ago = nc.dram_tensor("ago", [NCORES * NQH * HD, S], BF16)
    agom = nc.dram_tensor("agom", [NCORES * NQH * HD, TOK], BF16)
    agw_in = nc.dram_tensor("agw_in", [D, 2 * OROWS], BF16)
    agw = nc.dram_tensor("agw", [NCORES * D, 2 * OROWS], BF16)
    x2sp = nc.dram_tensor("x2sp", [TOK, D], F32)
    wsp = nc.dram_tensor("wsp", [2 * OROWS, D], BF16)

    dbg_t = {}
    if dbg:
        for nm, shp in [("z1", [128, D]), ("kT", [128, 512]), ("expS", [128, 1024]),
                        ("x2", [128, D]), ("wl", [128, 512]), ("idx8", [128, 512]),
                        ("mcol", [128, 8]), ("oT", [128, 512]), ("den", [1, 512]),
                        ("scr", [128, 512]), ("rot", [128, 512]), ("cs", [128, 512]),
                        ("sn", [128, 512]), ("kTlate", [128, 512]),
                        ("ag6", [128, 1024]), ("o2", [128, 512]),
                        ("axin", [128, 512]), ("agx0", [128, 512]),
                        ("x2sp", [128, D]), ("x2nT", [128, 1024]),
                        ("x0", [128, D]), ("st0", [128, 20]),
                        ("mfin", [128, 8]), ("m1fin", [128, 8]),
                        ("xnt7", [128, 1024]), ("x2sp7a", [128, D]),
                        ("x2sp7b", [128, D]), ("xntall", [128, 32 * TOK]),
                        ("agw0", [128, 512]), ("x2all", [1024, D]),
                        ("stB", [128, 20]), ("ago9", [4096, 512]),
                        ("oT9", [128, 1024]), ("den9", [2, 512]),
                        ("es9", [128, 1024])]:
            dbg_t[nm] = nc.dram_tensor("dbg_" + nm, shp, F32, kind="ExternalOutput")

    ops = {k: [] for k in "gtvsy"}

    def p(eng, fn):
        ops[eng].append(fn)

    sems = {}
    ctrs = {}
    for nm in ["ld", "x1", "v1", "sc", "tp", "ag", "xn", "xnc", "ev", "rot",
               "g1", "q4", "exp", "msk", "av", "nrm", "rec", "bc", "ao", "wo",
               "mm6", "ep6", "x2", "u", "sw", "wst", "ff", "fe", "st", "cs",
               "vt", "qs", "x2s", "x7",
               "x1b", "sx", "xn1", "xn2", "xn3", "csb", "rotb", "qp",
               "q4b", "x7b", "wt6", "wo2"]:
        sems[nm] = nc.alloc_semaphore(nm)
        ctrs[nm] = Ctr()

    SB = ExitStack()

    def sb(name, shape, dt, stack):
        return stack.enter_context(nc.sbuf_tensor(name, shape, dt))

    def ps(name, shape, dt, stack):
        return stack.enter_context(nc.psum_tensor(name, shape, dt))

    with SB:
        s_ident = sb("s_ident", [128, 128], BF16, SB)
        s_ones_col = sb("s_ones_col", [128, 1], BF16, SB)
        s_ones_row = sb("s_ones_row", [1, 128], F32, SB)
        s_bias = sb("s_bias", [1, D], F32, SB)
        s_m = sb("s_m", [128, 8], F32, SB)
        s_m1 = sb("s_m1", [128, 8], F32, SB)
        stat2b = [sb("statA", [128, 20], F32, SB),
                  sb("statB", [128, 20], F32, SB)]
        s_recip = [sb("s_recip0", [1, 512], F32, SB),
                   sb("s_recip1", [1, 512], F32, SB)]

        def setup_y(y):
            y.dma_start(out=s_ident[:], in_=d_ident.ap()).then_inc(sems["ld"], 16)
            y.dma_start(out=s_bias[:], in_=d_bias.ap()).then_inc(sems["ld"], 16)

        p("y", setup_y)
        ld_setup = ctrs["ld"].bump(32)

        def setup_g(g):
            g.memset(s_ones_col[:], 1.0)
            g.memset(s_ones_row[:], 1.0).then_inc(sems["g1"], 1)

        p("g", setup_g)
        g1_setup = ctrs["g1"].bump(1)

        def emit_lnstats(stat, buf, ldwait, wdbg, sqscr, p7):
            wsem, wval = ldwait

            def v0(v, wval=wval):
                v.wait_ge(sems[wsem], wval)
                v.reduce_sum(stat[:, 0:1], buf[:],
                             axis=mybir.AxisListType.X).then_inc(sems["v1"], 1)

            p("v", v0)
            tv0 = ctrs["v1"].bump(1)

            def s0(s, wval=wval, wdbg=wdbg, tv0=tv0):
                s.wait_ge(sems[wsem], wval)
                if wdbg:
                    s.wait_ge(sems["st"], wdbg)
                s.activation(sqscr[:], buf[:], AF.Square, accum_out=stat[:, 2:3])
                s.wait_ge(sems["v1"], tv0)
                s.activation(stat[:, 1:2], stat[:, 0:1], AF.Copy,
                             scale=1.0 / D).then_inc(sems["sc"], 1)

            p("s", s0)
            ts0 = ctrs["sc"].bump(1)

            def v1f(v, ts0=ts0):
                v.wait_ge(sems["sc"], ts0)
                v.tensor_scalar(stat[:, 3:4], stat[:, 2:3], 1.0 / D, None, ALU.mult)
                v.scalar_tensor_tensor(stat[:, 4:5], stat[:, 1:2], -1.0,
                                       stat[:, 1:2], ALU.mult,
                                       ALU.mult).then_inc(sems["v1"], 1)

            p("v", v1f)
            tv1 = ctrs["v1"].bump(1)

            def s1(s, tv1=tv1):
                s.wait_ge(sems["v1"], tv1)
                s.add(stat[:, 6:7], stat[:, 3:4], stat[:, 4:5]).then_inc(sems["sc"], 1)

            p("s", s1)
            ts1 = ctrs["sc"].bump(1)

            def v2(v, ts1=ts1):
                v.wait_ge(sems["sc"], ts1)
                v.tensor_scalar(stat[:, 7:8], stat[:, 6:7], EPS, None,
                                ALU.add).then_inc(sems["v1"], 1)

            p("v", v2)
            tv2 = ctrs["v1"].bump(1)

            def s2(s, tv2=tv2):
                s.wait_ge(sems["v1"], tv2)
                s.activation(stat[:, 8:9], stat[:, 7:8], AF.Ln).then_inc(sems["sc"], 1)

            p("s", s2)
            ts2 = ctrs["sc"].bump(1)

            def v3(v, ts2=ts2):
                v.wait_ge(sems["sc"], ts2)
                v.tensor_copy(stat[:, 9:10], stat[:, 8:9]).then_inc(sems["v1"], 1)

            p("v", v3)
            tv3 = ctrs["v1"].bump(1)

            def s3(s, tv3=tv3):
                s.wait_ge(sems["v1"], tv3)
                s.activation(stat[:, 10:11], stat[:, 9:10], AF.Exp,
                             scale=-0.5).then_inc(sems["sc"], 1)

            p("s", s3)
            ts3 = ctrs["sc"].bump(1)

            def v4(v, ts3=ts3):
                v.wait_ge(sems["sc"], ts3)
                v.tensor_mul(stat[:, 11:12], stat[:, 10:11], stat[:, 10:11])
                v.tensor_copy(stat[:, 5:6], stat[:, 10:11]).then_inc(sems["v1"], 1)

            p("v", v4)
            tv4 = ctrs["v1"].bump(1)

            def s4(s, tv4=tv4):
                s.wait_ge(sems["v1"], tv4)
                s.mul(stat[:, 12:13], stat[:, 11:12], stat[:, 7:8]).then_inc(sems["sc"], 1)

            p("s", s4)
            ts4 = ctrs["sc"].bump(1)

            def v5(v, ts4=ts4):
                v.wait_ge(sems["sc"], ts4)
                v.tensor_scalar(stat[:, 14:15], stat[:, 12:13], -0.5, 1.5,
                                ALU.mult, ALU.add).then_inc(sems["v1"], 1)

            p("v", v5)
            tv5 = ctrs["v1"].bump(1)

            def s5(s, tv5=tv5):
                s.wait_ge(sems["v1"], tv5)
                s.mul(stat[:, 13:14], stat[:, 14:15],
                      stat[:, 5:6]).then_inc(sems["sc"], 1)

            p("s", s5)
            ts5 = ctrs["sc"].bump(1)
            if not p7:
                return ts5

            def v6(v, ts5=ts5):
                v.wait_ge(sems["sc"], ts5)
                v.tensor_mul(stat[:, 15:16], stat[:, 13:14], stat[:, 13:14])
                v.tensor_scalar(stat[:, 17:18], stat[:, 6:7], float(D), None,
                                ALU.mult).then_inc(sems["v1"], 1)

            p("v", v6)
            tv6 = ctrs["v1"].bump(1)

            def s6(s, tv6=tv6):
                s.wait_ge(sems["v1"], tv6)
                s.mul(stat[:, 16:17], stat[:, 15:16],
                      stat[:, 17:18]).then_inc(sems["sc"], 1)

            p("s", s6)
            return ctrs["sc"].bump(1)

        with ExitStack() as DQ:
            HW = 2048
            s_lut4 = sb("s_lut4", [128, 4 * 16], F32, DQ)
            s_lut8 = sb("s_lut8", [128, 4 * 256], F32, DQ)
            s_qw = sb("s_qw", [128, 8 * 64], I32, DQ)
            s_ctrI = [sb("s_ctrI0", [128, HW], I32, DQ),
                      sb("s_ctrI1", [128, HW], I32, DQ)]
            s_idx8I = sb("s_idx8I", [128, HW], I32, DQ)
            s_ctr = sb("s_ctr", [128, HW], BF16, DQ)
            s_idx8 = sb("s_idx8", [128, HW], BF16, DQ)
            s_idx4 = sb("s_idx4", [128, HW], BF16, DQ)
            s_wl = sb("s_wl", [128, HW], BF16, DQ)
            s_wh = sb("s_wh", [128, HW], BF16, DQ)

            def lut_loads(y):
                for j in range(4):
                    y.dma_start(out=s_lut4[:, 16 * j:16 * (j + 1)],
                                in_=d_lut4[128 * j:128 * (j + 1), :]).then_inc(sems["ld"], 16)
                    y.dma_start(out=s_lut8[:, 256 * j:256 * (j + 1)],
                                in_=d_lut8[128 * j:128 * (j + 1), :]).then_inc(sems["ld"], 16)

            p("y", lut_loads)
            ld_luts = ctrs["ld"].bump(8 * 16)

            def emit_sweep():
                last_store = [0]
                for j in range(4):
                    for hf in range(2):
                        uprev = ctrs["u"].v

                        def ldqw(y, j=j, hf=hf, uprev=uprev):
                            y.wait_ge(sems["u"], uprev)
                            src = bass.AP(d_qw, 128 * j * 128 + 64 * hf,
                                          [[128, 128], [OROWS * 128, 8], [1, 64]])
                            dst = bass.AP(s_qw, 0, [[8 * 64, 128], [64, 8], [1, 64]])
                            y.dma_start(out=dst, in_=src).then_inc(sems["qs"], 16)

                        yield ("y", ldqw)
                        qs_t = ctrs["qs"].bump(16)

                        gor = {}
                        for pl in range(8):
                            def vext(v, pl=pl, qs_t=qs_t, ld_luts=ld_luts,
                                     wprev=last_store[0], wg=gor.get(pl - 2, 0)):
                                if pl == 0:
                                    v.wait_ge(sems["qs"], qs_t)
                                    v.wait_ge(sems["ld"], ld_luts)
                                    v.wait_ge(sems["wst"], wprev)
                                v.wait_ge(sems["g1"], wg)
                                tgt = s_idx8I if pl == 0 else s_ctrI[pl % 2]
                                ins = None
                                for b in range(32):
                                    sh = b - pl
                                    op0 = (ALU.logical_shift_right if sh >= 0
                                           else ALU.logical_shift_left)
                                    dst = bass.AP(tgt, b, [[HW, 128], [32, 64]])
                                    ins = v.tensor_scalar(dst,
                                                          s_qw[:, 64 * pl:64 * (pl + 1)],
                                                          abs(sh), 1 << pl, op0,
                                                          ALU.bitwise_and)
                                ins.then_inc(sems["msk"], 1)

                            yield ("v", vext)
                            vet = ctrs["msk"].bump(1)

                            if pl >= 1:
                                def gor_fn(g, pl=pl, vet=vet):
                                    g.wait_ge(sems["msk"], vet)
                                    g.tensor_tensor(s_idx8I[:], s_idx8I[:],
                                                    s_ctrI[pl % 2][:],
                                                    ALU.add).then_inc(sems["g1"], 1)

                                yield ("g", gor_fn)
                                gor[pl] = ctrs["g1"].bump(1)

                        def vun(v, wg=gor[7]):
                            v.wait_ge(sems["g1"], wg)
                            v.tensor_copy(s_idx8[:], s_idx8I[:])
                            v.tensor_scalar(s_ctrI[0][:], s_idx8I[:], 15, None,
                                            ALU.bitwise_and)
                            v.tensor_copy(s_idx4[:], s_ctrI[0][:])
                            v.memset(s_wl[:], 0.0)
                            v.memset(s_wh[:], 0.0).then_inc(sems["u"], 1)

                        yield ("v", vun)
                        ut = ctrs["u"].bump(1)
                        if dbg and j == 0 and hf == 0:
                            def di(g, ut=ut):
                                g.wait_ge(sems["u"], ut)
                                g.dma_start(out=dbg_t["idx8"].ap(),
                                            in_=s_idx8[:, 0:512]).then_inc(sems["st"], 16)

                            yield ("g", di)
                            ctrs["st"].bump(16)

                        for c in range(16):
                            def vc4(v, c=c, j=j):
                                v.tensor_scalar(s_ctr[:], s_idx4[:], float(c),
                                                s_lut4[:, 16 * j + c:16 * j + c + 1],
                                                ALU.is_equal, ALU.mult)
                                v.tensor_add(s_wl[:], s_wl[:], s_ctr[:])

                            yield ("v", vc4)
                        for c in range(256):
                            def vc8(v, c=c, j=j):
                                v.tensor_scalar(s_ctr[:], s_idx8[:], float(c),
                                                s_lut8[:, 256 * j + c:256 * j + c + 1],
                                                ALU.is_equal, ALU.mult)
                                v.tensor_add(s_wh[:], s_wh[:], s_ctr[:])

                            yield ("v", vc8)

                        def vdone(v):
                            v.tensor_scalar(s_wh[:], s_wh[:], 0.0, None,
                                            ALU.add).then_inc(sems["sw"], 1)

                        yield ("v", vdone)
                        swt = ctrs["sw"].bump(1)
                        if dbg and j == 0 and hf == 0:
                            def dw(g, swt=swt):
                                g.wait_ge(sems["sw"], swt)
                                g.dma_start(out=dbg_t["wl"].ap(),
                                            in_=s_wl[:, 0:512]).then_inc(sems["st"], 16)

                            yield ("g", dw)
                            ctrs["st"].bump(16)

                        def stw(y, j=j, hf=hf, swt=swt):
                            y.wait_ge(sems["sw"], swt)
                            y.dma_start(out=wsp[128 * j:128 * (j + 1),
                                                HW * hf:HW * (hf + 1)],
                                        in_=s_wl[:]).then_inc(sems["wst"], 16)
                            y.dma_start(out=wsp[OROWS + 128 * j:OROWS + 128 * (j + 1),
                                                HW * hf:HW * (hf + 1)],
                                        in_=s_wh[:]).then_inc(sems["wst"], 16)

                        yield ("y", stw)
                        last_store[0] = ctrs["wst"].bump(32)

            sweep_gen = emit_sweep()
            sweep_live = [True]

            def drain_sweep(n):
                while sweep_live[0] and n > 0:
                    try:
                        eng, fn = next(sweep_gen)
                        p(eng, fn)
                        n -= 1
                    except StopIteration:
                        sweep_live[0] = False

            with ExitStack() as KV:
                s_kT = sb("s_kT", [128, S], BF16, KV)
                s_V = sb("s_V", [128, S], BF16, KV)

                with ExitStack() as P1S:
                    s_x = [sb(f"s_x{i}", [128, D], F32, P1S) for i in range(2)]
                    s_z = sb("s_z", [128, D], BF16, P1S)
                    s_sqscr = sb("s_sqscr", [128, D], F32, P1S)
                    s_xnstage = sb("s_xnstage", [128, D], BF16, P1S)
                    p_tp = [ps(f"p_tp{i}", [128, 1024], BF16, P1S) for i in range(2)]

                    apply_t = {}
                    dbg_gate = [0]
                    buf0_ref = s_x[0]
                    sq_t = {}
                    ev_t = {}
                    ttr_t = {}
                    stx_t = {}

                    for i in range(8):
                        buf = s_x[i % 2]
                        xsem = "x1" if i % 2 == 0 else "x1b"

                        def ldx(y, i=i, buf=buf, xsem=xsem,
                                wv=apply_t.get(i - 2, 0), ws=sq_t.get(i - 2, 0)):
                            y.wait_ge(sems["v1"], wv)
                            y.wait_ge(sems["sc"], ws)
                            y.dma_start(out=buf[:],
                                        in_=d_x[128 * i:128 * (i + 1), :]).then_inc(sems[xsem], 16)

                        p("y", ldx)
                        x1t = ctrs[xsem].bump(16)

                        stat = stat2b[i % 2]
                        sc_b = emit_lnstats(stat, buf, (xsem, x1t),
                                            (dbg_gate[0] if (dbg and i == 1) else 0),
                                            s_sqscr, p7=False)
                        sq_t[i] = sc_b

                        def vapply(v, buf=buf, stat=stat, sc_b=sc_b,
                                   wtp=ttr_t.get(i - 1, 0),
                                   wdbg=(dbg_gate[0] if (dbg and i == 1) else 0)):
                            v.wait_ge(sems["sc"], sc_b)
                            v.wait_ge(sems["tp"], wtp)
                            if wdbg:
                                v.wait_ge(sems["st"], wdbg)
                            v.tensor_scalar(s_z[:], buf[:], stat[:, 1:2], stat[:, 13:14],
                                            ALU.subtract, ALU.mult).then_inc(sems["v1"], 1)

                        p("v", vapply)
                        apply_t[i] = ctrs["v1"].bump(1)

                        if dbg and i == 0:
                            def dz(g, w=apply_t[i], stat=stat):
                                g.wait_ge(sems["v1"], w)
                                g.dma_start(out=dbg_t["z1"].ap(),
                                            in_=s_z[:]).then_inc(sems["st"], 16)
                                g.dma_start(out=dbg_t["x0"].ap(),
                                            in_=buf0_ref[:]).then_inc(sems["st"], 16)
                                g.dma_start(out=dbg_t["st0"].ap(),
                                            in_=stat[:]).then_inc(sems["st"], 16)

                            p("g", dz)
                            dz_t = ctrs["st"].bump(48)
                            dbg_gate[0] = dz_t

                        for gidx in range(4):
                            pb = p_tp[gidx % 2]
                            gi = 4 * i + gidx

                            def ttr(t, gidx=gidx, pb=pb, wv=apply_t[i], i=i,
                                    wev=ev_t.get((i, gidx - 2) if gidx >= 2 else (i - 1, gidx + 2), 0)):
                                if i == 0 and gidx == 0:
                                    t.wait_ge(sems["ld"], 160)
                                t.wait_ge(sems["v1"], wv)
                                t.wait_ge(sems["ev"], wev)
                                ins = None
                                for f in range(8):
                                    ft = 8 * gidx + f
                                    ins = t.transpose(pb[:, 128 * f:128 * (f + 1)],
                                                      s_z[:, 128 * ft:128 * (ft + 1)], s_ident[:])
                                ins.then_inc(sems["tp"], 1)

                            p("t", ttr)
                            tpt = ctrs["tp"].bump(1)
                            ttr_t[i] = tpt

                            def sev(s, gidx=gidx, pb=pb, tpt=tpt, wst=stx_t.get(i - 1, 0)):
                                s.wait_ge(sems["tp"], tpt)
                                if gidx == 0:
                                    s.wait_ge(sems["sx"], wst)
                                s.copy(s_xnstage[:, 1024 * gidx:1024 * (gidx + 1)],
                                       pb[:]).then_inc(sems["ev"], 1)

                            p("s", sev)
                            ev_t[(i, gidx)] = ctrs["ev"].bump(1)

                        def stx(y, i=i, w=ev_t[(i, 3)]):
                            y.wait_ge(sems["ev"], w)
                            src = bass.AP(s_xnstage, 0, [[D, 128], [128, 32], [1, 128]])
                            dst = bass.AP(agx_in, 128 * i,
                                          [[TOK, 128], [128 * TOK, 32], [1, 128]])
                            y.dma_start(out=dst, in_=src).then_inc(sems["sx"], 16)

                        p("y", stx)
                        stx_t[i] = ctrs["sx"].bump(16)

                    emit_lnstats(stat2b[0], s_x[1], ("x1b", ctrs["x1b"].v), 0,
                                 s_sqscr, p7=False)

                    ag1_after = (ctrs["x1"].v, ctrs["x1b"].v, ctrs["sx"].v)

                    def ag1(g, w=ag1_after):
                        g.wait_ge(sems["x1"], w[0])
                        g.wait_ge(sems["x1b"], w[1])
                        g.wait_ge(sems["sx"], w[2])
                        g.collective_compute("AllGather", ALU.bypass, replica_groups=RG,
                                             ins=[agx_in.ap().opt()],
                                             outs=[agx.ap().opt()]).then_inc(sems["ag"], 1)

                    p("g", ag1)
                    ag1_t = ctrs["ag"].bump(1)

                    if dbg:
                        def daxin(g, w=ag1_t):
                            g.wait_ge(sems["ag"], w)
                            g.dma_start(out=dbg_t["axin"].ap(),
                                        in_=agx_in[0:128, 0:512]).then_inc(sems["st"], 16)
                            g.dma_start(out=dbg_t["agx0"].ap(),
                                        in_=agx[0:128, 0:512]).then_inc(sems["st"], 16)

                        p("g", daxin)
                        ctrs["st"].bump(32)

                with ExitStack() as P3S:
                    s_wqkv = sb("s_wqkv", [128, 32 * 6 * 128], BF16, P3S)
                    s_xnr = [sb(f"s_xnr{i}", [128, 512], BF16, P3S) for i in range(4)]
                    s_cs = [sb(f"s_cs{i}", [128, 512], F32, P3S) for i in range(2)]
                    s_sn = [sb(f"s_sn{i}", [128, 512], F32, P3S) for i in range(2)]
                    s_scr = [sb(f"s_scr{i}", [128, 512], F32, P3S) for i in range(2)]
                    s_rot = [sb(f"s_rot{i}", [128, 512], F32, P3S) for i in range(2)]
                    s_qstage = sb("s_qstage", [128, 512], BF16, P3S)
                    s_vtb = sb("s_vtb", [128, 512], BF16, P3S)
                    p_qkv = [ps(f"p_qkv{i}", [128, 512], F32, P3S) for i in range(6)]
                    p_vt = ps("p_vt", [128, 512], BF16, P3S)

                    def wload(g, w=g1_setup, wp1=ag1_after):
                        g.wait_ge(sems["g1"], w)
                        g.wait_ge(sems["x1"], wp1[0])
                        g.wait_ge(sems["x1b"], wp1[1])
                        g.wait_ge(sems["sx"], wp1[2])
                        for ft in range(32):
                            base = ft * 6 * 128
                            g.dma_start(out=s_wqkv[:, base:base + 512],
                                        in_=d_wq[128 * ft:128 * (ft + 1), :]).then_inc(sems["ld"], 16)
                            g.dma_start(out=s_wqkv[:, base + 512:base + 640],
                                        in_=d_wk[128 * ft:128 * (ft + 1), :]).then_inc(sems["ld"], 16)
                            g.dma_start(out=s_wqkv[:, base + 640:base + 768],
                                        in_=d_wv[128 * ft:128 * (ft + 1), :]).then_inc(sems["ld"], 16)

                    p("g", wload)
                    wld_t = ctrs["ld"].bump(96 * 16)

                    scr_war = {}
                    qst_war = [0]
                    cs_war = {}
                    vt_war = [0]
                    evac3_t = {}

                    for tc in range(16):
                        rnk, ch = tc // 2, tc % 2
                        for ft in range(32):
                            ring = s_xnr[ft % 4]
                            gidx_xn = 32 * tc + ft
                            xnsem = ["xn", "xn1", "xn2", "xn3"][ft % 4]

                            def ldxn(y, tc=tc, ft=ft, ring=ring, rnk=rnk, ch=ch,
                                     xnsem=xnsem,
                                     agw_=ag1_t, xw=max(0, 32 * tc + ft - 3)):
                                y.wait_ge(sems["ag"], agw_)
                                y.wait_ge(sems["xnc"], xw)
                                y.dma_start(out=ring[:],
                                            in_=agx[D * rnk + 128 * ft:D * rnk + 128 * (ft + 1),
                                                    512 * ch:512 * (ch + 1)]).then_inc(sems[xnsem], 16)

                            p("y", ldxn)
                            xnt = ctrs[xnsem].bump(16)

                            def mm6(t, tc=tc, ft=ft, ring=ring, xnt=xnt, wld_t=wld_t,
                                    xnsem=xnsem,
                                    wev=evac3_t.get(tc - 1, 0)):
                                t.wait_ge(sems[xnsem], xnt)
                                if ft == 0:
                                    t.wait_ge(sems["ev"], wev)
                                if tc == 0 and ft == 0:
                                    t.wait_ge(sems["ld"], wld_t)
                                base = ft * 6 * 128
                                for ti in range(6):
                                    ins = t.matmul(p_qkv[ti][:],
                                                   s_wqkv[:, base + 128 * ti:base + 128 * (ti + 1)],
                                                   ring[:], start=(ft == 0), stop=(ft == 31))
                                ins.then_inc(sems["xnc"], 1)

                            p("t", mm6)
                            ctrs["xnc"].bump(1)
                        vtt = ctrs["xnc"].v

                        cssem = "cs" if tc % 2 == 0 else "csb"

                        def ldcs(y, tc=tc, w=cs_war.get(tc % 2, 0), wag=ag1_t,
                                 cssem=cssem):
                            y.wait_ge(sems["g1"], w)
                            y.wait_ge(sems["ag"], wag)
                            y.dma_start(out=s_cs[tc % 2][:],
                                        in_=d_cos[:, 512 * tc:512 * (tc + 1)]).then_inc(sems[cssem], 16)
                            y.dma_start(out=s_sn[tc % 2][:],
                                        in_=d_sin[:, 512 * tc:512 * (tc + 1)]).then_inc(sems[cssem], 16)

                        p("y", ldcs)
                        cst = ctrs[cssem].bump(32)

                        for ti in range(5):
                            scr = s_scr[ti % 2]
                            rot = s_rot[ti % 2]

                            def sev2(s, ti=ti, scr=scr, vtt=vtt, w=scr_war.get(ti % 2, 0)):
                                s.wait_ge(sems["xnc"], vtt)
                                s.wait_ge(sems["g1"], w)
                                s.copy(scr[:], p_qkv[ti][:]).then_inc(sems["ev"], 1)

                            p("s", sev2)
                            evt2 = ctrs["ev"].bump(1)

                            rotsem = "rot" if ti % 2 == 0 else "rotb"

                            def rotd(y, scr=scr, rot=rot, evt2=evt2, rotsem=rotsem):
                                y.wait_ge(sems["ev"], evt2)
                                y.dma_start(out=rot[0:64, :], in_=scr[64:128, :]).then_inc(sems[rotsem], 16)
                                y.dma_start(out=rot[64:128, :], in_=scr[0:64, :]).then_inc(sems[rotsem], 16)

                            p("y", rotd)
                            rott = ctrs[rotsem].bump(32)

                            def grope(g, ti=ti, tc=tc, scr=scr, rot=rot, rott=rott,
                                      cst=cst, evt2=evt2, qw=qst_war[0],
                                      rotsem=rotsem, cssem=cssem):
                                g.wait_ge(sems[rotsem], rott)
                                g.wait_ge(sems[cssem], cst)
                                g.wait_ge(sems["ev"], evt2)
                                if ti < 4:
                                    g.wait_ge(sems["qp"], qw)
                                cs_ = s_cs[tc % 2]
                                sn_ = s_sn[tc % 2]
                                g.tensor_mul(scr[:], scr[:], cs_[:])
                                g.tensor_mul(rot[:], rot[:], sn_[:])
                                if ti < 4:
                                    g.tensor_add(s_qstage[:], scr[:], rot[:]).then_inc(sems["g1"], 1)
                                else:
                                    g.tensor_add(s_kT[:, 512 * tc:512 * (tc + 1)],
                                                 scr[:], rot[:]).then_inc(sems["g1"], 1)

                            p("g", grope)
                            g1t = ctrs["g1"].bump(1)
                            scr_war[ti % 2] = g1t
                            cs_war[tc % 2] = g1t

                            if ti < 4:
                                def stq(y, ti=ti, tc=tc, g1t=g1t):
                                    y.wait_ge(sems["g1"], g1t)
                                    y.dma_start(out=qspill[128 * ti:128 * (ti + 1),
                                                           512 * tc:512 * (tc + 1)],
                                                in_=s_qstage[:]).then_inc(sems["qp"], 16)

                                p("y", stq)
                                qst_war[0] = ctrs["qp"].bump(16)
                            if dbg and tc == 0 and ti == 4:
                                def dk(g, g1t=g1t, scr=scr, rot=rot, tc=tc):
                                    g.wait_ge(sems["g1"], g1t)
                                    g.dma_start(out=dbg_t["kT"].ap(),
                                                in_=s_kT[:, 0:512]).then_inc(sems["st"], 16)
                                    g.dma_start(out=dbg_t["scr"].ap(),
                                                in_=scr[:]).then_inc(sems["st"], 16)
                                    g.dma_start(out=dbg_t["rot"].ap(),
                                                in_=rot[:]).then_inc(sems["st"], 16)
                                    g.dma_start(out=dbg_t["cs"].ap(),
                                                in_=s_cs[tc % 2][:]).then_inc(sems["st"], 16)
                                    g.dma_start(out=dbg_t["sn"].ap(),
                                                in_=s_sn[tc % 2][:]).then_inc(sems["st"], 16)

                                p("g", dk)
                                ctrs["st"].bump(16 * 5)

                        def sev3(s, vtt=vtt, w=vt_war[0]):
                            s.wait_ge(sems["xnc"], vtt)
                            s.wait_ge(sems["ev"], w)
                            s.copy(s_vtb[:], p_qkv[5][:]).then_inc(sems["ev"], 1)

                        p("s", sev3)
                        evt3 = ctrs["ev"].bump(1)
                        evac3_t[tc] = evt3

                        def tvt(t, evt3=evt3, wvt=vt_war[0]):
                            t.wait_ge(sems["ev"], evt3)
                            t.wait_ge(sems["ev"], wvt)
                            ins = None
                            for f in range(4):
                                ins = t.transpose(p_vt[:, 128 * f:128 * (f + 1)],
                                                  s_vtb[:, 128 * f:128 * (f + 1)], s_ident[:])
                            ins.then_inc(sems["tp"], 1)

                        p("t", tvt)
                        tpt2 = ctrs["tp"].bump(1)

                        def sev4(s, tpt2=tpt2, tc=tc):
                            s.wait_ge(sems["tp"], tpt2)
                            s.copy(s_V[:, 512 * tc:512 * (tc + 1)],
                                   p_vt[:]).then_inc(sems["ev"], 1)

                        p("s", sev4)
                        vt_war[0] = ctrs["ev"].bump(1)

                        drain_sweep(24)

                    p3_ev = ctrs["ev"].v
                    p3_g1 = ctrs["g1"].v
                    p3_qs = ctrs["qs"].v
                    p3_qp = ctrs["qp"].v

                with ExitStack() as P4S:
                    s_masks = sb("s_masks", [128, 4 * 512], BF16, P4S)
                    s_qt = [sb(f"s_qt{i}", [128, 1024], BF16, P4S) for i in range(2)]
                    s_es = [sb(f"s_es{i}", [128, 1024], BF16, P4S) for i in range(2)]
                    s_ot = sb("s_ot", [128, 1024], BF16, P4S)
                    s_rb = sb("s_rb", [128, 512], F32, P4S)
                    s_dbgk = sb("s_dbgk", [128, 512], BF16, P4S) if dbg else None
                    p_S = ps("p_S", [128, 1024], F32, P4S)
                    p_oT = [ps(f"p_oT{i}", [128, 1024], F32, P4S) for i in range(2)]
                    p_den = ps("p_den", [128, 512], F32, P4S)
                    p_bc = ps("p_bc", [128, 512], F32, P4S)

                    def ldmask(y, wev=p3_ev, wg=p3_g1, wq=p3_qs, wqp=p3_qp):
                        y.wait_ge(sems["ev"], wev)
                        y.wait_ge(sems["g1"], wg)
                        y.wait_ge(sems["qs"], wq)
                        y.wait_ge(sems["qp"], wqp)
                        src = bass.AP(d_masks, 0, [[512, 128], [128 * 512, 4], [1, 512]])
                        dst = bass.AP(s_masks, 0, [[4 * 512, 128], [512, 4], [1, 512]])
                        y.dma_start(out=dst, in_=src).then_inc(sems["ld"], 16)

                    p("y", ldmask)
                    mask_t = ctrs["ld"].bump(16)

                    if dbg:
                        def dkcap(s, w=p3_g1):
                            s.wait_ge(sems["g1"], w)
                            s.copy(s_dbgk[:], s_kT[:, 0:512]).then_inc(sems["sc"], 1)

                        p("s", dkcap)
                        dk_t = ctrs["sc"].bump(1)

                        def dkd(g, w=dk_t):
                            g.wait_ge(sems["sc"], w)
                            g.dma_start(out=dbg_t["kTlate"].ap(),
                                        in_=s_dbgk[:]).then_inc(sems["st"], 16)

                        p("g", dkd)
                        ctrs["st"].bump(16)

                    q_war = {0: 0, 1: 0}
                    es_war = {0: 0, 1: 0}
                    nrm_war = {0: 0, 1: 0}
                    rec_war = [0]
                    bc_war = [0]
                    ot_war = [0]
                    rb_war = [0]

                    for qc in range(16):
                        ntile = 4 * qc + 4
                        for pr in range(2):
                            qb = s_qt[pr]

                            q4sem = "q4" if pr == 0 else "q4b"

                            def ldq(y, qc=qc, pr=pr, qb=qb, p3_qp=p3_qp, w=q_war[pr],
                                    wev=p3_ev, q4sem=q4sem):
                                y.wait_ge(sems["qp"], p3_qp)
                                y.wait_ge(sems["ev"], wev)
                                y.wait_ge(sems["av"], w)
                                for hl in range(2):
                                    h = 2 * pr + hl
                                    y.dma_start(out=qb[:, 512 * hl:512 * (hl + 1)],
                                                in_=qspill[128 * h:128 * (h + 1),
                                                           512 * qc:512 * (qc + 1)]).then_inc(sems[q4sem], 16)

                            p("y", ldq)
                            q4t = ctrs[q4sem].bump(32)

                            pend_av = [None]
                            pend_par = [0]
                            for t_i in range(ntile):
                                es = s_es[t_i % 2]
                                diag = t_i >= 4 * qc
                                di = t_i - 4 * qc

                                def smm(t, t_i=t_i, qb=qb, q4t=q4t, qc=qc, pr=pr,
                                        p3_ev=p3_ev, p3_g1=p3_g1, expw=ctrs["exp"].v,
                                        q4sem=q4sem, diag=diag, di=di, mask_t=mask_t):
                                    if t_i == 0:
                                        t.wait_ge(sems[q4sem], q4t)
                                        if qc == 0 and pr == 0:
                                            t.wait_ge(sems["g1"], p3_g1)
                                            t.wait_ge(sems["ev"], p3_ev)
                                    if diag:
                                        t.wait_ge(sems["ld"], mask_t)
                                    t.wait_ge(sems["exp"], expw)
                                    ins = None
                                    for hl in range(2):
                                        ins = t.matmul(p_S[:, 512 * hl:512 * (hl + 1)],
                                                       s_kT[:, 128 * t_i:128 * (t_i + 1)],
                                                       qb[:, 512 * hl:512 * (hl + 1)],
                                                       start=True, stop=not diag)
                                        if diag:
                                            ins = t.matmul(p_S[:, 512 * hl:512 * (hl + 1)],
                                                           s_ident[:],
                                                           s_masks[:, 512 * di:512 * (di + 1)],
                                                           start=False, stop=True)
                                    ins.then_inc(sems["av"], 1)

                                p("t", smm)
                                a1 = ctrs["av"].bump(1)
                                q_war[pr] = a1

                                def sexp(s, a1=a1, es=es, esw=es_war[t_i % 2]):
                                    s.wait_ge(sems["av"], a1)
                                    s.wait_ge(sems["av"], esw)
                                    s.activation(es[:], p_S[:], AF.Exp,
                                                 scale=SCALE).then_inc(sems["exp"], 1)

                                p("s", sexp)
                                expt = ctrs["exp"].bump(1)

                                def avmm(t, t_i=t_i, es=es, expt=expt, pr=pr, qc=qc,
                                         wn=nrm_war[pr], wr=rec_war[0]):
                                    t.wait_ge(sems["exp"], expt)
                                    if t_i == 0:
                                        t.wait_ge(sems["nrm"], wn)
                                        t.wait_ge(sems["rec"], wr)
                                    first = t_i == 0
                                    last = t_i == 4 * qc + 3
                                    ins = None
                                    for hl in range(2):
                                        base = 32 * hl
                                        t.matmul(p_oT[pr][:, 512 * hl:512 * (hl + 1)],
                                                 s_V[:, 128 * t_i:128 * (t_i + 1)],
                                                 es[:, 512 * hl:512 * (hl + 1)],
                                                 start=first, stop=last)
                                        ins = t.matmul(p_den[base:base + 1, :],
                                                       s_ones_col[:],
                                                       es[:, 512 * hl:512 * (hl + 1)],
                                                       start=first, stop=last,
                                                       tile_position=(0, base))
                                    ins.then_inc(sems["av"], 1)

                                if pend_av[0] is not None:
                                    p("t", pend_av[0])
                                    es_war[pend_par[0]] = ctrs["av"].bump(1)
                                pend_av[0] = avmm
                                pend_par[0] = t_i % 2

                            p("t", pend_av[0])
                            es_war[pend_par[0]] = ctrs["av"].bump(1)
                            pend_av[0] = None

                            av_end = ctrs["av"].v
                            if dbg and qc == 9 and pr == 1:
                                def dexp9(g, av_end=av_end, par=(4 * 9 + 4 - 1) % 2):
                                    g.wait_ge(sems["av"], av_end)
                                    g.dma_start(out=dbg_t["es9"].ap(),
                                                in_=s_es[par][:]).then_inc(sems["st"], 16)

                                p("g", dexp9)
                                ctrs["st"].bump(16)
                            if dbg and qc == 1 and pr == 0:
                                def dexp(g, av_end=av_end, par=(4 * 1 + 4 - 1) % 2):
                                    g.wait_ge(sems["av"], av_end)
                                    g.dma_start(out=dbg_t["expS"].ap(),
                                                in_=s_es[par][:]).then_inc(sems["st"], 16)

                                p("g", dexp)
                                ctrs["st"].bump(16)

                            nrm0 = ctrs["nrm"].v
                            for hl in range(2):
                                def srec(s, hl=hl, av_end=av_end):
                                    s.wait_ge(sems["av"], av_end)
                                    base = 32 * hl
                                    s.activation(s_recip[hl][:],
                                                 p_den[base:base + 1, :], AF.Ln)
                                    s.activation(s_recip[hl][:],
                                                 s_recip[hl][:], AF.Exp,
                                                 scale=-1.0).then_inc(sems["rec"], 1)

                                p("s", srec)
                                rect = ctrs["rec"].bump(1)
                                rec_war[0] = rect
                                if dbg and qc == 1 and pr == 0 and hl == 0:
                                    def dden(g, rect=rect):
                                        g.wait_ge(sems["rec"], rect)
                                        g.dma_start(out=dbg_t["den"].ap(),
                                                    in_=s_recip[0][:]).then_inc(sems["st"], 16)

                                    p("g", dden)
                                    ctrs["st"].bump(16)

                                def tbc(t, hl=hl, rect=rect, w=bc_war[0]):
                                    t.wait_ge(sems["rec"], rect)
                                    t.wait_ge(sems["rec"], w)
                                    t.matmul(p_bc[:], s_ones_row[:],
                                             s_recip[hl][:], start=True,
                                             stop=True).then_inc(sems["bc"], 1)

                                p("t", tbc)
                                bct = ctrs["bc"].bump(1)

                                def sbcc(s, bct=bct, wrb=rb_war[0]):
                                    s.wait_ge(sems["bc"], bct)
                                    s.wait_ge(sems["nrm"], wrb)
                                    s.copy(s_rb[:], p_bc[:]).then_inc(sems["rec"], 1)

                                p("s", sbcc)
                                rbt = ctrs["rec"].bump(1)
                                bc_war[0] = rbt

                                def vnrm(v, hl=hl, pr=pr, rbt=rbt, w=ot_war[0]):
                                    v.wait_ge(sems["rec"], rbt)
                                    if hl == 0:
                                        v.wait_ge(sems["ao"], w)
                                    v.tensor_mul(s_ot[:, 512 * hl:512 * (hl + 1)],
                                                 p_oT[pr][:, 512 * hl:512 * (hl + 1)],
                                                 s_rb[:]).then_inc(sems["nrm"], 1)

                                p("v", vnrm)
                                rb_war[0] = ctrs["nrm"].bump(1)
                            nrm_war[pr] = ctrs["nrm"].v

                            nrmt = ctrs["nrm"].v

                            def sto(y, qc=qc, pr=pr, nrmt=nrmt):
                                y.wait_ge(sems["nrm"], nrmt)
                                for hl in range(2):
                                    h = 2 * pr + hl
                                    y.dma_start(out=ago_in[128 * h:128 * (h + 1),
                                                           512 * qc:512 * (qc + 1)],
                                                in_=s_ot[:, 512 * hl:512 * (hl + 1)]).then_inc(sems["ao"], 16)

                            p("y", sto)
                            ot_war[0] = ctrs["ao"].bump(32)
                            if dbg and qc == 9 and pr == 1:
                                def dot9(g, nrmt=nrmt):
                                    g.wait_ge(sems["nrm"], nrmt)
                                    g.dma_start(out=dbg_t["oT9"].ap(),
                                                in_=s_ot[:]).then_inc(sems["st"], 16)
                                    g.dma_start(out=dbg_t["den9"][0:1, :],
                                                in_=s_recip[0][:]).then_inc(sems["st"], 16)
                                    g.dma_start(out=dbg_t["den9"][1:2, :],
                                                in_=s_recip[1][:]).then_inc(sems["st"], 16)

                                p("g", dot9)
                                ctrs["st"].bump(48)
                            if dbg and qc == 0 and pr == 0:
                                def dot(g, nrmt=nrmt):
                                    g.wait_ge(sems["nrm"], nrmt)
                                    g.dma_start(out=dbg_t["oT"].ap(),
                                                in_=s_ot[:, 0:512]).then_inc(sems["st"], 16)

                                p("g", dot)
                                ctrs["st"].bump(16)

                            drain_sweep(56)

                    ao_all = ctrs["ao"].v

                    def ag2(g, w=ao_all):
                        g.wait_ge(sems["ao"], w)
                        g.collective_compute("AllGather", ALU.bypass, replica_groups=RG,
                                             ins=[ago_in.ap().opt()],
                                             outs=[ago.ap().opt()]).then_inc(sems["ag"], 1)

                    p("g", ag2)
                    ag2_t = ctrs["ag"].bump(1)

            with ExitStack() as P6S:
                s_woH = [sb(f"s_wo{i}", [128, 16 * 512], BF16, P6S) for i in range(2)]
                s_xc = sb("s_xc", [128, 512], F32, P6S)
                s_x2st = sb("s_x2st", [128, 512], F32, P6S)
                s_do2 = sb("s_do2", [128, 512], F32, P6S) if dbg else None
                AG6S = ExitStack()
                s_ag6 = sb("s_ag6", [128, 32 * 1024], BF16, AG6S)
                p_o2 = [ps(f"p_o2{i}", [128, 512], F32, P6S) for i in range(2)]
                p_wt = [ps(f"p_wt{i}", [128, 1024], BF16, P6S) for i in range(2)]

                def ldago(g, w=ag2_t):
                    g.wait_ge(sems["ag"], w)
                    pid = nc.partition_id([mybir.EngineType.Pool])
                    with g.register("agoff0") as r0:
                        g.reg_alu(r0, pid, TOK, ALU.mult)
                        with g.register("agoff") as r:
                            for ch in range(8):
                                g.reg_alu(r, r0, 512 * ch * S, ALU.add)
                                g.dma_start(
                                    out=bass.AP(agom, 512 * ch * TOK, [[TOK, 512], [1, TOK]]),
                                    in_=bass.AP(ago, r, [[S, 512], [1, TOK]]),
                                ).then_inc(sems["ao"], 16)
                    g.wait_ge(sems["ao"], stage_done)
                    for ct in range(32):
                        row0 = 512 * (ct // 4) + 128 * (ct % 4)
                        g.dma_start(out=s_ag6[:, 1024 * ct:1024 * (ct + 1)],
                                    in_=agom[row0:row0 + 128, :]).then_inc(sems["ao"], 16)

                stage_done = ctrs["ao"].v + 8 * 16
                p("g", ldago)
                ago_ld = ctrs["ao"].bump(40 * 16)

                if dbg:
                    def dag6(g, w=ago_ld):
                        g.wait_ge(sems["ao"], w)
                        g.dma_start(out=dbg_t["ag6"].ap(),
                                    in_=s_ag6[:, 0:1024]).then_inc(sems["st"], 16)

                    p("g", dag6)
                    ctrs["st"].bump(16)

                woh_war = {}
                ep6_war = {}
                xc_war = [0]
                x2st_war = [0]

                for F in range(8):
                    woth = {}
                    for h in range(2):
                        wsem = "wo" if h == 0 else "wo2"

                        def ldwoH(g, F=F, h=h, wsem=wsem,
                                  w=woh_war.get((F - 1, h), 0), wag=ag2_t):
                            g.wait_ge(sems["ag"], wag)
                            g.wait_ge(sems["mm6"], w)
                            src = bass.AP(d_wo, 512 * F + 16 * h * 128 * D,
                                          [[D, 128], [128 * D, 16], [1, 512]])
                            dst = bass.AP(s_woH[h], 0,
                                          [[16 * 512, 128], [512, 16], [1, 512]])
                            g.dma_start(out=dst, in_=src).then_inc(sems[wsem], 16)

                        p("g", ldwoH)
                        woth[h] = ctrs[wsem].bump(16)

                    for TC in range(8):
                        pb = p_o2[TC % 2]
                        g6 = 8 * F + TC

                        def mmwoA(t, F=F, TC=TC, pb=pb, wot=woth[0], ago_ld=ago_ld,
                                  wep=max(0, 8 * F + TC - 1)):
                            t.wait_ge(sems["wo"], wot)
                            if F == 0 and TC == 0:
                                t.wait_ge(sems["ao"], ago_ld)
                            t.wait_ge(sems["ep6"], wep)
                            for ct in range(16):
                                ins = t.matmul(pb[:],
                                               s_ag6[:, 1024 * ct + 128 * TC:1024 * ct + 128 * (TC + 1)],
                                               s_woH[0][:, 512 * ct:512 * (ct + 1)],
                                               start=(ct == 0), stop=False)
                            ins.then_inc(sems["mm6"], 1)

                        p("t", mmwoA)
                        mm6a = ctrs["mm6"].bump(1)

                        def mmwoB(t, F=F, TC=TC, pb=pb, wot=woth[1]):
                            t.wait_ge(sems["wo2"], wot)
                            for ct in range(16):
                                ins = t.matmul(pb[:],
                                               s_ag6[:, 1024 * (16 + ct) + 128 * TC:1024 * (16 + ct) + 128 * (TC + 1)],
                                               s_woH[1][:, 512 * ct:512 * (ct + 1)],
                                               start=False, stop=(ct == 15))
                            ins.then_inc(sems["mm6"], 1)

                        p("t", mmwoB)
                        mm6t = ctrs["mm6"].bump(1)
                        if TC == 7:
                            woh_war[(F, 0)] = mm6a
                            woh_war[(F, 1)] = mm6t

                        if False and dbg and F == 0 and TC == 0:
                            def do2(s, pb=pb, w=mm6t):
                                s.wait_ge(sems["mm6"], w)
                                s.copy(s_do2[:], pb[:]).then_inc(sems["sc"], 1)

                            p("s", do2)
                            do2t = ctrs["sc"].bump(1)

                            def do2d(g, w=do2t):
                                g.wait_ge(sems["sc"], w)
                                g.dma_start(out=dbg_t["o2"].ap(),
                                            in_=s_do2[:]).then_inc(sems["st"], 16)

                            p("g", do2d)
                            ctrs["st"].bump(16)

                        def ldxc(y, F=F, TC=TC, w=g6):
                            y.wait_ge(sems["ep6"], w)
                            y.dma_start(out=s_xc[:],
                                        in_=bass.AP(d_x, 128 * TC * D + 512 * F,
                                                    [[D, 128], [1, 512]])).then_inc(sems["x2"], 16)

                        p("y", ldxc)
                        xct = ctrs["x2"].bump(16)

                        def vep6(v, pb=pb, mm6t=mm6t, xct=xct, w=x2st_war[0]):
                            v.wait_ge(sems["mm6"], mm6t)
                            v.wait_ge(sems["x2"], xct)
                            v.wait_ge(sems["x2s"], w)
                            v.tensor_add(s_x2st[:], pb[:], s_xc[:]).then_inc(sems["ep6"], 1)

                        p("v", vep6)
                        ep6t = ctrs["ep6"].bump(1)

                        def stx2(y, F=F, TC=TC, ep6t=ep6t):
                            y.wait_ge(sems["ep6"], ep6t)
                            y.dma_start(out=bass.AP(x2sp, 128 * TC * D + 512 * F,
                                                    [[D, 128], [1, 512]]),
                                        in_=s_x2st[:]).then_inc(sems["x2s"], 16)

                        p("y", stx2)
                        x2st_war[0] = ctrs["x2s"].bump(16)

                        drain_sweep(14)

                drain_sweep(10 ** 9)
                sw_all = ctrs["sw"].v
                wst_all = ctrs["wst"].v
                mm6_all = ctrs["mm6"].v
                AG6S.close()
                s_wts = sb("s_wts", [128, D], BF16, P6S)
                s_wtst = sb("s_wtst", [128, D], BF16, P6S)

                wts_war = [0]
                tp6_war = [0]
                ev6_war = {}
                for wi in range(2):
                    for j in range(4):
                        def ldwt(y, wi=wi, j=j, w=wst_all, wtp=tp6_war[0], wag=ag2_t,
                                 wmm=mm6_all):
                            y.wait_ge(sems["ag"], wag)
                            y.wait_ge(sems["wst"], w)
                            y.wait_ge(sems["tp"], wtp)
                            y.wait_ge(sems["mm6"], wmm)
                            y.dma_start(out=s_wts[:],
                                        in_=wsp[OROWS * wi + 128 * j:OROWS * wi + 128 * (j + 1),
                                                :]).then_inc(sems["wt6"], 16)

                        p("y", ldwt)
                        wtt = ctrs["wt6"].bump(16)

                        for gidx in range(4):
                            pbw = p_wt[gidx % 2]
                            key = (wi, j, gidx)
                            prev_key = (wi, j, gidx - 2) if gidx >= 2 else \
                                ((wi, j - 1, gidx + 2) if j >= 1 else
                                 ((wi - 1, 3, gidx + 2) if wi >= 1 else None))

                            def twt(t, gidx=gidx, pbw=pbw, wtt=wtt,
                                    wev=ev6_war.get(prev_key, 0)):
                                t.wait_ge(sems["wt6"], wtt)
                                t.wait_ge(sems["ev"], wev)
                                ins = None
                                for f in range(8):
                                    ft = 8 * gidx + f
                                    ins = t.transpose(pbw[:, 128 * f:128 * (f + 1)],
                                                      s_wts[:, 128 * ft:128 * (ft + 1)], s_ident[:])
                                ins.then_inc(sems["tp"], 1)

                            p("t", twt)
                            tptw = ctrs["tp"].bump(1)
                            tp6_war[0] = tptw

                            def swev(s, gidx=gidx, pbw=pbw, tptw=tptw, w=wts_war[0]):
                                s.wait_ge(sems["tp"], tptw)
                                if gidx == 0:
                                    s.wait_ge(sems["wst"], w)
                                s.copy(s_wtst[:, 1024 * gidx:1024 * (gidx + 1)],
                                       pbw[:]).then_inc(sems["ev"], 1)

                            p("s", swev)
                            ev6_war[key] = ctrs["ev"].bump(1)

                        def stwT(y, wi=wi, j=j, w=ev6_war[(wi, j, 3)]):
                            y.wait_ge(sems["ev"], w)
                            src = bass.AP(s_wtst, 0, [[D, 128], [128, 32], [1, 128]])
                            dst = bass.AP(agw_in, OROWS * wi + 128 * j,
                                          [[2 * OROWS, 128], [128 * 2 * OROWS, 32], [1, 128]])
                            y.dma_start(out=dst, in_=src).then_inc(sems["wst"], 16)

                        p("y", stwT)
                        wts_war[0] = ctrs["wst"].bump(16)

                agw_done = ctrs["wst"].v

                def ag3(g, w=agw_done):
                    g.wait_ge(sems["wst"], w)
                    g.collective_compute("AllGather", ALU.bypass, replica_groups=RG,
                                         ins=[agw_in.ap().opt()],
                                         outs=[agw.ap().opt()]).then_inc(sems["ag"], 1)

                p("g", ag3)
                ag3_t = ctrs["ag"].bump(1)
                p6_x2s = ctrs["x2s"].v

                if dbg:
                    def dx2sp(g, w=p6_x2s):
                        g.wait_ge(sems["x2s"], w)
                        g.dma_start(out=dbg_t["x2sp"].ap(),
                                    in_=x2sp[0:128, :]).then_inc(sems["st"], 16)
                        g.dma_start(out=dbg_t["x2sp7a"].ap(),
                                    in_=x2sp[896:1024, :]).then_inc(sems["st"], 16)

                    p("g", dx2sp)
                    ctrs["st"].bump(32)

        with ExitStack() as X2N:
            s_x2nT = sb("s_x2nT", [128, 32 * TOK], BF16, X2N)
            x2nT_ref = s_x2nT
            with ExitStack() as P7S:
                s_x2r = [sb(f"s_x2r{i}", [128, D], F32, P7S) for i in range(2)]
                s_z2 = sb("s_z2", [128, D], BF16, P7S)
                s_sq2 = sb("s_sq2", [128, D], F32, P7S)
                p_t7 = [ps(f"p_t7{i}", [128, 1024], BF16, P7S) for i in range(2)]

                v7_done = {}
                s7_done = {}
                ev7 = {}
                t7_t = {}
                for TC in range(8):
                    buf = s_x2r[TC % 2]
                    x7sem = "x7" if TC % 2 == 0 else "x7b"

                    def ldx2(y, TC=TC, buf=buf, w=p6_x2s, wagw=agw_done, x7sem=x7sem,
                             wv=v7_done.get(TC - 2, 0), ws=s7_done.get(TC - 2, 0)):
                        y.wait_ge(sems["x2s"], w)
                        y.wait_ge(sems["wst"], wagw)
                        y.wait_ge(sems["v1"], wv)
                        y.wait_ge(sems["sc"], ws)
                        y.dma_start(out=buf[:],
                                    in_=x2sp[128 * TC:128 * (TC + 1), :]).then_inc(sems[x7sem], 16)

                    p("y", ldx2)
                    x2t = ctrs[x7sem].bump(16)

                    stat = stat2b[TC % 2]
                    s7rt = emit_lnstats(stat, buf, (x7sem, x2t), 0, s_sq2, p7=True)
                    s7_done[TC] = s7rt

                    def v7c(v, buf=buf, stat=stat, s7rt=s7rt, TC=TC,
                            wtp=t7_t.get(TC - 1, 0)):
                        v.wait_ge(sems["sc"], s7rt)
                        v.wait_ge(sems["tp"], wtp)
                        v.tensor_scalar(s_m[:, TC:TC + 1], stat[:, 16:17], 4096.0, None, ALU.is_gt)
                        v.tensor_scalar(s_m1[:, TC:TC + 1], stat[:, 16:17], 4096.0, None, ALU.is_le)
                        v.tensor_scalar(s_z2[:], buf[:], stat[:, 1:2], stat[:, 13:14],
                                        ALU.subtract, ALU.mult).then_inc(sems["v1"], 1)

                    p("v", v7c)
                    v7_done[TC] = ctrs["v1"].bump(1)
                    if dbg and TC == 0:
                        def dx2(g, w=v7_done[TC], buf=buf):
                            g.wait_ge(sems["v1"], w)
                            g.dma_start(out=dbg_t["x2"].ap(), in_=buf[:]).then_inc(sems["st"], 16)
                            g.dma_start(out=dbg_t["mcol"].ap(), in_=s_m[:]).then_inc(sems["st"], 16)

                        p("g", dx2)
                        ctrs["st"].bump(32)

                    for gidx in range(4):
                        pb7 = p_t7[gidx % 2]
                        key7 = (TC, gidx)
                        prev7 = (TC, gidx - 2) if gidx >= 2 else (TC - 1, gidx + 2)

                        def t7(t, gidx=gidx, pb7=pb7, w=v7_done[TC],
                               wev=ev7.get(prev7, 0)):
                            t.wait_ge(sems["v1"], w)
                            t.wait_ge(sems["ev"], wev)
                            ins = None
                            for f in range(8):
                                ft = 8 * gidx + f
                                ins = t.transpose(pb7[:, 128 * f:128 * (f + 1)],
                                                  s_z2[:, 128 * ft:128 * (ft + 1)], s_ident[:])
                            ins.then_inc(sems["tp"], 1)

                        p("t", t7)
                        t7t = ctrs["tp"].bump(1)
                        t7_t[TC] = t7t

                        def s7e(s, gidx=gidx, pb7=pb7, t7t=t7t, TC=TC):
                            s.wait_ge(sems["tp"], t7t)
                            ins = None
                            for f in range(8):
                                ft = 8 * gidx + f
                                ins = s.copy(s_x2nT[:, TOK * ft + 128 * TC:TOK * ft + 128 * (TC + 1)],
                                             pb7[:, 128 * f:128 * (f + 1)])
                            ins.then_inc(sems["ev"], 1)

                        p("s", s7e)
                        ev7[key7] = ctrs["ev"].bump(1)

                emit_lnstats(stat2b[0], s_x2r[1], ("x7b", ctrs["x7b"].v), 0,
                             s_sq2, p7=True)

            p7_ev = ctrs["ev"].v
            p7_v1 = ctrs["v1"].v

            if dbg:
                def dx2nT(g, w=p7_ev):
                    g.wait_ge(sems["ev"], w)
                    g.dma_start(out=dbg_t["x2nT"].ap(),
                                in_=s_x2nT[:, 0:1024]).then_inc(sems["st"], 16)

                p("g", dx2nT)
                ctrs["st"].bump(16)

            with ExitStack() as P9S:
                s_wbl = sb("s_wbl", [128, 32 * 512], BF16, P9S)
                s_wbh = sb("s_wbh", [128, 32 * 512], BF16, P9S)
                s_ylev = sb("s_ylev", [128, 512], F32, P9S)
                s_t1 = sb("s_t1", [128, 512], F32, P9S)
                s_t2 = sb("s_t2", [128, 512], F32, P9S)
                s_t3 = sb("s_t3", [128, 512], F32, P9S)
                s_t4 = sb("s_t4", [128, 512], F32, P9S)
                s_x2c = sb("s_x2c", [128, 512], F32, P9S)
                s_bb = sb("s_bb", [128, 512], F32, P9S)
                s_ost = sb("s_ost", [128, 512], F32, P9S)
                p_y = [ps(f"p_y{i}", [128, 1024], F32, P9S) for i in range(2)]
                p_bb = ps("p_bb", [128, 512], F32, P9S)

                ff_rank_done = {}
                fe_war = {}
                ye_war = [0]
                ye_war2 = [0]
                ge_war = [0]
                bb_war = [0]
                x2c_war = [0]
                ost_war = [0]

                for r in range(NCORES):
                    def ldw(g, r=r, w=ag3_t, wff=ff_rank_done.get(r - 1, 0),
                            wev=p7_ev):
                        g.wait_ge(sems["ag"], w)
                        g.wait_ge(sems["ev"], wev)
                        g.wait_ge(sems["ff"], wff)
                        srcl = bass.AP(agw, (D * r) * (2 * OROWS),
                                       [[2 * OROWS, 128], [128 * 2 * OROWS, 32], [1, 512]])
                        srch = bass.AP(agw, (D * r) * (2 * OROWS) + OROWS,
                                       [[2 * OROWS, 128], [128 * 2 * OROWS, 32], [1, 512]])
                        dstl = bass.AP(s_wbl, 0, [[32 * 512, 128], [512, 32], [1, 512]])
                        dsth = bass.AP(s_wbh, 0, [[32 * 512, 128], [512, 32], [1, 512]])
                        g.dma_start(out=dstl, in_=srcl).then_inc(sems["wo"], 16)
                        g.dma_start(out=dsth, in_=srch).then_inc(sems["wo"], 16)

                    p("g", ldw)
                    wt9 = ctrs["wo"].bump(32)

                    def tb9(t, r=r, wld=ld_setup, w=bb_war[0], wev=p7_ev):
                        t.wait_ge(sems["ld"], wld)
                        t.wait_ge(sems["ev"], wev)
                        t.wait_ge(sems["fe"], w)
                        t.matmul(p_bb[:], s_ones_row[:],
                                 s_bias[0:1, 512 * r:512 * (r + 1)],
                                 start=True, stop=True).then_inc(sems["bc"], 1)

                    p("t", tb9)
                    bct9 = ctrs["bc"].bump(1)

                    def sb9(s, bct9=bct9):
                        s.wait_ge(sems["bc"], bct9)
                        s.copy(s_bb[:], p_bb[:]).then_inc(sems["rec"], 1)

                    p("s", sb9)
                    sb9t = ctrs["rec"].bump(1)

                    for TCi, TC in enumerate(range(8)):
                        py = p_y[TC % 2]
                        g9 = 8 * r + TCi

                        def mf(t, r=r, TC=TC, py=py, wt9=wt9, p7_ev=p7_ev,
                               wfe=fe_war.get(g9 - 2, 0)):
                            t.wait_ge(sems["wo"], wt9)
                            if r == 0 and TC == 0:
                                t.wait_ge(sems["ev"], p7_ev)
                            t.wait_ge(sems["fe"], wfe)
                            for it in range(32):
                                xt = s_x2nT[:, TOK * it + 128 * TC:TOK * it + 128 * (TC + 1)]
                                t.matmul(py[:, 0:512], xt,
                                         s_wbl[:, 512 * it:512 * (it + 1)],
                                         start=(it == 0), stop=(it == 31))
                                ins = t.matmul(py[:, 512:1024], xt,
                                               s_wbh[:, 512 * it:512 * (it + 1)],
                                               start=(it == 0), stop=(it == 31))
                            ins.then_inc(sems["ff"], 1)

                        p("t", mf)
                        fft = ctrs["ff"].bump(1)
                        if TC == 7:
                            ff_rank_done[r] = fft

                        def ldx2c(y, r=r, TC=TC, w=p6_x2s, wfe=x2c_war[0]):
                            y.wait_ge(sems["x2s"], w)
                            y.wait_ge(sems["fe"], wfe)
                            y.dma_start(out=s_x2c[:],
                                        in_=bass.AP(x2sp, 128 * TC * D + 512 * r,
                                                    [[D, 128], [1, 512]])).then_inc(sems["x2"], 16)

                        p("y", ldx2c)
                        x2ct = ctrs["x2"].bump(16)

                        def sy9b(s, fft=fft, py=py, w=ye_war2[0], TC=TC, wv=p7_v1):
                            s.wait_ge(sems["ff"], fft)
                            s.wait_ge(sems["v1"], w)
                            s.wait_ge(sems["v1"], wv)
                            s.copy(s_ylev[:], py[:, 0:512])
                            s.mul(s_t1[:], s_ylev[:], s_m1[:, TC:TC + 1])
                            s.mul(s_t2[:], py[:, 512:1024],
                                  s_m[:, TC:TC + 1]).then_inc(sems["sc"], 1)

                        p("s", sy9b)
                        syt = ctrs["sc"].bump(1)

                        def ve9(v, syt=syt, x2ct=x2ct, sb9t=sb9t, wge=ge_war[0]):
                            v.wait_ge(sems["sc"], syt)
                            v.wait_ge(sems["x2"], x2ct)
                            v.wait_ge(sems["rec"], sb9t)
                            v.wait_ge(sems["fe"], wge)
                            v.tensor_add(s_t3[:], s_t1[:], s_t2[:])
                            v.tensor_add(s_t4[:], s_bb[:], s_x2c[:]).then_inc(sems["v1"], 1)

                        p("v", ve9)
                        vet9 = ctrs["v1"].bump(1)
                        ye_war2[0] = vet9

                        def ge9(g, vet9=vet9, wost=ost_war[0]):
                            g.wait_ge(sems["v1"], vet9)
                            g.wait_ge(sems["st"], wost)
                            g.tensor_add(s_ost[:], s_t3[:], s_t4[:]).then_inc(sems["fe"], 1)

                        p("g", ge9)
                        fet = ctrs["fe"].bump(1)
                        ge_war[0] = fet
                        fe_war[g9] = fet
                        ye_war[0] = ctrs["sc"].v
                        x2c_war[0] = fet
                        bb_war[0] = fet if TC == 7 else bb_war[0]

                        def sto9(y, r=r, TC=TC, fet=fet):
                            y.wait_ge(sems["fe"], fet)
                            y.dma_start(out=bass.AP(d_out, 128 * TC * D + 512 * r,
                                                    [[D, 128], [1, 512]]),
                                        in_=s_ost[:]).then_inc(sems["st"], 16)

                        p("y", sto9)
                        ost_war[0] = ctrs["st"].bump(16)

        if dbg:
            def dfin(g, w=ctrs["fe"].v):
                g.wait_ge(sems["fe"], w)
                g.dma_start(out=dbg_t["mfin"].ap(), in_=s_m[:]).then_inc(sems["st"], 16)
                g.dma_start(out=dbg_t["m1fin"].ap(), in_=s_m1[:]).then_inc(sems["st"], 16)
                g.dma_start(out=dbg_t["xnt7"].ap(),
                            in_=x2nT_ref[:, 0:1024]).then_inc(sems["st"], 16)
                g.dma_start(out=dbg_t["x2sp7b"].ap(),
                            in_=x2sp[896:1024, :]).then_inc(sems["st"], 16)
                g.dma_start(out=dbg_t["xntall"].ap(),
                            in_=x2nT_ref[:]).then_inc(sems["st"], 16)
                g.dma_start(out=dbg_t["agw0"].ap(),
                            in_=agw[0:128, 0:512]).then_inc(sems["st"], 16)
                g.dma_start(out=dbg_t["x2all"].ap(),
                            in_=x2sp.ap()).then_inc(sems["st"], 16)
                g.dma_start(out=dbg_t["stB"].ap(),
                            in_=stat2b[1][:]).then_inc(sems["st"], 16)
                g.dma_start(out=dbg_t["ago9"].ap(),
                            in_=ago[0:4096, 4608:5120]).then_inc(sems["st"], 16)

            p("g", dfin)
            ctrs["st"].bump(144)

        st_all = ctrs["st"].v

        def fin(y, w=st_all):
            y.wait_ge(sems["st"], w)

        p("y", fin)

        with nc.Block() as block:
            @block.gpsimd
            def _(g):
                for fn in ops["g"]:
                    fn(g)

            @block.tensor
            def _(t):
                for fn in ops["t"]:
                    fn(t)

            @block.vector
            def _(v):
                for fn in ops["v"]:
                    fn(v)

            @block.scalar
            def _(s):
                for fn in ops["s"]:
                    fn(s)

            @block.sync
            def _(y):
                for fn in ops["y"]:
                    fn(y)

    return nc


def kernel(L0_attn_hidden_states, L0_attn_Wq, L0_attn_Wk, L0_attn_Wv, L0_attn_Wo,
           L0_ffn_qweight, L0_ffn_lut4, L0_ffn_lut8, L0_ffn_bias, _dbg=False):
    x = np.ascontiguousarray(np.asarray(L0_attn_hidden_states)[0], dtype=np.float32)
    Wq = np.asarray(L0_attn_Wq, dtype=np.float32)
    Wk = np.asarray(L0_attn_Wk, dtype=np.float32)
    Wv = np.asarray(L0_attn_Wv, dtype=np.float32)
    Wo = np.ascontiguousarray(np.asarray(L0_attn_Wo, dtype=np.float32))
    qw = np.ascontiguousarray(np.asarray(L0_ffn_qweight, dtype=np.int32))
    lut4 = np.asarray(L0_ffn_lut4, dtype=np.float32)
    lut8 = np.asarray(L0_ffn_lut8, dtype=np.float32)
    bias = np.ascontiguousarray(np.asarray(L0_ffn_bias, dtype=np.float32).reshape(1, D))
    cosT, sinTs, _, _ = _host_consts()

    nc = build(dbg=_dbg)
    in_maps = []
    for c in range(NCORES):
        in_maps.append({
            "x_mine": np.ascontiguousarray(x[TOK * c:TOK * (c + 1)]),
            "wq_m": np.ascontiguousarray(Wq[:, 512 * c:512 * (c + 1)]),
            "wk_m": np.ascontiguousarray(Wk[:, 128 * c:128 * (c + 1)]),
            "wv_m": np.ascontiguousarray(Wv[:, 128 * c:128 * (c + 1)]),
            "wo": Wo,
            "qw_m": np.ascontiguousarray(qw[:, OROWS * c:OROWS * (c + 1), :]),
            "lut4_m": np.ascontiguousarray(lut4[OROWS * c:OROWS * (c + 1)]),
            "lut8_m": np.ascontiguousarray(lut8[OROWS * c:OROWS * (c + 1)]),
            "bias": bias,
            "cosT": cosT,
            "sinTs": sinTs,
        })
    res = run_bass_kernel_spmd(nc, in_maps, core_ids=list(range(NCORES)))
    out = np.concatenate([res.results[c]["out_m"] for c in range(NCORES)], axis=0)
    if _dbg:
        return out[None], res
    return out[None]
